# version 16
# baseline (speedup 1.0000x reference)
import contextlib
import numpy as np
import ml_dtypes
import concourse.bass as bass
import concourse.mybir as mybir
from concourse.bass_utils import run_bass_kernel_spmd

F32 = mybir.dt.float32
BF16 = mybir.dt.bfloat16
U8 = mybir.dt.uint8
AF = mybir.ActivationFunctionType
ALU = mybir.AluOpType
bfnp = ml_dtypes.bfloat16

B, T, DM = 4, 4096, 2048
H, DH = 16, 128
KD = H * DH
HPC = 8
HCH = HPC * DH
SLAB = 512
NSLAB = T // SLAB
CC = 128
NCOL = SLAB // CC
F0 = 32
CLAMP = 43.0
OWN = T // 2
NCH = DM // DH
NKCH = KD // DH
SUB = 512
NSUB = OWN // SUB
WD = HPC * CC


class KB:
    def __init__(self):
        self.nc = bass.Bass("TRN2", target_bir_lowering=False)
        self.q = {e: [] for e in ("sync", "gpsimd", "vector", "scalar", "tensor")}
        self.cnt = {}
        self.dr = {}
        self.sb = {}
        self.ps = {}
        self.semh = {}
        self.stack = contextlib.ExitStack()

    def din(self, name, shape, dt=F32):
        self.dr[name] = self.nc.dram_tensor(name, list(shape), dt, kind="ExternalInput").ap()
        return self.dr[name]

    def dout(self, name, shape, dt=F32):
        self.dr[name] = self.nc.dram_tensor(name, list(shape), dt, kind="ExternalOutput").ap()
        return self.dr[name]

    def dint(self, name, shape, dt):
        self.dr[name] = self.nc.dram_tensor(name, list(shape), dt).ap()
        return self.dr[name]

    def sbuf(self, name, shape, dt, stack=None):
        h = (stack or self.stack).enter_context(self.nc.sbuf_tensor(name, list(shape), dt))
        self.sb[name] = h
        return h

    def psum_t(self, name, shape, dt):
        h = self.stack.enter_context(self.nc.psum_tensor(name, list(shape), dt))
        self.ps[name] = h
        return h

    def sem(self, name):
        self.semh[name] = self.stack.enter_context(self.nc.semaphore(name))
        self.cnt[name] = 0
        return self.semh[name]

    def op(self, eng, fn):
        self.q[eng].append(fn)

    def inc(self, eng, fn, sem, by):
        self.cnt[sem] += by
        def wrapped(e, fn=fn, sem=sem, by=by):
            fn(e).then_inc(self.semh[sem], by)
        self.q[eng].append(wrapped)
        return self.cnt[sem]

    def wait(self, eng, sem, val=None):
        v = self.cnt[sem] if val is None else val
        if v <= 0:
            return
        def w(e, sem=sem, v=v):
            e.wait_ge(self.semh[sem], v)
        self.q[eng].append(w)

    def dma(self, eng, out, in_, sem, accum=None):
        if accum is None:
            return self.inc(eng, lambda e, o=out, i=in_: e.dma_start(out=o, in_=i), sem, 16)
        return self.inc(eng, lambda e, o=out, i=in_, a=accum: e.dma_start(out=o, in_=i, accum_op=a), sem, 16)

    def mm(self, out, lhsT, rhs, start=True, stop=True, tp=None):
        def f(e, out=out, lhsT=lhsT, rhs=rhs, start=start, stop=stop, tp=tp):
            kw = {}
            if tp is not None:
                kw["tile_position"] = tp
            return e.matmul(out, lhsT, rhs, start=start, stop=stop,
                            skip_group_check=True, **kw)
        return self.inc("tensor", f, "sMM", 1)

    def tr(self, out, in_, ident):
        return self.inc("tensor", lambda e, o=out, i=in_, d=ident: e.transpose(o, i, d), "sMM", 1)

    def v(self, fn):
        self.wait("vector", "sV")
        return self.inc("vector", fn, "sV", 1)

    def s(self, fn):
        self.wait("scalar", "sS")
        return self.inc("scalar", fn, "sS", 1)

    def p(self, fn):
        self.wait("gpsimd", "sP")
        return self.inc("gpsimd", fn, "sP", 1)

    def snap(self):
        return {k: self.cnt[k] for k in ("sMM", "sV", "sS", "sP", "sOG")}

    def run(self):
        block = self.stack.enter_context(self.nc.Block())
        for ename in ("sync", "gpsimd", "vector", "scalar", "tensor"):
            ops = self.q[ename]
            dec = getattr(block, ename)
            def make(ops):
                def body(engine):
                    for fn in ops:
                        fn(engine)
                return body
            dec(make(ops))


def build_nc(stop_after="full", debug=False):
    K = KB()
    nc = K.nc

    xT = K.din("xT", [DM, T])
    wq = K.din("wq", [DM, HCH]); wk = K.din("wk", [DM, HCH]); wv = K.din("wv", [DM, HCH])
    cwres = K.din("cwres", [DH, 3 * 4 * HPC])
    wfa = K.din("wfa", [DM, DH]); wfb = K.din("wfb", [DH, HCH])
    dtb = K.din("dtb", [DH, HPC])
    alog = K.din("alog", [HPC, 1])
    wb = K.din("wb", [DM, HPC])
    wga = K.din("wga", [DM, DH]); wgb = K.din("wgb", [DH, HCH])
    nw = K.din("nw", [DH, 1])
    wo = K.din("wo", [KD, DM])
    fac = K.din("fac", [8, DM, DM])
    fbias = K.din("fbias", [1, DM])
    ident = K.din("ident", [DH, DH])
    ident16 = K.din("ident16", [DH, DH], BF16)
    negident16 = K.din("negident16", [DH, DH], BF16)
    onescol16 = K.din("onescol16", [DH, 1], BF16)
    onesrowf = K.din("onesrowf", [1, DH])
    onesrow16 = K.din("onesrow16", [1, SUB], BF16)
    onesblk16 = K.din("onesblk16", [DH, DH], BF16)
    maskSL = K.din("maskSL", [CC, HPC * CC], U8)
    maskSU = K.din("maskSU", [CC, HPC * CC], U8)

    out = K.dout("out", [OWN, DM])
    if debug:
        for nm in ("d_qf", "d_kf", "d_v", "d_g", "d_gout", "d_og"):
            K.dout(nm, [DH, HPC * T])
        K.dout("d_beta", [HPC, T])

    wq16t = K.dint("wq16t", [HPC, DH, NCH, DH], BF16)
    wk16t = K.dint("wk16t", [HPC, DH, NCH, DH], BF16)
    wv16t = K.dint("wv16t", [HPC, DH, NCH, DH], BF16)
    wo16t = K.dint("wo16t", [NCH, DH, NKCH, DH], BF16)
    wsumf = K.dint("wsumf", [DM, DM], F32)
    ib = [K.dint(f"ib{s}", [HCH, SLAB], BF16) for s in range(NSLAB)]
    obs = [K.dint(f"ob{s}", [KD, SLAB], BF16) for s in range(NSLAB)]
    obAll = K.dint("obAll", [KD, T], BF16)

    for s_ in ("sW", "sX", "sMM", "sV", "sS", "sD", "sO", "sG",
               "sXa", "sXb", "sWa", "sWb", "sCa", "sCb", "sOG",
               "sFa", "sFb", "sFc", "sFd", "sP"):
        K.sem(s_)
    ccsem = [K.sem(f"sCC{s}") for s in range(NSLAB)]

    psal = K.psum_t("psal", [DH, WD], F32)
    psbe = K.psum_t("psbe", [DH, SLAB], F32)
    psga = K.psum_t("psga", [DH, WD], F32)
    psgb = K.psum_t("psgb", [DH, WD], F32)
    pst16 = K.psum_t("pst16", [DH, WD], BF16)
    psP = psga[:, 0:SLAB]
    psQ = psga[:, SLAB:WD]

    for nm, shape, dt in (
        ("c_ident", [DH, DH], F32), ("c_ident16", [DH, DH], BF16),
        ("c_negident16", [DH, DH], BF16), ("c_onescol16", [DH, 1], BF16),
        ("c_onesrowf", [1, DH], F32), ("c_onesrow16", [1, SUB], BF16),
        ("c_onesblk16", [DH, DH], BF16),
        ("c_maskSL", [CC, HPC * CC], U8), ("c_maskSU", [CC, HPC * CC], U8),
        ("c_cwres", [DH, 3 * 4 * HPC], F32),
        ("c_zero", [DH, 1], F32), ("c_epsv", [DH, 1], F32), ("c_eps128", [DH, 1], F32),
        ("c_epslnv", [DH, 1], F32), ("c_epsln1", [1, 1], F32), ("c_m20", [DH, 1], F32),
        ("wfa16", [DH, NCH * DH], BF16), ("wfb16", [DH, HCH], BF16),
        ("wga16", [DH, NCH * DH], BF16), ("wgb16", [DH, HCH], BF16),
        ("wb16", [DH, NCH * HPC], BF16),
        ("dtbc", [DH, HPC], F32), ("negA", [DH, HPC], F32), ("nwcol", [DH, 1], F32),
        ("fb8", [1, DM], BF16),
        ("S", [DH, HPC * DH], F32), ("S2", [DH, HPC * DH], F32),
        ("S16", [DH, HPC * DH], BF16),
    ):
        K.sbuf(nm, shape, dt)

    slab_stack = contextlib.ExitStack()
    for nm, shape, dt in (
        ("beta", [HPC, SLAB], F32),
        ("betaTs", [CC, NCOL * 16], F32),
        ("xb", [DH, NCH * SLAB], BF16),
        ("hsc0", [DH, SLAB + 3], F32), ("hsc1", [DH, SLAB + 3], F32),
        ("stash", [DH, 3 * HPC * 3], F32),
        ("qf", [DH, HPC * SLAB], BF16), ("kf", [DH, HPC * SLAB], BF16),
        ("vv", [DH, HPC * SLAB], BF16), ("gout", [DH, HPC * SLAB], BF16),
        ("gg", [DH, HPC * SLAB], F32),
        ("y1", [DH, SLAB], BF16),
        ("g1", [DH, SLAB], F32), ("g2", [DH, SLAB], F32),
        ("rawq", [DH, SLAB], BF16), ("rstdq", [DH, SLAB], BF16),
        ("pw0", [DH, NCH * DH], BF16), ("pw1", [DH, NCH * DH], BF16),
        ("G", [DH, WD], F32), ("Gloc", [DH, WD], F32), ("Gtmp", [DH, WD], F32),
        ("Eloc16", [DH, WD], BF16), ("Einv", [DH, WD], BF16),
        ("EG", [DH, WD], BF16), ("EK2", [DH, WD], BF16),
        ("kbar", [DH, WD], BF16), ("qbar", [DH, WD], BF16), ("ktil", [DH, WD], BF16),
        ("ktil2", [DH, WD], BF16), ("qk16", [DH, WD], BF16),
        ("rhsS0", [DH, WD], BF16), ("rhsS1", [DH, WD], BF16),
        ("Lt", [DH, WD], BF16), ("LtT", [DH, WD], BF16),
        ("Pa", [DH, WD], BF16), ("Pb", [DH, WD], BF16),
        ("T0", [DH, WD], BF16), ("T1", [DH, WD], BF16),
        ("C3Tp", [DH, WD], BF16), ("C3p", [DH, WD], BF16),
        ("EAll", [DH, HPC * 16], BF16), ("nEAll", [DH, HPC * 16], BF16),
        ("kbarg0", [DH, WD], BF16), ("kbarg1", [DH, WD], BF16),
        ("qbarg0", [DH, WD], BF16), ("qbarg1", [DH, WD], BF16),
        ("k2t0", [DH, WD], BF16), ("k2t1", [DH, WD], BF16),
        ("App0", [DH, WD], BF16), ("App1", [DH, WD], BF16),
        ("C1T0", [DH, WD], BF16), ("C1T1", [DH, WD], BF16),
        ("C2T0", [DH, WD], BF16), ("C2T1", [DH, WD], BF16),
        ("C4T0", [DH, WD], BF16), ("C4T1", [DH, WD], BF16),
        ("EGend0", [DH, HPC], F32), ("EGend1", [DH, HPC], F32),
        ("dcol0", [CC, HPC], F32), ("dcol1", [CC, HPC], F32),
        ("Rt", [DH, WD], BF16), ("Ya", [DH, WD], BF16),
        ("o16", [DH, WD], BF16), ("osq", [DH, WD], BF16), ("rstd16", [DH, WD], BF16),
    ):
        K.sbuf(nm, shape, dt, stack=slab_stack)

    t = K.sb

    for nm, src in (("c_ident", ident), ("c_ident16", ident16),
                    ("c_negident16", negident16), ("c_onescol16", onescol16),
                    ("c_onesrowf", onesrowf), ("c_onesrow16", onesrow16),
                    ("c_onesblk16", onesblk16), ("c_maskSL", maskSL), ("c_maskSU", maskSU),
                    ("c_cwres", cwres)):
        K.dma("sync", t[nm][:, :], src, "sW")
    K.v(lambda e: e.memset(t["c_zero"][:, :], 0.0))
    K.v(lambda e: e.memset(t["c_epsv"][:, :], 1e-6))
    K.v(lambda e: e.memset(t["c_eps128"][:, :], 128e-6))
    K.v(lambda e: e.memset(t["c_epslnv"][:, :], 1e-5))
    K.v(lambda e: e.memset(t["c_epsln1"][:, :], 1e-5))
    K.v(lambda e: e.memset(t["c_m20"][:, :], -20.0))
    K.v(lambda e: e.memset(t["stash"][:, :], 0.0))
    K.v(lambda e: e.memset(psga[:, :], 0.0))
    K.v(lambda e: e.memset(psgb[:, :], 0.0))

    def small16(dst, view, cols):
        K.wait("sync", "sV", K.cnt["sV"])
        K.dma("sync", t["gg"][:, 0:cols], view, "sW")
        K.wait("vector", "sW")
        K.v(lambda e, dst=dst, cols=cols: e.tensor_copy(t[dst][:, 0:cols], t["gg"][:, 0:cols]))
    small16("wfa16", wfa.rearrange("(c p) d -> p c d", p=DH), NCH * DH)
    small16("wga16", wga.rearrange("(c p) d -> p c d", p=DH), NCH * DH)
    small16("wfb16", wfb, HCH)
    small16("wgb16", wgb, HCH)
    small16("wb16", wb.rearrange("(c p) d -> p c d", p=DH), NCH * HPC)
    K.wait("sync", "sW")
    K.dma("sync", t["dtbc"][:, :], dtb, "sW")
    K.wait("sync", "sW")
    K.dma("sync", t["negA"][0:HPC, 0:1], alog, "sW")
    K.wait("sync", "sW")
    K.dma("sync", t["nwcol"][:, :], nw, "sW")
    K.wait("sync", "sW")
    K.wait("sync", "sV", K.cnt["sV"])
    K.dma("sync", t["gg"][0:1, 0:DM], fbias, "sW")
    K.wait("sync", "sW")
    K.wait("scalar", "sW")
    K.s(lambda e: e.mul(t["fb8"][0:1, :], t["gg"][0:1, 0:DM], float(np.sqrt(8.0))))
    K.s(lambda e: e.activation(t["negA"][0:HPC, 1:2], t["negA"][0:HPC, 0:1], AF.Exp))
    K.wait("tensor", "sS")
    K.tr(psga[0:1, 0:HPC], t["negA"][0:HPC, 1:2], t["c_ident"][0:HPC, 0:HPC])
    K.wait("vector", "sMM")
    K.v(lambda e: e.tensor_copy(t["Gloc"][0:1, 0:HPC], psga[0:1, 0:HPC]))
    K.wait("tensor", "sV")
    K.mm(psga[0:DH, 8:16], t["c_onesrowf"][0:1, :], t["Gloc"][0:1, 0:HPC])
    K.wait("vector", "sMM")
    K.v(lambda e: e.tensor_scalar(t["negA"][:, 0:HPC], psga[0:DH, 8:16], -1.0, None, ALU.mult))

    K.wait("sync", "sS", K.cnt["sS"])
    K.wait("sync", "sMM", K.cnt["sMM"])
    K.wait("sync", "sV", K.cnt["sV"])

    def wprep(src, dstt, nr, per_oc):
        ncols = src.shape[1]
        nci = nr // DH
        for ci in range(nci):
            stg = t["gg"][:, 0:ncols] if ci % 2 == 0 else t["gg"][:, 2048:2048 + ncols]
            stb = t["qf"][:, 0:ncols] if ci % 2 == 0 else t["kf"][:, 0:ncols]
            K.wait("sync", "sV", K.cnt["sV"])
            K.dma("sync", stg, src[ci * DH:(ci + 1) * DH, :], "sW")
            K.wait("vector", "sW")
            K.wait("vector", "sD", K.cnt["sD"])
            K.v(lambda e, stb=stb, stg=stg: e.tensor_copy(stb, stg))
            K.wait("gpsimd", "sV")
            K.dma("gpsimd", dstt.rearrange("o p c d -> c p o d", d=per_oc)[ci], stb, "sD")
    wprep(wq, wq16t, DM, DH)
    wprep(wk, wk16t, DM, DH)
    wprep(wv, wv16t, DM, DH)
    fsems = ("sFa", "sFb", "sFc")
    fstage = ("S", "S2", "Gtmp")
    fchains = [(ci, hf) for ci in range(NCH) for hf in range(2)]
    base_mm, base_v = K.cnt["sMM"], K.cnt["sV"]
    K.wait("gpsimd", "sMM", base_mm)
    K.wait("gpsimd", "sV", base_v)
    fout = [0, 0, 0]
    w = 0
    while w * 3 < len(fchains):
        grp = fchains[3 * w:3 * w + 3]
        for r_ in range(8):
            for q in range(len(grp)):
                ci, hf = grp[q]
                stg = t[fstage[q]][:, 0:1024]
                if r_ == 0:
                    if fout[q]:
                        K.wait("gpsimd", "sD", fout[q])
                    K.dma("gpsimd", stg,
                          fac[0, ci * DH:(ci + 1) * DH, hf * 1024:hf * 1024 + 1024],
                          fsems[q])
                else:
                    K.wait("gpsimd", fsems[q])
                    K.dma("gpsimd", stg,
                          fac[r_, ci * DH:(ci + 1) * DH, hf * 1024:hf * 1024 + 1024],
                          fsems[q], accum=ALU.add)
        for q in range(len(grp)):
            ci, hf = grp[q]
            stg = t[fstage[q]][:, 0:1024]
            K.wait("gpsimd", fsems[q])
            K.dma("gpsimd", wsumf[ci * DH:(ci + 1) * DH, hf * 1024:hf * 1024 + 1024],
                  stg, "sD")
            fout[q] = K.cnt["sD"]
        w += 1
    fac_out_dma = K.cnt["sD"]
    wprep(wo, wo16t, KD, DH)

    if stop_after == "prep":
        K.wait("gpsimd", "sD")
        K.run()
        return nc, K

    pwmm = [0, 0]
    evac = {"psP": 0, "psQ": 0}
    evacS = {"psP": 0, "psQ": 0}
    xstage_free = [0, 0]

    def slab_qkv(s):
        t0 = s * SLAB
        if debug:
            K.wait("vector", "sO", K.cnt["sO"])
            K.wait("scalar", "sO", K.cnt["sO"])
        K.wait("vector", "sS", K.cnt["sS"])
        K.wait("vector", "sP", K.cnt["sP"])
        K.wait("vector", "sMM", K.cnt["sMM"])
        K.wait("sync", "sV", K.cnt["sV"])
        K.wait("sync", "sS", K.cnt["sS"])
        K.wait("sync", "sD", K.cnt["sD"])
        K.wait("sync", "sMM", K.cnt["sMM"])
        xbv = 0
        for ci in range(NCH):
            slot = ci % 2
            sap = t["G"][:, 0:SLAB] if slot == 0 else t["G"][:, SLAB:2 * SLAB]
            xsem = "sXa" if slot == 0 else "sXb"
            if xstage_free[slot]:
                K.wait("sync", "sV", xstage_free[slot])
            K.dma("sync", sap, xT[ci * DH:(ci + 1) * DH, t0:t0 + SLAB], xsem)
            K.wait("vector", xsem)
            xbv = K.v(lambda e, ci=ci, sap=sap: e.tensor_copy(
                t["xb"][:, ci * SLAB:(ci + 1) * SLAB], sap))
            xstage_free[slot] = xbv
        projs = ((wq16t, 0, "qf"), (wk16t, 1, "kf"), (wv16t, 2, "vv"))
        for wdt, pi, dstname in projs:
            for oc in range(HPC):
                it = pi * HPC + oc
                slot = it % 2
                pwb = t["pw0"] if slot == 0 else t["pw1"]
                psX, psXn = (psP, "psP") if it % 2 == 0 else (psQ, "psQ")
                wsem = "sWa" if slot == 0 else "sWb"
                if pwmm[slot]:
                    K.wait("sync", "sMM", pwmm[slot])
                K.dma("sync", pwb[:, :], wdt.rearrange("o p c d -> o p (c d)")[oc], wsem)
                K.wait("tensor", wsem)
                K.wait("tensor", "sV", max(xbv, evac[psXn]))
                K.wait("tensor", "sS", evacS[psXn])
                for ci in range(NCH):
                    K.mm(psX[:, 0:SLAB], pwb[:, ci * DH:(ci + 1) * DH],
                         t["xb"][:, ci * SLAB:(ci + 1) * SLAB],
                         start=(ci == 0), stop=(ci == NCH - 1))
                pwmm[slot] = K.cnt["sMM"]
                hs = "hsc0" if it % 2 == 0 else "hsc1"
                K.wait("vector", "sMM")
                hv0 = K.v(lambda e, hs=hs, psX=psX: e.tensor_copy(
                    t[hs][:, 3:3 + SLAB], psX[:, 0:SLAB]))
                evac[psXn] = hv0
                st_off = it * 3
                K.v(lambda e, hs=hs, st_off=st_off: e.tensor_copy(
                    t[hs][:, 0:3], t["stash"][:, st_off:st_off + 3]))
                K.v(lambda e, hs=hs, st_off=st_off: e.tensor_copy(
                    t["stash"][:, st_off:st_off + 3], t[hs][:, SLAB:SLAB + 3]))
                cA = t["Gloc"][:, 0:SLAB]
                cB = t["Gloc"][:, SLAB:2 * SLAB]
                cwc = lambda dl: t["c_cwres"][:, pi * 32 + oc * 4 + dl:pi * 32 + oc * 4 + dl + 1]
                K.v(lambda e, hs=hs, cA=cA, w0=cwc(0): e.tensor_scalar(
                    cA, t[hs][:, 0:SLAB], w0, None, ALU.mult))
                K.v(lambda e, hs=hs, cA=cA, cB=cB, w1=cwc(1): e.scalar_tensor_tensor(
                    cB, t[hs][:, 1:1 + SLAB], w1, cA, ALU.mult, ALU.add))
                K.v(lambda e, hs=hs, cA=cA, cB=cB, w2=cwc(2): e.scalar_tensor_tensor(
                    cA, t[hs][:, 2:2 + SLAB], w2, cB, ALU.mult, ALU.add))
                cvv = K.v(lambda e, hs=hs, cA=cA, cB=cB, w3=cwc(3): e.scalar_tensor_tensor(
                    cB, t[hs][:, 3:3 + SLAB], w3, cA, ALU.mult, ALU.add))
                K.wait("scalar", "sV", cvv)
                sg = K.s(lambda e, cB=cB: e.activation(t["g1"][:, :], cB, AF.Sigmoid))
                K.wait("vector", "sS", sg)
                if dstname == "vv":
                    K.v(lambda e, cB=cB, oc=oc: e.tensor_mul(
                        t["vv"][:, oc * SLAB:(oc + 1) * SLAB], cB, t["g1"][:, :]))
                else:
                    K.v(lambda e, cB=cB: e.tensor_mul(
                        t["rawq"][:, :], cB, t["g1"][:, :]))
                    sqv = K.v(lambda e: e.tensor_mul(t["y1"][:, :], t["rawq"][:, :],
                                                     t["rawq"][:, :]))
                    K.wait("tensor", "sV", sqv)
                    K.mm(psX[:, 0:SLAB], t["c_onesblk16"][:, :], t["y1"][:, :])
                    K.wait("scalar", "sMM")
                    sc = 128.0 if dstname == "qf" else 1.0
                    epscol = "c_eps128" if dstname == "qf" else "c_epsv"
                    K.s(lambda e, psX=psX, sc=sc, epscol=epscol: e.activation(
                        t["g1"][:, :], psX[:, 0:SLAB], AF.Ln, bias=t[epscol][:, 0:1], scale=sc))
                    sd = K.s(lambda e: e.activation(t["rstdq"][:, :], t["g1"][:, :],
                                                    AF.Exp, scale=-0.5))
                    evacS[psXn] = sd
                    K.wait("vector", "sS", sd)
                    K.v(lambda e, dstname=dstname, oc=oc: e.tensor_mul(
                        t[dstname][:, oc * SLAB:(oc + 1) * SLAB], t["rawq"][:, :], t["rstdq"][:, :]))

    def slab_gates(s):
        K.wait("vector", "sP", K.cnt["sP"])
        K.wait("scalar", "sP", K.cnt["sP"])
        K.wait("vector", "sS", K.cnt["sS"])
        K.wait("tensor", "sV", evac["psP"])
        K.wait("tensor", "sS", evacS["psP"])
        for ci in range(NCH):
            K.mm(psP[:, 0:SLAB], t["wfa16"][:, ci * DH:(ci + 1) * DH],
                 t["xb"][:, ci * SLAB:(ci + 1) * SLAB], start=(ci == 0), stop=(ci == NCH - 1))
        K.wait("vector", "sMM")
        yv = K.v(lambda e: e.tensor_copy(t["y1"][:, :], psP[:, 0:SLAB]))
        evac["psP"] = yv
        for oc in range(HPC):
            psX, psXn = (psQ, "psQ") if oc % 2 == 0 else (psP, "psP")
            K.wait("tensor", "sV", max(evac[psXn], yv))
            K.wait("tensor", "sS", evacS[psXn])
            K.mm(psX[:, 0:SLAB], t["wfb16"][:, oc * DH:(oc + 1) * DH], t["y1"][:, :])
            K.wait("vector", "sMM")
            c1 = K.v(lambda e, psX=psX, oc=oc: e.scalar_tensor_tensor(
                t["g1"][:, :], psX[:, 0:SLAB], t["dtbc"][:, oc:oc + 1],
                t["c_m20"][:, 0:1].broadcast_to((DH, SLAB)), ALU.add, ALU.max))
            evac[psXn] = c1
            K.v(lambda e: e.tensor_scalar(t["g2"][:, :], t["g1"][:, :], 20.0, None, ALU.min))
            K.wait("scalar", "sV")
            K.s(lambda e: e.activation(t["g1"][:, :], t["g2"][:, :], AF.Exp))
            K.s(lambda e: e.activation(t["g2"][:, :], t["g1"][:, :], AF.Ln, bias=1.0))
            K.wait("vector", "sS")
            K.v(lambda e, oc=oc: e.tensor_scalar(
                t["gg"][:, oc * SLAB:(oc + 1) * SLAB], t["g2"][:, :],
                t["negA"][:, oc:oc + 1], None, ALU.mult))
        K.wait("tensor", "sV", evac["psP"])
        K.wait("tensor", "sS", evacS["psP"])
        for ci in range(NCH):
            K.mm(psP[:, 0:SLAB], t["wga16"][:, ci * DH:(ci + 1) * DH],
                 t["xb"][:, ci * SLAB:(ci + 1) * SLAB], start=(ci == 0), stop=(ci == NCH - 1))
        K.wait("vector", "sMM")
        yv = K.v(lambda e: e.tensor_copy(t["y1"][:, :], psP[:, 0:SLAB]))
        evac["psP"] = yv
        for oc in range(HPC):
            psX, psXn = (psQ, "psQ") if oc % 2 == 0 else (psP, "psP")
            K.wait("tensor", "sV", max(evac[psXn], yv))
            K.wait("tensor", "sS", evacS[psXn])
            K.mm(psX[:, 0:SLAB], t["wgb16"][:, oc * DH:(oc + 1) * DH], t["y1"][:, :])
            K.wait("scalar", "sMM")
            sg = K.s(lambda e, psX=psX, oc=oc: e.activation(
                t["gout"][:, oc * SLAB:(oc + 1) * SLAB], psX[:, 0:SLAB], AF.Sigmoid))
            evacS[psXn] = sg
        K.wait("tensor", "sV", evac["psQ"])
        K.wait("tensor", "sS", evacS["psQ"])
        for ci in range(NCH):
            K.mm(psQ[0:HPC, 0:SLAB], t["wb16"][:, ci * HPC:(ci + 1) * HPC],
                 t["xb"][:, ci * SLAB:(ci + 1) * SLAB], start=(ci == 0), stop=(ci == NCH - 1))
        K.wait("scalar", "sMM")
        sg = K.s(lambda e: e.activation(t["beta"][0:HPC, :], psQ[0:HPC, 0:SLAB], AF.Sigmoid))
        evacS["psQ"] = sg
        K.wait("vector", "sS", sg)
        K.wait("vector", "sP", K.cnt["sP"])
        K.wait("vector", "sMM", K.cnt["sMM"])
        bv = K.v(lambda e: e.tensor_copy(t["y1"][0:HPC, 0:SLAB], t["beta"][0:HPC, :]))
        K.wait("tensor", "sV", K.cnt["sV"])
        K.wait("tensor", "sS", K.cnt["sS"])
        for c4_ in range(NCOL):
            K.tr(pst16[:, c4_ * 8:(c4_ + 1) * 8],
                 t["y1"][0:HPC, c4_ * CC:(c4_ + 1) * CC], t["c_ident16"][0:HPC, 0:HPC])
        btm = K.cnt["sMM"]
        K.wait("vector", "sMM", btm)
        K.v(lambda e: e.tensor_copy(
            t["betaTs"][:, 0:NCOL * 16].rearrange("p (c g) -> p c g", g=16)[:, :, 0:8],
            pst16[:, 0:NCOL * 8].rearrange("p (c h) -> p c h", h=8)))
        K.v(lambda e: e.tensor_scalar(
            t["betaTs"][:, 0:NCOL * 16].rearrange("p (c g) -> p c g", g=16)[:, :, 8:16],
            pst16[:, 0:NCOL * 8].rearrange("p (c h) -> p c h", h=8),
            -1.0, None, ALU.mult))

    def h_ap(name, h, cc, width=CC):
        return t[name][:, h * SLAB + cc * CC: h * SLAB + cc * CC + width]

    def w_ap(name, h, width=CC):
        return t[name][:, h * CC: h * CC + width]

    def w3(name):
        return t[name][:, 0:WD].rearrange("p (h x) -> p h x", x=CC)

    def h3(name, cc):
        return t[name][:, 0:HPC * SLAB].rearrange(
            "p (h x) -> p h x", x=SLAB)[:, :, cc * CC:(cc + 1) * CC]

    def sq_mms(dstps, lhs, rhs):
        for h in range(HPC):
            K.mm(dstps[:, h * CC:(h + 1) * CC], w_ap(lhs, h), w_ap(rhs, h))
        return K.cnt["sMM"]

    def A_gen(s, cc, aprev, bpar):
        p = cc % 2
        sx = str(p)
        K.wait("vector", "sS", aprev["sS"])
        for h in range(HPC):
            K.v(lambda e, h=h, cc=cc: e.tensor_tensor_scan(
                w_ap("G", h), h_ap("gg", h, cc), h_ap("gg", h, cc), 0.0, ALU.add, ALU.bypass))
        yield
        def gloc(e):
            g3 = t["G"][:, 0:WD].rearrange("p (h b i) -> p h b i", b=4, i=F0)
            r3 = t["G"][:, 0:WD].rearrange("p (h x) -> p h x", x=CC)[:, :, F0 // 2 - 1::F0]
            return e.tensor_tensor(
                t["Gloc"][:, 0:WD].rearrange("p (h b i) -> p h b i", b=4, i=F0),
                g3, r3.rearrange("p h (b o) -> p h b o", o=1).broadcast_to((DH, HPC, 4, F0)),
                ALU.subtract)
        K.v(gloc)
        c2 = K.v(lambda e: e.tensor_scalar(t["Gtmp"][:, 0:WD], t["Gloc"][:, 0:WD],
                                           CLAMP, None, ALU.min))
        yield
        K.wait("scalar", "sV", c2)
        K.wait("scalar", "sP", aprev["sP"])
        el = K.s(lambda e: e.activation(t["Eloc16"][:, 0:WD], t["Gtmp"][:, 0:WD], AF.Exp))
        yield
        K.wait("vector", "sS", el)
        c4 = K.v(lambda e: e.tensor_scalar(t["Gtmp"][:, 0:WD], t["Gloc"][:, 0:WD],
                                           -1.0, CLAMP, ALU.mult, ALU.min))
        yield
        K.wait("scalar", "sV", c4)
        ei = K.s(lambda e: e.activation(t["Einv"][:, 0:WD], t["Gtmp"][:, 0:WD], AF.Exp))
        K.s(lambda e: e.activation(t["EG"][:, 0:WD], t["G"][:, 0:WD], AF.Exp))
        K.wait("scalar", "sV", bpar["sV"])
        K.s(lambda e, sx=sx: e.activation(
            t["EGend" + sx][:, 0:HPC],
            t["G"][:, 0:WD].rearrange("p (h x) -> p h x", x=CC)[:, :, CC - 1], AF.Exp))
        yield
        K.wait("vector", "sS", ei)
        def k2arg(e):
            gend = t["G"][:, 0:WD].rearrange("p (h x) -> p h x", x=CC)[:, :, CC - 1]
            return e.tensor_tensor(
                t["Gtmp"][:, 0:WD].rearrange("p (h x) -> p h x", x=CC),
                gend.rearrange("p (h o) -> p h o", o=1).broadcast_to((DH, HPC, CC)),
                t["G"][:, 0:WD].rearrange("p (h x) -> p h x", x=CC), ALU.subtract)
        c6 = K.v(k2arg)
        K.wait("scalar", "sV", c6)
        ek = K.s(lambda e: e.activation(t["EK2"][:, 0:WD], t["Gtmp"][:, 0:WD], AF.Exp))
        yield
        K.wait("gpsimd", "sS", ek)
        K.wait("gpsimd", "sV", aprev["sV"])
        K.wait("gpsimd", "sMM", max(aprev["sMM"], bpar["sMM"]))
        qf3 = h3("qf", cc)
        kf3 = h3("kf", cc)
        for dst, a3, b3 in (("kbar", kf3, "Eloc16"), ("qbar", qf3, "Eloc16"),
                            ("ktil", kf3, "Einv"), ("kbarg" + sx, kf3, "EG"),
                            ("qbarg" + sx, qf3, "EG"), ("ktil2", kf3, "EK2")):
            K.p(lambda e, dst=dst, a3=a3, b3=b3: e.tensor_tensor(
                w3(dst), a3, w3(b3), ALU.mult))
        sc = K.p(lambda e, qf3=qf3, kf3=kf3: e.tensor_tensor(
            w3("qk16"), qf3, kf3, ALU.mult))
        yield
        K.wait("vector", "sS", ek)
        for h in range(HPC):
            def ea(e, h=h):
                r = t["G"][:, 0:WD].rearrange("p (h x) -> p h x", x=CC)[:, h, F0 // 2 - 1::F0]
                return e.tensor_tensor(
                    t["Gtmp"][:, h * 16:(h + 1) * 16].rearrange("p (i j) -> p i j", j=4),
                    r.rearrange("p (i o) -> p i o", o=1).broadcast_to((DH, 4, 4)),
                    r.rearrange("p (o j) -> p o j", o=1).broadcast_to((DH, 4, 4)),
                    ALU.subtract)
            K.v(ea)
        c8 = K.v(lambda e: e.tensor_scalar(t["Gloc"][:, 0:HPC * 16], t["Gtmp"][:, 0:HPC * 16],
                                           0.0, None, ALU.min))
        K.wait("scalar", "sV", c8)
        ea2 = K.s(lambda e: e.activation(t["EAll"][:, 0:HPC * 16], t["Gloc"][:, 0:HPC * 16],
                                         AF.Exp))
        K.wait("vector", "sS", ea2)
        K.v(lambda e: e.tensor_scalar(t["nEAll"][:, 0:HPC * 16], t["EAll"][:, 0:HPC * 16],
                                      -1.0, None, ALU.mult))
        yield
        wmm, amm = aprev["sMM"], aprev["sMM"]
        K.wait("tensor", "sP", sc)
        for I in range(4):
            if wmm:
                K.wait("vector", "sMM", wmm)
            def rbW(e, I=I):
                nk = t["ktil"][:, 0:WD].rearrange("p (h s) -> p h s", s=CC)[:, :, 0:(I + 1) * F0]
                scl = t["nEAll"][:, 0:HPC * 16].rearrange(
                    "p (h i j) -> p h i j", i=4, j=4)[:, :, I, 0:I + 1]
                return e.tensor_tensor(
                    t["rhsS0"][:, 0:WD].rearrange("p (h s) -> p h s", s=CC)[:, :, 0:(I + 1) * F0]
                        .rearrange("p h (j f) -> p h j f", f=F0),
                    nk.rearrange("p h (j f) -> p h j f", f=F0),
                    scl.rearrange("p h (j o) -> p h j o", o=1).broadcast_to((DH, HPC, I + 1, F0)),
                    ALU.mult)
            rb = K.v(rbW)
            tp = (0, 96) if I == 3 else None
            K.wait("tensor", "sV", rb)
            for h in range(HPC):
                K.mm(psga[I * F0:(I + 1) * F0, h * CC:h * CC + (I + 1) * F0],
                     t["kbar"][:, h * CC + I * F0:h * CC + (I + 1) * F0],
                     t["rhsS0"][:, h * CC:h * CC + (I + 1) * F0], tp=tp)
            wmm = K.cnt["sMM"]
            if amm:
                K.wait("vector", "sMM", amm)
            def rbA(e, I=I):
                qb = t["qbar"][:, 0:WD].rearrange("p (h s) -> p h s", s=CC)[:, :, I * F0:CC]
                scl = t["nEAll"][:, 0:HPC * 16].rearrange(
                    "p (h i j) -> p h i j", i=4, j=4)[:, :, I:4, I]
                return e.tensor_tensor(
                    t["rhsS1"][:, 0:WD].rearrange("p (h s) -> p h s", s=CC)[:, :, I * F0:CC]
                        .rearrange("p h (j f) -> p h j f", f=F0),
                    qb.rearrange("p h (j f) -> p h j f", f=F0),
                    scl.rearrange("p h (j o) -> p h j o", o=1).broadcast_to((DH, HPC, 4 - I, F0)),
                    ALU.mult)
            rb2 = K.v(rbA)
            K.wait("tensor", "sV", rb2)
            for h in range(HPC):
                K.mm(psgb[I * F0:(I + 1) * F0, h * CC + I * F0:h * CC + CC],
                     t["ktil"][:, h * CC + I * F0:h * CC + (I + 1) * F0],
                     t["rhsS1"][:, h * CC + I * F0:h * CC + CC], tp=tp)
            amm = K.cnt["sMM"]
            yield
        K.wait("vector", "sMM", amm)
        lt = K.v(lambda e: e.select(
            t["Lt"][:, 0:WD], t["c_maskSL"][:, :], psga[:, 0:WD],
            t["c_zero"][:, 0:1].broadcast_to((CC, WD)), add_drain=True))
        K.wait("vector", "sMM", bpar["sMM"])
        ap_ = K.v(lambda e, sx=sx: e.select(
            t["App" + sx][:, 0:WD], t["c_maskSU"][:, :], psgb[:, 0:WD],
            t["c_zero"][:, 0:1].broadcast_to((CC, WD)), add_drain=True))
        ltm = K.v(lambda e, cc=cc: e.tensor_tensor(
            w3("Lt"), w3("Lt"),
            t["betaTs"][:, cc * 16 + 8:cc * 16 + 16].rearrange("p (h o) -> p h o", o=1)
                .broadcast_to((CC, HPC, CC)), ALU.mult))
        yield
        K.wait("tensor", "sV", ltm)
        for h in range(HPC):
            K.tr(pst16[:, h * CC:(h + 1) * CC], w_ap("Lt", h), t["c_ident16"][:, :])
        lttm = K.cnt["sMM"]
        K.wait("vector", "sMM", lttm)
        ltt = K.v(lambda e: e.tensor_copy(t["LtT"][:, 0:WD], pst16[:, 0:WD]))
        yield
        K.wait("tensor", "sV", ltt)
        sq_mms(psga, "Lt", "LtT")
        mmL = sq_mms(psgb, "LtT", "Lt")
        K.wait("vector", "sMM", mmL)
        t0c = K.v(lambda e: e.tensor_copy(t["T0"][:, 0:WD], psga[:, 0:WD]))
        K.wait("scalar", "sMM", mmL)
        pac = K.s(lambda e: e.copy(t["Pa"][:, 0:WD], psgb[:, 0:WD]))
        yield
        K.wait("tensor", "sV", t0c)
        K.wait("tensor", "sS", pac)
        sq_mms(psga, "Pa", "T0")
        mmL = sq_mms(psgb, "T0", "Pa")
        K.wait("vector", "sMM", mmL)
        t1c = K.v(lambda e: e.tensor_copy(t["T1"][:, 0:WD], psga[:, 0:WD]))
        K.wait("scalar", "sMM", mmL)
        pbc = K.s(lambda e: e.copy(t["Pb"][:, 0:WD], psgb[:, 0:WD]))
        yield
        K.wait("gpsimd", "sV", t1c)
        K.wait("gpsimd", "sS", pbc)
        c3t_p = K.p(lambda e: e.tensor_tensor(t["C3Tp"][:, 0:WD], t["T0"][:, 0:WD],
                                              t["T1"][:, 0:WD], ALU.add))
        c3p_p = K.p(lambda e: e.tensor_tensor(t["C3p"][:, 0:WD], t["Pa"][:, 0:WD],
                                              t["Pb"][:, 0:WD], ALU.add))
        K.wait("tensor", "sV", t1c)
        K.wait("tensor", "sS", pbc)
        sq_mms(psga, "Pa", "T1")
        mmL = sq_mms(psgb, "T1", "Pa")
        K.wait("vector", "sMM", mmL)
        K.wait("vector", "sP", c3p_p)
        c3t = K.v(lambda e: e.tensor_tensor(t["C3Tp"][:, 0:WD], psga[:, 0:WD],
                                            t["C3Tp"][:, 0:WD], ALU.add))
        c3pl = K.v(lambda e: e.tensor_tensor(t["C3p"][:, 0:WD], psgb[:, 0:WD],
                                             t["C3p"][:, 0:WD], ALU.add))
        yield
        K.wait("tensor", "sV", c3pl)
        sq_mms(psga, "Pb", "T1")
        mmL = sq_mms(psgb, "T1", "Pb")
        K.wait("vector", "sMM", mmL)
        K.wait("vector", "sP", c3t_p)
        t0c = K.v(lambda e: e.tensor_copy(t["T0"][:, 0:WD], psga[:, 0:WD]))
        K.wait("scalar", "sMM", mmL)
        K.wait("scalar", "sP", c3p_p)
        pac = K.s(lambda e: e.copy(t["Pa"][:, 0:WD], psgb[:, 0:WD]))
        yield
        K.wait("tensor", "sV", t0c)
        K.wait("tensor", "sS", pac)
        sq_mms(psga, "Pa", "T0")
        mmL = sq_mms(psgb, "T0", "Pa")
        K.wait("vector", "sMM", mmL)
        K.wait("vector", "sP", c3t_p)
        t1c = K.v(lambda e: e.tensor_copy(t["T1"][:, 0:WD], psga[:, 0:WD]))
        K.wait("scalar", "sMM", mmL)
        K.wait("scalar", "sP", c3p_p)
        pbc = K.s(lambda e: e.copy(t["Pb"][:, 0:WD], psgb[:, 0:WD]))
        yield
        K.wait("gpsimd", "sV", t1c)
        K.wait("gpsimd", "sMM", bpar["sMM"])
        c2t_p = K.p(lambda e, sx=sx: e.tensor_tensor(
            t["C2T" + sx][:, 0:WD], t["T0"][:, 0:WD], t["T1"][:, 0:WD], ALU.add))
        K.wait("tensor", "sV", t1c)
        mmL = sq_mms(psga, "Pa", "T1")
        K.wait("vector", "sMM", mmL)
        K.wait("vector", "sP", c2t_p)
        c2t = K.v(lambda e, sx=sx: e.tensor_tensor(
            t["C2T" + sx][:, 0:WD], psga[:, 0:WD], t["C2T" + sx][:, 0:WD], ALU.add))
        yield
        K.wait("tensor", "sV", c2t)
        K.wait("tensor", "sS", pbc)
        sq_mms(psga, "Pb", "T1")
        mmL = sq_mms(psgb, "T1", "Pb")
        K.wait("vector", "sMM", mmL)
        K.wait("vector", "sP", c2t_p)
        t0c = K.v(lambda e: e.tensor_copy(t["T0"][:, 0:WD], psga[:, 0:WD]))
        K.wait("scalar", "sMM", mmL)
        pac = K.s(lambda e: e.copy(t["Pa"][:, 0:WD], psgb[:, 0:WD]))
        yield
        K.wait("tensor", "sV", t0c)
        K.wait("tensor", "sS", pac)
        mmL = sq_mms(psga, "Pa", "T0")
        K.wait("vector", "sMM", mmL)
        K.wait("vector", "sP", c2t_p)
        t1c = K.v(lambda e: e.tensor_copy(t["T1"][:, 0:WD], psga[:, 0:WD]))
        yield
        K.wait("gpsimd", "sV", t1c)
        K.wait("gpsimd", "sMM", bpar["sMM"])
        c1t_p = K.p(lambda e, sx=sx: e.tensor_tensor(
            t["C1T" + sx][:, 0:WD], t["T0"][:, 0:WD], t["T1"][:, 0:WD], ALU.add))
        K.wait("tensor", "sV", t1c)
        mmL = sq_mms(psga, "Pa", "T1")
        K.wait("vector", "sMM", mmL)
        K.wait("vector", "sP", c1t_p)
        K.v(lambda e, sx=sx: e.tensor_tensor(
            t["C1T" + sx][:, 0:WD], psga[:, 0:WD], t["C1T" + sx][:, 0:WD], ALU.add))
        yield
        K.wait("tensor", "sV", c3pl)
        K.wait("tensor", "sS", pac)
        mmL = sq_mms(psgb, "C3p", "LtT")
        K.wait("gpsimd", "sV", c3t)
        K.wait("gpsimd", "sMM", bpar["sMM"])
        c4t_p = K.p(lambda e, sx=sx: e.tensor_tensor(
            t["C4T" + sx][:, 0:WD], t["C3Tp"][:, 0:WD], t["LtT"][:, 0:WD], ALU.subtract))
        K.wait("vector", "sMM", mmL)
        K.wait("vector", "sP", c4t_p)
        c4t = K.v(lambda e, sx=sx: e.tensor_tensor(
            t["C4T" + sx][:, 0:WD], t["C4T" + sx][:, 0:WD], psgb[:, 0:WD], ALU.subtract))
        yield
        K.wait("tensor", "sV", c4t)
        K.wait("tensor", "sP", sc)
        for h in range(HPC):
            K.tr(pst16[:, h * CC:(h + 1) * CC], w_ap("ktil2", h), t["c_ident16"][:, :])
        k2m = K.cnt["sMM"]
        for h in range(HPC):
            K.mm(psgb[:, h:h + 1], w_ap("qk16", h), t["c_onescol16"][:, 0:1])
        dcm = K.cnt["sMM"]
        K.wait("vector", "sMM", k2m)
        K.wait("vector", "sMM", bpar["sMM"])
        K.v(lambda e, sx=sx: e.tensor_copy(t["k2t" + sx][:, 0:WD], pst16[:, 0:WD]))
        K.wait("vector", "sMM", dcm)
        K.wait("vector", "sP", bpar["sP"])
        K.v(lambda e, sx=sx: e.tensor_copy(t["dcol" + sx][:, 0:HPC], psgb[:, 0:HPC]))
        yield

    def B_gen(s, cc, bprev, colg):
        p = cc % 2
        sx = str(p)
        Scur = "S" if colg % 2 == 0 else "S2"
        Snxt = "S2" if colg % 2 == 0 else "S"
        K.wait("tensor", "sV", bprev["sV"])
        K.wait("tensor", "sP", bprev["sP"])
        K.wait("tensor", "sS", bprev["sS"])
        for h in range(HPC):
            K.mm(psal[:, h * CC:(h + 1) * CC], w_ap("kbarg" + sx, h),
                 t["S16"][:, h * DH:(h + 1) * DH], start=True, stop=False)
            K.mm(psal[:, h * CC:(h + 1) * CC], h_ap("vv", h, cc), t["c_negident16"][:, :],
                 start=False, stop=True)
        rmm = K.cnt["sMM"]
        yield
        K.wait("vector", "sMM", rmm)
        K.wait("vector", "sS", bprev["sS"])
        K.wait("vector", "sP", bprev["sP"])
        rt = K.v(lambda e, cc=cc: e.tensor_tensor(
            w3("Rt"), psal[:, 0:WD].rearrange("p (h x) -> p h x", x=CC),
            t["betaTs"][:, cc * 16:cc * 16 + 8].rearrange("p (h o) -> p h o", o=1)
                .broadcast_to((CC, HPC, CC)), ALU.mult))
        yield
        cur, ev, m_ = "Rt", rt, 0
        for fac_, nxt in (("C1T", "Ya"), ("C2T", "Rt"), ("C4T", "Ya")):
            K.wait("tensor", "sV", ev)
            for h in range(HPC):
                K.mm(psal[:, h * CC:(h + 1) * CC], w_ap(fac_ + sx, h), w_ap(cur, h))
            m_ = K.cnt["sMM"]
            K.wait("vector", "sMM", m_)
            ev = K.v(lambda e, nxt=nxt, cur=cur: e.tensor_tensor(
                t[nxt][:, 0:WD], psal[:, 0:WD], t[cur][:, 0:WD], ALU.add))
            cur = nxt
            yield
        ya2, lastapply = ev, m_
        K.wait("tensor", "sV", ya2)
        for h in range(HPC):
            K.mm(psal[:, h * DH:(h + 1) * DH], w_ap("k2t" + sx, h), w_ap("Ya", h))
        m4 = K.cnt["sMM"]
        K.wait("vector", "sS", bprev["sS"])
        K.v(lambda e: e.tensor_tensor(
            t[Snxt][:, 0:WD].rearrange("p (h x) -> p h x", x=DH),
            t[Scur][:, 0:WD].rearrange("p (h x) -> p h x", x=DH),
            t["EGend" + sx][:, 0:HPC].rearrange("p (h o) -> p h o", o=1)
                .broadcast_to((DH, HPC, DH)), ALU.mult))
        K.wait("vector", "sMM", m4)
        st = K.v(lambda e: e.tensor_tensor(
            t[Snxt][:, 0:WD], t[Snxt][:, 0:WD], psal[:, 0:WD], ALU.subtract))
        yield
        K.wait("vector", "sMM", lastapply)
        rtd = K.v(lambda e: e.tensor_tensor(
            w3("Rt"), w3("Ya"),
            t["dcol" + sx][:, 0:HPC].rearrange("p (h o) -> p h o", o=1)
                .broadcast_to((CC, HPC, CC)), ALU.mult))
        K.wait("tensor", "sV", rtd)
        K.wait("tensor", "sS", bprev["sS"])
        K.wait("tensor", "sV", bprev["sV"])
        o, omm = 0, 0
        for g in range(2):
            if g == 1:
                K.wait("tensor", "sV", o)
            for h in range(4 * g, 4 * g + 4):
                hc = (h - 4 * g) * CC
                K.mm(psbe[:, hc:hc + CC], t["S16"][:, h * DH:(h + 1) * DH],
                     w_ap("qbarg" + sx, h), start=True, stop=False)
                K.mm(psbe[:, hc:hc + CC], w_ap("Ya", h), w_ap("App" + sx, h),
                     start=False, stop=False)
                K.mm(psbe[:, hc:hc + CC], w_ap("Rt", h), t["c_negident16"][:, :],
                     start=False, stop=True)
            omm = K.cnt["sMM"]
            K.wait("vector", "sMM", omm)
            o = K.v(lambda e, g=g: e.tensor_copy(
                t["o16"][:, g * SLAB:(g + 1) * SLAB], psbe[:, 0:SLAB]))
        K.v(lambda e: e.tensor_copy(t["S16"][:, 0:HPC * DH], t[Snxt][:, 0:HPC * DH]))
        yield
        K.wait("gpsimd", "sV", o)
        K.wait("gpsimd", "sOG", bprev["sOG"])
        sq = K.p(lambda e: e.tensor_tensor(t["osq"][:, 0:WD], t["o16"][:, 0:WD],
                                           t["o16"][:, 0:WD], ALU.mult))
        K.wait("tensor", "sV", o)
        K.wait("tensor", "sP", sq)
        K.mm(psbe[:, 0:SLAB], t["c_onesblk16"][:, :], t["osq"][:, 0:SLAB])
        smm = K.cnt["sMM"]
        K.wait("scalar", "sMM", smm)
        K.wait("scalar", "sV", st)
        ln1 = K.s(lambda e, Scur=Scur: e.activation(t[Scur][:, 0:SLAB], psbe[:, 0:SLAB],
                                                    AF.Ln, bias=t["c_epsv"][:, 0:1],
                                                    scale=1.0 / DH))
        K.wait("tensor", "sS", ln1)
        K.mm(psbe[:, 0:SLAB], t["c_onesblk16"][:, :], t["osq"][:, SLAB:WD])
        smm = K.cnt["sMM"]
        yield
        K.wait("scalar", "sMM", smm)
        K.s(lambda e, Scur=Scur: e.activation(t[Scur][:, SLAB:WD], psbe[:, 0:SLAB],
                                              AF.Ln, bias=t["c_epsv"][:, 0:1],
                                              scale=1.0 / DH))
        rs = K.s(lambda e, Scur=Scur: e.activation(t["rstd16"][:, 0:WD], t[Scur][:, 0:WD],
                                                   AF.Exp, scale=-0.5))
        K.wait("gpsimd", "sS", rs)
        gmp = K.p(lambda e: e.tensor_tensor(t["rstd16"][:, 0:WD], t["rstd16"][:, 0:WD],
                                            t["o16"][:, 0:WD], ALU.mult))
        K.wait("vector", "sP", gmp)
        og = K.v(lambda e, cc=cc: e.scalar_tensor_tensor(
            w3("osq"), w3("rstd16"), t["nwcol"][:, 0:1], h3("gout", cc),
            ALU.mult, ALU.mult))
        K.wait("gpsimd", "sV", og)
        for h in range(HPC):
            K.dma("gpsimd", ib[s][h * DH:(h + 1) * DH, cc * CC:(cc + 1) * CC],
                  w_ap("osq", h), "sOG")
        if debug:
            K.dma("gpsimd",
                  K.dr["d_og"].rearrange("p (h x) -> p h x", x=T)[:, :, s * SLAB + cc * CC:
                                                                 s * SLAB + (cc + 1) * CC],
                  w3("osq"), "sO")
        yield

    bendp = [dict(sMM=0, sV=0, sS=0, sP=0, sOG=0), dict(sMM=0, sV=0, sS=0, sP=0, sOG=0)]
    colg = [0]
    for s in range(NSLAB):
        evac["psP"] = K.cnt["sV"]
        evac["psQ"] = K.cnt["sV"]
        evacS["psP"] = K.cnt["sS"]
        evacS["psQ"] = K.cnt["sS"]
        slab_qkv(s)
        slab_gates(s)
        if s == 0:
            K.wait("vector", "sD", fac_out_dma)
            K.v(lambda e: e.memset(t["S"][:, :], 0.0))
            K.v(lambda e: e.memset(t["S2"][:, :], 0.0))
            K.v(lambda e: e.memset(t["S16"][:, :], 0.0))
        pend = K.snap()
        for _ in A_gen(s, 0, pend, bendp[0]):
            pass
        pend = K.snap()
        aprev = pend
        bprev = pend
        for cc in range(1, NCOL + 1):
            bgen = B_gen(s, cc - 1, bprev, colg[0])
            agen = A_gen(s, cc, aprev, bendp[cc % 2]) if cc < NCOL else None
            while bgen is not None or agen is not None:
                if bgen is not None:
                    try:
                        next(bgen)
                    except StopIteration:
                        bgen = None
                if agen is not None:
                    for _ in range(2):
                        try:
                            next(agen)
                        except StopIteration:
                            agen = None
                            break
            pend = K.snap()
            bendp[(cc - 1) % 2] = pend
            bprev = pend
            aprev = pend
            colg[0] += 1
        if debug:
            for nm, buf in (("d_qf", "qf"), ("d_kf", "kf"), ("d_v", "vv"),
                            ("d_gout", "gout")):
                K.dma("gpsimd",
                      K.dr[nm].rearrange("p (h x) -> p h x", x=T)[:, :, s * SLAB:(s + 1) * SLAB],
                      t[buf][:, 0:HPC * SLAB].rearrange("p (h x) -> p h x", x=SLAB), "sO")
            K.dma("gpsimd",
                  K.dr["d_g"].rearrange("p (h x) -> p h x", x=T)[:, :, s * SLAB:(s + 1) * SLAB],
                  t["gg"][:, 0:HPC * SLAB].rearrange("p (h x) -> p h x", x=SLAB), "sO")
            K.dma("gpsimd", K.dr["d_beta"][:, s * SLAB:(s + 1) * SLAB], t["beta"][0:HPC, :], "sO")
        if s > 0:
            K.wait("gpsimd", f"sCC{s - 1}", 1)
            K.dma("gpsimd", obAll[:, (s - 1) * SLAB:s * SLAB], obs[s - 1][:, :], "sD")
        K.wait("gpsimd", "sOG")
        K.inc("gpsimd", lambda e, s=s: e.collective_compute(
            "AllGather", ALU.bypass, replica_groups=[[0, 1], [2, 3], [4, 5], [6, 7]],
            ins=[ib[s][:, :].opt()], outs=[obs[s][:, :].opt()]), f"sCC{s}", 1)
    K.wait("gpsimd", f"sCC{NSLAB - 1}", 1)
    K.dma("gpsimd", obAll[:, (NSLAB - 1) * SLAB:NSLAB * SLAB], obs[NSLAB - 1][:, :], "sD")

    if stop_after == "scan":
        K.wait("gpsimd", "sD")
        K.wait("gpsimd", "sO", K.cnt["sO"])
        K.run()
        return nc, K

    SL_V, SL_MM, SL_D, SL_S = K.cnt["sV"], K.cnt["sMM"], K.cnt["sD"], K.cnt["sS"]
    slab_stack.close()
    post = contextlib.ExitStack()
    po = K.sbuf("po", [DH, NKCH * SUB], BF16, stack=post)
    pwf0 = K.sbuf("pwf0", [DH, NCH * DH], F32, stack=post)
    pwf1 = K.sbuf("pwf1", [DH, NCH * DH], F32, stack=post)
    pz = K.sbuf("pz", [DH, NCH * SUB], F32, stack=post)
    pzb = K.sbuf("pzb", [DH, NCH * SUB], BF16, stack=post)
    px = K.sbuf("px", [DH, NCH * SUB], F32, stack=post)
    pwa = K.sbuf("pwa", [DH, NKCH * DH], BF16, stack=post)
    pwb_ = K.sbuf("pwb", [DH, NKCH * DH], BF16, stack=post)
    prow = K.sbuf("prow", [1, 8 * SUB], F32, stack=post)
    pab = K.sbuf("pab", [DH, 2 * SUB], F32, stack=post)
    pzt = K.sbuf("pzt", [CC, 4 * DM], F32, stack=post)
    pcol = K.sbuf("pcol", [CC, 40], F32, stack=post)

    for eng in ("sync", "gpsimd", "vector", "scalar", "tensor"):
        K.wait(eng, "sV", SL_V)
        K.wait(eng, "sMM", SL_MM)
        K.wait(eng, "sD", SL_D)
        K.wait(eng, "sS", SL_S)

    pwcnt = [0, 0]
    pwcast = [0, 0]
    pevac = {"psP": 0, "psQ": 0}

    _offcache = {}
    def _post_off(e, i):
        key = (id(e), i)
        if key not in _offcache:
            pk = (id(e), "pid")
            if pk not in _offcache:
                _offcache[pk] = e.partition_id()
            _offcache[key] = (_offcache[pk] % 2) * OWN + i * SUB
        return _offcache[key]

    for i in range(NSUB):
        K.wait("sync", "sMM", K.cnt["sMM"])
        K.wait("sync", "sV", K.cnt["sV"])
        def ld_o(e, i=i):
            off = _post_off(e, i)
            return e.dma_start(out=po[:, :],
                               in_=obAll.rearrange("(c p) t -> p c t", p=DH)[:, :, bass.ds(off, SUB)])
        K.inc("sync", ld_o, "sXa", 16)
        def ld_x(e, i=i):
            off = _post_off(e, i)
            return e.dma_start(out=px[:, :],
                               in_=xT.rearrange("(c p) t -> p c t", p=DH)[:, :, bass.ds(off, SUB)])
        K.inc("sync", ld_x, "sXb", 16)
        XDa, XDb = K.cnt["sXa"], K.cnt["sXb"]
        for dc in range(NCH):
            slot = dc % 2
            pwt = pwa if slot == 0 else pwb_
            if pwcnt[slot]:
                K.wait("sync", "sMM", pwcnt[slot])
            wsem = "sWa" if slot == 0 else "sWb"
            K.dma("sync", pwt[:, :], wo16t.rearrange("o p c d -> o p (c d)")[dc], wsem)
            psX, psXn = (psP, "psP") if dc % 2 == 0 else (psQ, "psQ")
            K.wait("tensor", wsem)
            K.wait("tensor", "sXa", XDa)
            K.wait("tensor", "sXb", XDb)
            K.wait("tensor", "sV", pevac[psXn])
            for c in range(NKCH):
                K.mm(psX[:, 0:SUB], pwt[:, c * DH:(c + 1) * DH], po[:, c * SUB:(c + 1) * SUB],
                     start=(c == 0), stop=(c == NKCH - 1))
            pwcnt[slot] = K.cnt["sMM"]
            K.wait("vector", "sMM")
            zv = K.v(lambda e, dc=dc, psX=psX: e.tensor_tensor(
                pz[:, dc * SUB:(dc + 1) * SUB], psX[:, 0:SUB],
                px[:, dc * SUB:(dc + 1) * SUB], ALU.add))
            pevac[psXn] = zv
        K.v(lambda e: e.tensor_copy(pzb[:, :], pz[:, :]))
        sqv = K.v(lambda e: e.tensor_mul(po[:, 0:NCH * SUB], pzb[:, :], pzb[:, :]))
        K.wait("tensor", "sV", sqv)
        for c in range(NCH):
            K.mm(psP[0:1, 0:SUB], t["c_onescol16"][:, 0:1], pzb[:, c * SUB:(c + 1) * SUB],
                 start=(c == 0), stop=(c == NCH - 1))
        for c in range(NCH):
            K.mm(psQ[0:1, 0:SUB], t["c_onescol16"][:, 0:1], po[:, c * SUB:(c + 1) * SUB],
                 start=(c == 0), stop=(c == NCH - 1))
        K.wait("vector", "sMM")
        K.v(lambda e: e.tensor_scalar(prow[0:1, 0:SUB], psP[0:1, 0:SUB], 1.0 / DM, None, ALU.mult))
        K.v(lambda e: e.tensor_mul(prow[0:1, SUB:2 * SUB], prow[0:1, 0:SUB], prow[0:1, 0:SUB]))
        vv_ = K.v(lambda e: e.scalar_tensor_tensor(
            prow[0:1, 2 * SUB:3 * SUB], psQ[0:1, 0:SUB], 1.0 / DM,
            prow[0:1, SUB:2 * SUB], ALU.mult, ALU.subtract))
        pevac["psP"] = vv_; pevac["psQ"] = vv_
        K.wait("scalar", "sV")
        K.s(lambda e: e.activation(prow[0:1, 3 * SUB:4 * SUB], prow[0:1, 2 * SUB:3 * SUB],
                                   AF.Ln, bias=t["c_epsln1"][0:1, 0:1]))
        av = K.s(lambda e: e.activation(prow[0:1, 4 * SUB:5 * SUB], prow[0:1, 3 * SUB:4 * SUB],
                                        AF.Exp, scale=-0.5))
        K.wait("vector", "sS", av)
        K.v(lambda e: e.tensor_mul(prow[0:1, 5 * SUB:6 * SUB], prow[0:1, 0:SUB],
                                   prow[0:1, 4 * SUB:5 * SUB]))
        bv = K.v(lambda e: e.tensor_scalar(prow[0:1, 6 * SUB:7 * SUB], prow[0:1, 5 * SUB:6 * SUB],
                                           -1.0, None, ALU.mult))
        K.wait("tensor", "sV", bv)
        K.mm(psP[:, 0:SUB], t["c_onesrowf"][0:1, :], prow[0:1, 4 * SUB:5 * SUB])
        K.mm(psQ[:, 0:SUB], t["c_onesrowf"][0:1, :], prow[0:1, 6 * SUB:7 * SUB])
        K.wait("vector", "sMM")
        K.v(lambda e: e.tensor_copy(pab[:, 0:SUB], psP[:, 0:SUB]))
        abv = K.v(lambda e: e.tensor_copy(pab[:, SUB:2 * SUB], psQ[:, 0:SUB]))
        pevac["psP"] = abv; pevac["psQ"] = abv
        for c in range(NCH):
            K.v(lambda e, c=c: e.tensor_mul(px[:, c * SUB:(c + 1) * SUB],
                                            pz[:, c * SUB:(c + 1) * SUB], pab[:, 0:SUB]))
            K.v(lambda e, c=c: e.tensor_tensor(
                px[:, c * SUB:(c + 1) * SUB], px[:, c * SUB:(c + 1) * SUB],
                pab[:, SUB:2 * SUB], ALU.add))
        xbv = K.v(lambda e: e.tensor_copy(pzb[:, :], px[:, :]))
        for dc in range(NCH):
            slot = dc % 2
            pwt = pwa if slot == 0 else pwb_
            pwf = pwf0 if slot == 0 else pwf1
            K.wait("sync", "sV", pwcast[slot])
            wsem = "sWa" if slot == 0 else "sWb"
            K.dma("sync", pwf[:, :],
                  wsumf.rearrange("(c p) (o d) -> o p c d", p=DH, d=DH)[dc], wsem)
            psX, psXn = (psP, "psP") if dc % 2 == 0 else (psQ, "psQ")
            K.wait("vector", wsem)
            K.wait("vector", "sMM", pwcnt[slot])
            pwcast[slot] = K.v(lambda e, pwt=pwt, pwf=pwf: e.tensor_copy(pwt[:, :], pwf[:, :]))
            K.wait("tensor", "sV", pwcast[slot])
            K.wait("tensor", "sV", max(xbv, pevac[psXn]))
            for c in range(NCH):
                K.mm(psX[:, 0:SUB], pwt[:, c * DH:(c + 1) * DH], pzb[:, c * SUB:(c + 1) * SUB],
                     start=(c == 0), stop=False)
            K.mm(psX[:, 0:SUB], t["fb8"][0:1, dc * DH:(dc + 1) * DH],
                 t["c_onesrow16"][0:1, 0:SUB], start=False, stop=True)
            pwcnt[slot] = K.cnt["sMM"]
            K.wait("vector", "sMM")
            z2v = K.v(lambda e, dc=dc, psX=psX: e.scalar_tensor_tensor(
                pz[:, dc * SUB:(dc + 1) * SUB], psX[:, 0:SUB], float(1.0 / np.sqrt(8.0)),
                px[:, dc * SUB:(dc + 1) * SUB], ALU.mult, ALU.add))
            pevac[psXn] = z2v
        K.wait("tensor", "sV")
        for tt in range(4):
            for g in range(2):
                for c8 in range(8):
                    c = g * 8 + c8
                    K.tr(psal[:, c8 * DH:(c8 + 1) * DH],
                         pz[:, c * SUB + tt * DH:c * SUB + (tt + 1) * DH], t["c_ident"][:, :])
                K.wait("vector", "sMM")
                K.v(lambda e, tt=tt, g=g: e.tensor_copy(
                    pzt[:, tt * DM + g * 8 * DH:tt * DM + (g + 1) * 8 * DH], psal[:, 0:8 * DH]))
                K.wait("tensor", "sV")
        z3 = pzt[:, 0:4 * DM].rearrange("p (a d) -> p a d", d=DM)
        K.v(lambda e, z3=z3: e.reduce_sum(
            pcol[:, 0:4].rearrange("p (a o) -> p a o", o=1), z3, mybir.AxisListType.X))
        K.v(lambda e: e.tensor_mul(pz[:, 0:4 * DM], pzt[:, 0:4 * DM], pzt[:, 0:4 * DM]))
        K.v(lambda e: e.reduce_sum(
            pcol[:, 4:8].rearrange("p (a o) -> p a o", o=1),
            pz[:, 0:4 * DM].rearrange("p (a d) -> p a d", d=DM), mybir.AxisListType.X))
        K.v(lambda e: e.tensor_scalar(pcol[:, 8:12], pcol[:, 0:4], 1.0 / DM, None, ALU.mult))
        K.v(lambda e: e.tensor_mul(pcol[:, 12:16], pcol[:, 8:12], pcol[:, 8:12]))
        cv = K.v(lambda e: e.scalar_tensor_tensor(
            pcol[:, 16:20], pcol[:, 4:8], 1.0 / DM, pcol[:, 12:16], ALU.mult, ALU.subtract))
        K.wait("scalar", "sV", cv)
        K.s(lambda e: e.activation(pcol[:, 20:24], pcol[:, 16:20], AF.Ln,
                                   bias=t["c_epslnv"][:, 0:1]))
        sv2 = K.s(lambda e: e.activation(pcol[:, 24:28], pcol[:, 20:24], AF.Exp, scale=-0.5))
        K.wait("vector", "sS", sv2)
        K.v(lambda e: e.tensor_mul(pcol[:, 28:32], pcol[:, 8:12], pcol[:, 24:28]))
        K.v(lambda e: e.tensor_scalar(pcol[:, 32:36], pcol[:, 28:32], -1.0, None, ALU.mult))
        K.v(lambda e, z3=z3: e.tensor_tensor(
            z3, z3,
            pcol[:, 24:28].rearrange("p (a o) -> p a o", o=1).broadcast_to((CC, 4, DM)),
            ALU.mult))
        ov = K.v(lambda e, z3=z3: e.tensor_tensor(
            z3, z3,
            pcol[:, 32:36].rearrange("p (a o) -> p a o", o=1).broadcast_to((CC, 4, DM)),
            ALU.add))
        K.wait("gpsimd", "sV", ov)
        K.dma("gpsimd", out[i * SUB:(i + 1) * SUB, :].rearrange("(a p) d -> p a d", p=CC),
              pzt[:, 0:4 * DM].rearrange("p (a d) -> p a d", d=DM), "sO")

    K.wait("gpsimd", "sO", K.cnt["sO"])
    K.run()
    return nc, K


def _host_prep(inputs, core):
    b, hh = core // 2, core % 2
    hs = slice(HCH * hh, HCH * hh + HCH)
    hsl = slice(HPC * hh, HPC * hh + HPC)
    f32 = np.float32
    x = np.asarray(inputs["L0_attn_x"], f32)
    d = {}
    d["xT"] = np.ascontiguousarray(x[b].T)
    d["wq"] = np.ascontiguousarray(np.asarray(inputs["L0_attn_Wq"], f32)[:, hs])
    d["wk"] = np.ascontiguousarray(np.asarray(inputs["L0_attn_Wk"], f32)[:, hs])
    d["wv"] = np.ascontiguousarray(np.asarray(inputs["L0_attn_Wv"], f32)[:, hs])
    cw = np.zeros((DH, 3 * 4 * HPC), f32)
    for pi, nm in enumerate(("conv_q", "conv_k", "conv_v")):
        w = np.asarray(inputs[f"L0_attn_{nm}"], f32)[:, hs].reshape(4, HPC, DH)
        for oc in range(HPC):
            for dl in range(4):
                cw[:, pi * 32 + oc * 4 + dl] = w[dl, oc]
    d["cwres"] = cw
    d["wfa"] = np.asarray(inputs["L0_attn_Wfa"], f32)
    d["wfb"] = np.ascontiguousarray(np.asarray(inputs["L0_attn_Wfb"], f32)[:, hs])
    d["dtb"] = np.ascontiguousarray(np.asarray(inputs["L0_attn_dt_bias"], f32).reshape(H, DH)[hsl].T)
    d["alog"] = np.ascontiguousarray(np.asarray(inputs["L0_attn_A_log"], f32)[hsl, None])
    d["wb"] = np.ascontiguousarray(np.asarray(inputs["L0_attn_Wb"], f32)[:, hsl])
    d["wga"] = np.asarray(inputs["L0_attn_Wga"], f32)
    d["wgb"] = np.ascontiguousarray(np.asarray(inputs["L0_attn_Wgb"], f32)[:, hs])
    d["nw"] = np.ascontiguousarray(np.asarray(inputs["L0_attn_norm_w"], f32)[:, None])
    d["wo"] = np.asarray(inputs["L0_attn_Wo"], f32)
    d["fac"] = np.asarray(inputs["L0_ffn_factors"], f32)
    d["fbias"] = np.ascontiguousarray(np.asarray(inputs["L0_ffn_bias"], f32)[None, :])
    d["ident"] = np.eye(DH, dtype=f32)
    d["ident16"] = np.eye(DH, dtype=bfnp)
    d["negident16"] = (-np.eye(DH)).astype(bfnp)
    d["onescol16"] = np.ones((DH, 1), bfnp)
    d["onesrowf"] = np.ones((1, DH), f32)
    d["onesrow16"] = np.ones((1, SUB), bfnp)
    d["onesblk16"] = np.ones((DH, DH), bfnp)
    d["maskSL"] = np.tile(np.tril(np.ones((CC, CC), np.uint8), -1), (1, HPC))
    d["maskSU"] = np.tile(np.triu(np.ones((CC, CC), np.uint8), 1), (1, HPC))
    return d


def kernel(**inputs):
    nc, K = build_nc("full", debug=False)
    in_maps = [_host_prep(inputs, c) for c in range(8)]
    res = run_bass_kernel_spmd(nc, in_maps, core_ids=list(range(8)))
    full = np.zeros((B, T, DM), np.float32)
    for c in range(8):
        b_, hh = c // 2, c % 2
        full[b_, hh * OWN:(hh + 1) * OWN, :] = res.results[c]["out"]
    return full


# version 17
# speedup vs baseline: 1.0056x; 1.0056x over previous
import contextlib
import numpy as np
import ml_dtypes
import concourse.bass as bass
import concourse.mybir as mybir
from concourse.bass_utils import run_bass_kernel_spmd

F32 = mybir.dt.float32
BF16 = mybir.dt.bfloat16
U8 = mybir.dt.uint8
AF = mybir.ActivationFunctionType
ALU = mybir.AluOpType
bfnp = ml_dtypes.bfloat16

B, T, DM = 4, 4096, 2048
H, DH = 16, 128
KD = H * DH
HPC = 8
HCH = HPC * DH
SLAB = 512
NSLAB = T // SLAB
CC = 128
NCOL = SLAB // CC
F0 = 32
CLAMP = 43.0
OWN = T // 2
NCH = DM // DH
NKCH = KD // DH
SUB = 512
NSUB = OWN // SUB
WD = HPC * CC


class KB:
    def __init__(self):
        self.nc = bass.Bass("TRN2", target_bir_lowering=False)
        self.q = {e: [] for e in ("sync", "gpsimd", "vector", "scalar", "tensor")}
        self.cnt = {}
        self.dr = {}
        self.sb = {}
        self.ps = {}
        self.semh = {}
        self.stack = contextlib.ExitStack()

    def din(self, name, shape, dt=F32):
        self.dr[name] = self.nc.dram_tensor(name, list(shape), dt, kind="ExternalInput").ap()
        return self.dr[name]

    def dout(self, name, shape, dt=F32):
        self.dr[name] = self.nc.dram_tensor(name, list(shape), dt, kind="ExternalOutput").ap()
        return self.dr[name]

    def dint(self, name, shape, dt):
        self.dr[name] = self.nc.dram_tensor(name, list(shape), dt).ap()
        return self.dr[name]

    def sbuf(self, name, shape, dt, stack=None):
        h = (stack or self.stack).enter_context(self.nc.sbuf_tensor(name, list(shape), dt))
        self.sb[name] = h
        return h

    def psum_t(self, name, shape, dt):
        h = self.stack.enter_context(self.nc.psum_tensor(name, list(shape), dt))
        self.ps[name] = h
        return h

    def sem(self, name):
        self.semh[name] = self.stack.enter_context(self.nc.semaphore(name))
        self.cnt[name] = 0
        return self.semh[name]

    def op(self, eng, fn):
        self.q[eng].append(fn)

    def inc(self, eng, fn, sem, by):
        self.cnt[sem] += by
        def wrapped(e, fn=fn, sem=sem, by=by):
            fn(e).then_inc(self.semh[sem], by)
        self.q[eng].append(wrapped)
        return self.cnt[sem]

    def wait(self, eng, sem, val=None):
        v = self.cnt[sem] if val is None else val
        if v <= 0:
            return
        def w(e, sem=sem, v=v):
            e.wait_ge(self.semh[sem], v)
        self.q[eng].append(w)

    def dma(self, eng, out, in_, sem, accum=None):
        if accum is None:
            return self.inc(eng, lambda e, o=out, i=in_: e.dma_start(out=o, in_=i), sem, 16)
        return self.inc(eng, lambda e, o=out, i=in_, a=accum: e.dma_start(out=o, in_=i, accum_op=a), sem, 16)

    def mm(self, out, lhsT, rhs, start=True, stop=True, tp=None):
        def f(e, out=out, lhsT=lhsT, rhs=rhs, start=start, stop=stop, tp=tp):
            kw = {}
            if tp is not None:
                kw["tile_position"] = tp
            return e.matmul(out, lhsT, rhs, start=start, stop=stop,
                            skip_group_check=True, **kw)
        return self.inc("tensor", f, "sMM", 1)

    def tr(self, out, in_, ident):
        return self.inc("tensor", lambda e, o=out, i=in_, d=ident: e.transpose(o, i, d), "sMM", 1)

    def v(self, fn):
        self.wait("vector", "sV")
        return self.inc("vector", fn, "sV", 1)

    def s(self, fn):
        self.wait("scalar", "sS")
        return self.inc("scalar", fn, "sS", 1)

    def p(self, fn):
        self.wait("gpsimd", "sP")
        return self.inc("gpsimd", fn, "sP", 1)

    def snap(self):
        return {k: self.cnt[k] for k in ("sMM", "sV", "sS", "sP", "sOG")}

    def run(self):
        block = self.stack.enter_context(self.nc.Block())
        for ename in ("sync", "gpsimd", "vector", "scalar", "tensor"):
            ops = self.q[ename]
            dec = getattr(block, ename)
            def make(ops):
                def body(engine):
                    for fn in ops:
                        fn(engine)
                return body
            dec(make(ops))


def build_nc(stop_after="full", debug=False):
    K = KB()
    nc = K.nc

    xT = K.din("xT", [DM, T])
    wq = K.din("wq", [DM, HCH]); wk = K.din("wk", [DM, HCH]); wv = K.din("wv", [DM, HCH])
    cwres = K.din("cwres", [DH, 3 * 4 * HPC])
    wfa = K.din("wfa", [DM, DH]); wfb = K.din("wfb", [DH, HCH])
    dtb = K.din("dtb", [DH, HPC])
    alog = K.din("alog", [HPC, 1])
    wb = K.din("wb", [DM, HPC])
    wga = K.din("wga", [DM, DH]); wgb = K.din("wgb", [DH, HCH])
    nw = K.din("nw", [DH, 1])
    wo = K.din("wo", [KD, DM])
    fac = K.din("fac", [8, DM, DM])
    fbias = K.din("fbias", [1, DM])
    ident = K.din("ident", [DH, DH])
    ident16 = K.din("ident16", [DH, DH], BF16)
    negident16 = K.din("negident16", [DH, DH], BF16)
    onescol16 = K.din("onescol16", [DH, 1], BF16)
    onesrowf = K.din("onesrowf", [1, DH])
    onesrow16 = K.din("onesrow16", [1, SUB], BF16)
    onesblk16 = K.din("onesblk16", [DH, DH], BF16)
    maskSL = K.din("maskSL", [CC, HPC * CC], U8)
    maskSU = K.din("maskSU", [CC, HPC * CC], U8)

    out = K.dout("out", [OWN, DM])
    if debug:
        for nm in ("d_qf", "d_kf", "d_v", "d_g", "d_gout", "d_og"):
            K.dout(nm, [DH, HPC * T])
        K.dout("d_beta", [HPC, T])

    wq16t = K.dint("wq16t", [HPC, DH, NCH, DH], BF16)
    wk16t = K.dint("wk16t", [HPC, DH, NCH, DH], BF16)
    wv16t = K.dint("wv16t", [HPC, DH, NCH, DH], BF16)
    wo16t = K.dint("wo16t", [NCH, DH, NKCH, DH], BF16)
    wsumf = K.dint("wsumf", [DM, DM], F32)
    ib = [K.dint(f"ib{s}", [HCH, SLAB], BF16) for s in range(NSLAB)]
    obs = [K.dint(f"ob{s}", [KD, SLAB], BF16) for s in range(NSLAB)]
    obAll = K.dint("obAll", [KD, T], BF16)

    for s_ in ("sW", "sX", "sMM", "sV", "sS", "sD", "sO", "sG",
               "sXa", "sXb", "sWa", "sWb", "sCa", "sCb", "sOG",
               "sFa", "sFb", "sFc", "sFd", "sP"):
        K.sem(s_)
    ccsem = [K.sem(f"sCC{s}") for s in range(NSLAB)]

    psal = K.psum_t("psal", [DH, WD], F32)
    psbe = K.psum_t("psbe", [DH, SLAB], F32)
    psga = K.psum_t("psga", [DH, WD], F32)
    psgb = K.psum_t("psgb", [DH, WD], F32)
    pst16 = K.psum_t("pst16", [DH, WD], BF16)
    psP = psga[:, 0:SLAB]
    psQ = psga[:, SLAB:WD]

    for nm, shape, dt in (
        ("c_ident", [DH, DH], F32), ("c_ident16", [DH, DH], BF16),
        ("c_negident16", [DH, DH], BF16), ("c_onescol16", [DH, 1], BF16),
        ("c_onesrowf", [1, DH], F32), ("c_onesrow16", [1, SUB], BF16),
        ("c_onesblk16", [DH, DH], BF16),
        ("c_maskSL", [CC, HPC * CC], U8), ("c_maskSU", [CC, HPC * CC], U8),
        ("c_cwres", [DH, 3 * 4 * HPC], F32),
        ("c_zero", [DH, 1], F32), ("c_epsv", [DH, 1], F32), ("c_eps128", [DH, 1], F32),
        ("c_epslnv", [DH, 1], F32), ("c_epsln1", [1, 1], F32), ("c_m20", [DH, 1], F32),
        ("wfa16", [DH, NCH * DH], BF16), ("wfb16", [DH, HCH], BF16),
        ("wga16", [DH, NCH * DH], BF16), ("wgb16", [DH, HCH], BF16),
        ("wb16", [DH, NCH * HPC], BF16),
        ("dtbc", [DH, HPC], F32), ("negA", [DH, HPC], F32), ("nwcol", [DH, 1], F32),
        ("fb8", [1, DM], BF16),
        ("S", [DH, HPC * DH], F32), ("S2", [DH, HPC * DH], F32),
        ("S16", [DH, HPC * DH], BF16),
    ):
        K.sbuf(nm, shape, dt)

    slab_stack = contextlib.ExitStack()
    for nm, shape, dt in (
        ("beta", [HPC, SLAB], F32),
        ("betaTs", [CC, NCOL * 16], F32),
        ("xb", [DH, NCH * SLAB], BF16),
        ("hsc0", [DH, SLAB + 3], F32), ("hsc1", [DH, SLAB + 3], F32),
        ("stash", [DH, 3 * HPC * 3], F32),
        ("qf", [DH, HPC * SLAB], BF16), ("kf", [DH, HPC * SLAB], BF16),
        ("vv", [DH, HPC * SLAB], BF16), ("gout", [DH, HPC * SLAB], BF16),
        ("gg", [DH, HPC * SLAB], F32),
        ("y1", [DH, SLAB], BF16),
        ("g1", [DH, SLAB], F32), ("g2", [DH, SLAB], F32),
        ("rawq", [DH, SLAB], BF16), ("rstdq", [DH, SLAB], BF16),
        ("pw0", [DH, NCH * DH], BF16), ("pw1", [DH, NCH * DH], BF16),
        ("G", [DH, WD], F32), ("Gloc", [DH, WD], F32), ("Gtmp", [DH, WD], F32),
        ("Eloc16", [DH, WD], BF16), ("Einv", [DH, WD], BF16),
        ("EG", [DH, WD], BF16), ("EK2", [DH, WD], BF16),
        ("kbar", [DH, WD], BF16), ("qbar", [DH, WD], BF16), ("ktil", [DH, WD], BF16),
        ("ktil2", [DH, WD], BF16), ("qk16", [DH, WD], BF16),
        ("rhsS0", [DH, WD], BF16), ("rhsS1", [DH, WD], BF16),
        ("Lt", [DH, WD], BF16), ("LtT", [DH, WD], BF16),
        ("Pa", [DH, WD], BF16), ("Pb", [DH, WD], BF16),
        ("T0", [DH, WD], BF16), ("T1", [DH, WD], BF16),
        ("C3Tp", [DH, WD], BF16), ("C3p", [DH, WD], BF16),
        ("EAll", [DH, HPC * 16], BF16), ("nEAll", [DH, HPC * 16], BF16),
        ("kbarg0", [DH, WD], BF16), ("kbarg1", [DH, WD], BF16),
        ("qbarg0", [DH, WD], BF16), ("qbarg1", [DH, WD], BF16),
        ("k2t0", [DH, WD], BF16), ("k2t1", [DH, WD], BF16),
        ("App0", [DH, WD], BF16), ("App1", [DH, WD], BF16),
        ("C1T0", [DH, WD], BF16), ("C1T1", [DH, WD], BF16),
        ("C2T0", [DH, WD], BF16), ("C2T1", [DH, WD], BF16),
        ("C4T0", [DH, WD], BF16), ("C4T1", [DH, WD], BF16),
        ("EGend0", [DH, HPC], F32), ("EGend1", [DH, HPC], F32),
        ("dcol0", [CC, HPC], F32), ("dcol1", [CC, HPC], F32),
        ("Rt", [DH, WD], BF16), ("Ya", [DH, WD], BF16),
        ("o16", [DH, WD], BF16), ("osq", [DH, WD], BF16), ("rstd16", [DH, WD], BF16),
    ):
        K.sbuf(nm, shape, dt, stack=slab_stack)

    t = K.sb

    for nm, src in (("c_ident", ident), ("c_ident16", ident16),
                    ("c_negident16", negident16), ("c_onescol16", onescol16),
                    ("c_onesrowf", onesrowf), ("c_onesrow16", onesrow16),
                    ("c_onesblk16", onesblk16), ("c_maskSL", maskSL), ("c_maskSU", maskSU),
                    ("c_cwres", cwres)):
        K.dma("sync", t[nm][:, :], src, "sW")
    K.v(lambda e: e.memset(t["c_zero"][:, :], 0.0))
    K.v(lambda e: e.memset(t["c_epsv"][:, :], 1e-6))
    K.v(lambda e: e.memset(t["c_eps128"][:, :], 128e-6))
    K.v(lambda e: e.memset(t["c_epslnv"][:, :], 1e-5))
    K.v(lambda e: e.memset(t["c_epsln1"][:, :], 1e-5))
    K.v(lambda e: e.memset(t["c_m20"][:, :], -20.0))
    K.v(lambda e: e.memset(t["stash"][:, :], 0.0))
    K.v(lambda e: e.memset(psga[:, :], 0.0))
    K.v(lambda e: e.memset(psgb[:, :], 0.0))

    def small16(dst, view, cols):
        K.wait("sync", "sV", K.cnt["sV"])
        K.dma("sync", t["gg"][:, 0:cols], view, "sW")
        K.wait("vector", "sW")
        K.v(lambda e, dst=dst, cols=cols: e.tensor_copy(t[dst][:, 0:cols], t["gg"][:, 0:cols]))
    small16("wfa16", wfa.rearrange("(c p) d -> p c d", p=DH), NCH * DH)
    small16("wga16", wga.rearrange("(c p) d -> p c d", p=DH), NCH * DH)
    small16("wfb16", wfb, HCH)
    small16("wgb16", wgb, HCH)
    small16("wb16", wb.rearrange("(c p) d -> p c d", p=DH), NCH * HPC)
    K.wait("sync", "sW")
    K.dma("sync", t["dtbc"][:, :], dtb, "sW")
    K.wait("sync", "sW")
    K.dma("sync", t["negA"][0:HPC, 0:1], alog, "sW")
    K.wait("sync", "sW")
    K.dma("sync", t["nwcol"][:, :], nw, "sW")
    K.wait("sync", "sW")
    K.wait("sync", "sV", K.cnt["sV"])
    K.dma("sync", t["gg"][0:1, 0:DM], fbias, "sW")
    K.wait("sync", "sW")
    K.wait("scalar", "sW")
    K.s(lambda e: e.mul(t["fb8"][0:1, :], t["gg"][0:1, 0:DM], float(np.sqrt(8.0))))
    K.s(lambda e: e.activation(t["negA"][0:HPC, 1:2], t["negA"][0:HPC, 0:1], AF.Exp))
    K.wait("tensor", "sS")
    K.tr(psga[0:1, 0:HPC], t["negA"][0:HPC, 1:2], t["c_ident"][0:HPC, 0:HPC])
    K.wait("vector", "sMM")
    K.v(lambda e: e.tensor_copy(t["Gloc"][0:1, 0:HPC], psga[0:1, 0:HPC]))
    K.wait("tensor", "sV")
    K.mm(psga[0:DH, 8:16], t["c_onesrowf"][0:1, :], t["Gloc"][0:1, 0:HPC])
    K.wait("vector", "sMM")
    K.v(lambda e: e.tensor_scalar(t["negA"][:, 0:HPC], psga[0:DH, 8:16], -1.0, None, ALU.mult))

    K.wait("sync", "sS", K.cnt["sS"])
    K.wait("sync", "sMM", K.cnt["sMM"])
    K.wait("sync", "sV", K.cnt["sV"])

    def wprep(src, dstt, nr, per_oc):
        ncols = src.shape[1]
        nci = nr // DH
        for ci in range(nci):
            stg = t["gg"][:, 0:ncols] if ci % 2 == 0 else t["gg"][:, 2048:2048 + ncols]
            stb = t["qf"][:, 0:ncols] if ci % 2 == 0 else t["kf"][:, 0:ncols]
            K.wait("sync", "sV", K.cnt["sV"])
            K.dma("sync", stg, src[ci * DH:(ci + 1) * DH, :], "sW")
            K.wait("vector", "sW")
            K.wait("vector", "sD", K.cnt["sD"])
            K.v(lambda e, stb=stb, stg=stg: e.tensor_copy(stb, stg))
            K.wait("gpsimd", "sV")
            K.dma("gpsimd", dstt.rearrange("o p c d -> c p o d", d=per_oc)[ci], stb, "sD")
    wprep(wq, wq16t, DM, DH)
    wprep(wk, wk16t, DM, DH)
    wprep(wv, wv16t, DM, DH)
    fsems = ("sFa", "sFb", "sFc")
    fstage = ("S", "S2", "Gtmp")
    fchains = [(ci, hf) for ci in range(NCH) for hf in range(2)]
    base_mm, base_v = K.cnt["sMM"], K.cnt["sV"]
    K.wait("gpsimd", "sMM", base_mm)
    K.wait("gpsimd", "sV", base_v)
    fout = [0, 0, 0]
    w = 0
    while w * 3 < len(fchains):
        grp = fchains[3 * w:3 * w + 3]
        for r_ in range(8):
            for q in range(len(grp)):
                ci, hf = grp[q]
                stg = t[fstage[q]][:, 0:1024]
                if r_ == 0:
                    if fout[q]:
                        K.wait("gpsimd", "sD", fout[q])
                    K.dma("gpsimd", stg,
                          fac[0, ci * DH:(ci + 1) * DH, hf * 1024:hf * 1024 + 1024],
                          fsems[q])
                else:
                    K.wait("gpsimd", fsems[q])
                    K.dma("gpsimd", stg,
                          fac[r_, ci * DH:(ci + 1) * DH, hf * 1024:hf * 1024 + 1024],
                          fsems[q], accum=ALU.add)
        for q in range(len(grp)):
            ci, hf = grp[q]
            stg = t[fstage[q]][:, 0:1024]
            K.wait("gpsimd", fsems[q])
            K.dma("gpsimd", wsumf[ci * DH:(ci + 1) * DH, hf * 1024:hf * 1024 + 1024],
                  stg, "sD")
            fout[q] = K.cnt["sD"]
        w += 1
    fac_out_dma = K.cnt["sD"]
    wprep(wo, wo16t, KD, DH)

    if stop_after == "prep":
        K.wait("gpsimd", "sD")
        K.run()
        return nc, K

    pwmm = [0, 0]
    evac = {"psP": 0, "psQ": 0}
    evacS = {"psP": 0, "psQ": 0}
    xstage_free = [0, 0]

    def slab_qkv(s):
        t0 = s * SLAB
        if debug:
            K.wait("vector", "sO", K.cnt["sO"])
            K.wait("scalar", "sO", K.cnt["sO"])
        K.wait("vector", "sS", K.cnt["sS"])
        K.wait("vector", "sP", K.cnt["sP"])
        K.wait("vector", "sMM", K.cnt["sMM"])
        K.wait("sync", "sV", K.cnt["sV"])
        K.wait("sync", "sS", K.cnt["sS"])
        K.wait("sync", "sD", K.cnt["sD"])
        K.wait("sync", "sMM", K.cnt["sMM"])
        xbv = 0
        for ci in range(NCH):
            slot = ci % 2
            sap = t["G"][:, 0:SLAB] if slot == 0 else t["G"][:, SLAB:2 * SLAB]
            xsem = "sXa" if slot == 0 else "sXb"
            if xstage_free[slot]:
                K.wait("sync", "sV", xstage_free[slot])
            K.dma("sync", sap, xT[ci * DH:(ci + 1) * DH, t0:t0 + SLAB], xsem)
            K.wait("vector", xsem)
            xbv = K.v(lambda e, ci=ci, sap=sap: e.tensor_copy(
                t["xb"][:, ci * SLAB:(ci + 1) * SLAB], sap))
            xstage_free[slot] = xbv
        projs = ((wq16t, 0, "qf"), (wk16t, 1, "kf"), (wv16t, 2, "vv"))
        for wdt, pi, dstname in projs:
            for oc in range(HPC):
                it = pi * HPC + oc
                slot = it % 2
                pwb = t["pw0"] if slot == 0 else t["pw1"]
                psX, psXn = (psP, "psP") if it % 2 == 0 else (psQ, "psQ")
                wsem = "sWa" if slot == 0 else "sWb"
                if pwmm[slot]:
                    K.wait("sync", "sMM", pwmm[slot])
                K.dma("sync", pwb[:, :], wdt.rearrange("o p c d -> o p (c d)")[oc], wsem)
                K.wait("tensor", wsem)
                K.wait("tensor", "sV", max(xbv, evac[psXn]))
                K.wait("tensor", "sS", evacS[psXn])
                for ci in range(NCH):
                    K.mm(psX[:, 0:SLAB], pwb[:, ci * DH:(ci + 1) * DH],
                         t["xb"][:, ci * SLAB:(ci + 1) * SLAB],
                         start=(ci == 0), stop=(ci == NCH - 1))
                pwmm[slot] = K.cnt["sMM"]
                hs = "hsc0" if it % 2 == 0 else "hsc1"
                K.wait("vector", "sMM")
                hv0 = K.v(lambda e, hs=hs, psX=psX: e.tensor_copy(
                    t[hs][:, 3:3 + SLAB], psX[:, 0:SLAB]))
                evac[psXn] = hv0
                st_off = it * 3
                K.v(lambda e, hs=hs, st_off=st_off: e.tensor_copy(
                    t[hs][:, 0:3], t["stash"][:, st_off:st_off + 3]))
                K.v(lambda e, hs=hs, st_off=st_off: e.tensor_copy(
                    t["stash"][:, st_off:st_off + 3], t[hs][:, SLAB:SLAB + 3]))
                cA = t["Gloc"][:, 0:SLAB]
                cB = t["Gloc"][:, SLAB:2 * SLAB]
                cwc = lambda dl: t["c_cwres"][:, pi * 32 + oc * 4 + dl:pi * 32 + oc * 4 + dl + 1]
                K.v(lambda e, hs=hs, cA=cA, w0=cwc(0): e.tensor_scalar(
                    cA, t[hs][:, 0:SLAB], w0, None, ALU.mult))
                K.v(lambda e, hs=hs, cA=cA, cB=cB, w1=cwc(1): e.scalar_tensor_tensor(
                    cB, t[hs][:, 1:1 + SLAB], w1, cA, ALU.mult, ALU.add))
                K.v(lambda e, hs=hs, cA=cA, cB=cB, w2=cwc(2): e.scalar_tensor_tensor(
                    cA, t[hs][:, 2:2 + SLAB], w2, cB, ALU.mult, ALU.add))
                cvv = K.v(lambda e, hs=hs, cA=cA, cB=cB, w3=cwc(3): e.scalar_tensor_tensor(
                    cB, t[hs][:, 3:3 + SLAB], w3, cA, ALU.mult, ALU.add))
                K.wait("scalar", "sV", cvv)
                sg = K.s(lambda e, cB=cB: e.activation(t["g1"][:, :], cB, AF.Sigmoid))
                K.wait("vector", "sS", sg)
                if dstname == "vv":
                    K.v(lambda e, cB=cB, oc=oc: e.tensor_mul(
                        t["vv"][:, oc * SLAB:(oc + 1) * SLAB], cB, t["g1"][:, :]))
                else:
                    K.v(lambda e, cB=cB: e.tensor_mul(
                        t["rawq"][:, :], cB, t["g1"][:, :]))
                    sqv = K.v(lambda e: e.tensor_mul(t["y1"][:, :], t["rawq"][:, :],
                                                     t["rawq"][:, :]))
                    K.wait("tensor", "sV", sqv)
                    K.mm(psX[:, 0:SLAB], t["c_onesblk16"][:, :], t["y1"][:, :])
                    K.wait("scalar", "sMM")
                    sc = 128.0 if dstname == "qf" else 1.0
                    epscol = "c_eps128" if dstname == "qf" else "c_epsv"
                    K.s(lambda e, psX=psX, sc=sc, epscol=epscol: e.activation(
                        t["g1"][:, :], psX[:, 0:SLAB], AF.Ln, bias=t[epscol][:, 0:1], scale=sc))
                    sd = K.s(lambda e: e.activation(t["rstdq"][:, :], t["g1"][:, :],
                                                    AF.Exp, scale=-0.5))
                    evacS[psXn] = sd
                    K.wait("vector", "sS", sd)
                    K.v(lambda e, dstname=dstname, oc=oc: e.tensor_mul(
                        t[dstname][:, oc * SLAB:(oc + 1) * SLAB], t["rawq"][:, :], t["rstdq"][:, :]))

    def slab_gates(s):
        K.wait("vector", "sP", K.cnt["sP"])
        K.wait("scalar", "sP", K.cnt["sP"])
        K.wait("vector", "sS", K.cnt["sS"])
        K.wait("tensor", "sV", evac["psP"])
        K.wait("tensor", "sS", evacS["psP"])
        for ci in range(NCH):
            K.mm(psP[:, 0:SLAB], t["wfa16"][:, ci * DH:(ci + 1) * DH],
                 t["xb"][:, ci * SLAB:(ci + 1) * SLAB], start=(ci == 0), stop=(ci == NCH - 1))
        K.wait("vector", "sMM")
        yv = K.v(lambda e: e.tensor_copy(t["y1"][:, :], psP[:, 0:SLAB]))
        evac["psP"] = yv
        for oc in range(HPC):
            psX, psXn = (psQ, "psQ") if oc % 2 == 0 else (psP, "psP")
            K.wait("tensor", "sV", max(evac[psXn], yv))
            K.wait("tensor", "sS", evacS[psXn])
            K.mm(psX[:, 0:SLAB], t["wfb16"][:, oc * DH:(oc + 1) * DH], t["y1"][:, :])
            K.wait("vector", "sMM")
            c1 = K.v(lambda e, psX=psX, oc=oc: e.scalar_tensor_tensor(
                t["g1"][:, :], psX[:, 0:SLAB], t["dtbc"][:, oc:oc + 1],
                t["c_m20"][:, 0:1].broadcast_to((DH, SLAB)), ALU.add, ALU.max))
            evac[psXn] = c1
            K.v(lambda e: e.tensor_scalar(t["g2"][:, :], t["g1"][:, :], 20.0, None, ALU.min))
            K.wait("scalar", "sV")
            K.s(lambda e: e.activation(t["g1"][:, :], t["g2"][:, :], AF.Exp))
            K.s(lambda e: e.activation(t["g2"][:, :], t["g1"][:, :], AF.Ln, bias=1.0))
            K.wait("vector", "sS")
            K.v(lambda e, oc=oc: e.tensor_scalar(
                t["gg"][:, oc * SLAB:(oc + 1) * SLAB], t["g2"][:, :],
                t["negA"][:, oc:oc + 1], None, ALU.mult))
    def gates_tail_gen(s):
        K.wait("tensor", "sV", evac["psP"])
        K.wait("tensor", "sS", evacS["psP"])
        for ci in range(NCH):
            K.mm(psP[:, 0:SLAB], t["wga16"][:, ci * DH:(ci + 1) * DH],
                 t["xb"][:, ci * SLAB:(ci + 1) * SLAB], start=(ci == 0), stop=(ci == NCH - 1))
        K.wait("vector", "sMM")
        yv = K.v(lambda e: e.tensor_copy(t["y1"][:, :], psP[:, 0:SLAB]))
        evac["psP"] = yv
        yield
        for oc in range(HPC):
            psX, psXn = (psQ, "psQ") if oc % 2 == 0 else (psP, "psP")
            K.wait("tensor", "sV", max(evac[psXn], yv))
            K.wait("tensor", "sS", evacS[psXn])
            K.mm(psX[:, 0:SLAB], t["wgb16"][:, oc * DH:(oc + 1) * DH], t["y1"][:, :])
            K.wait("scalar", "sMM")
            sg = K.s(lambda e, psX=psX, oc=oc: e.activation(
                t["gout"][:, oc * SLAB:(oc + 1) * SLAB], psX[:, 0:SLAB], AF.Sigmoid))
            evacS[psXn] = sg
            if oc % 2 == 1:
                yield
        K.wait("tensor", "sV", evac["psQ"])
        K.wait("tensor", "sS", evacS["psQ"])
        for ci in range(NCH):
            K.mm(psQ[0:HPC, 0:SLAB], t["wb16"][:, ci * HPC:(ci + 1) * HPC],
                 t["xb"][:, ci * SLAB:(ci + 1) * SLAB], start=(ci == 0), stop=(ci == NCH - 1))
        K.wait("scalar", "sMM")
        sg = K.s(lambda e: e.activation(t["beta"][0:HPC, :], psQ[0:HPC, 0:SLAB], AF.Sigmoid))
        evacS["psQ"] = sg
        yield
        K.wait("vector", "sS", sg)
        K.wait("vector", "sP", K.cnt["sP"])
        K.wait("vector", "sMM", K.cnt["sMM"])
        bv = K.v(lambda e: e.tensor_copy(t["y1"][0:HPC, 0:SLAB], t["beta"][0:HPC, :]))
        K.wait("tensor", "sV", K.cnt["sV"])
        K.wait("tensor", "sS", K.cnt["sS"])
        for c4_ in range(NCOL):
            K.tr(pst16[:, c4_ * 8:(c4_ + 1) * 8],
                 t["y1"][0:HPC, c4_ * CC:(c4_ + 1) * CC], t["c_ident16"][0:HPC, 0:HPC])
        btm = K.cnt["sMM"]
        K.wait("vector", "sMM", btm)
        K.v(lambda e: e.tensor_copy(
            t["betaTs"][:, 0:NCOL * 16].rearrange("p (c g) -> p c g", g=16)[:, :, 0:8],
            pst16[:, 0:NCOL * 8].rearrange("p (c h) -> p c h", h=8)))
        K.v(lambda e: e.tensor_scalar(
            t["betaTs"][:, 0:NCOL * 16].rearrange("p (c g) -> p c g", g=16)[:, :, 8:16],
            pst16[:, 0:NCOL * 8].rearrange("p (c h) -> p c h", h=8),
            -1.0, None, ALU.mult))

    def h_ap(name, h, cc, width=CC):
        return t[name][:, h * SLAB + cc * CC: h * SLAB + cc * CC + width]

    def w_ap(name, h, width=CC):
        return t[name][:, h * CC: h * CC + width]

    def w3(name):
        return t[name][:, 0:WD].rearrange("p (h x) -> p h x", x=CC)

    def h3(name, cc):
        return t[name][:, 0:HPC * SLAB].rearrange(
            "p (h x) -> p h x", x=SLAB)[:, :, cc * CC:(cc + 1) * CC]

    def sq_mms(dstps, lhs, rhs):
        for h in range(HPC):
            K.mm(dstps[:, h * CC:(h + 1) * CC], w_ap(lhs, h), w_ap(rhs, h))
        return K.cnt["sMM"]

    def A_gen(s, cc, aprev, bpar):
        p = cc % 2
        sx = str(p)
        K.wait("vector", "sS", aprev["sS"])
        for h in range(HPC):
            K.v(lambda e, h=h, cc=cc: e.tensor_tensor_scan(
                w_ap("G", h), h_ap("gg", h, cc), h_ap("gg", h, cc), 0.0, ALU.add, ALU.bypass))
        yield
        def gloc(e):
            g3 = t["G"][:, 0:WD].rearrange("p (h b i) -> p h b i", b=4, i=F0)
            r3 = t["G"][:, 0:WD].rearrange("p (h x) -> p h x", x=CC)[:, :, F0 // 2 - 1::F0]
            return e.tensor_tensor(
                t["Gloc"][:, 0:WD].rearrange("p (h b i) -> p h b i", b=4, i=F0),
                g3, r3.rearrange("p h (b o) -> p h b o", o=1).broadcast_to((DH, HPC, 4, F0)),
                ALU.subtract)
        K.v(gloc)
        c2 = K.v(lambda e: e.tensor_scalar(t["Gtmp"][:, 0:WD], t["Gloc"][:, 0:WD],
                                           CLAMP, None, ALU.min))
        yield
        K.wait("scalar", "sV", c2)
        K.wait("scalar", "sP", aprev["sP"])
        el = K.s(lambda e: e.activation(t["Eloc16"][:, 0:WD], t["Gtmp"][:, 0:WD], AF.Exp))
        yield
        K.wait("vector", "sS", el)
        c4 = K.v(lambda e: e.tensor_scalar(t["Gtmp"][:, 0:WD], t["Gloc"][:, 0:WD],
                                           -1.0, CLAMP, ALU.mult, ALU.min))
        yield
        K.wait("scalar", "sV", c4)
        ei = K.s(lambda e: e.activation(t["Einv"][:, 0:WD], t["Gtmp"][:, 0:WD], AF.Exp))
        K.s(lambda e: e.activation(t["EG"][:, 0:WD], t["G"][:, 0:WD], AF.Exp))
        K.wait("scalar", "sV", bpar["sV"])
        K.s(lambda e, sx=sx: e.activation(
            t["EGend" + sx][:, 0:HPC],
            t["G"][:, 0:WD].rearrange("p (h x) -> p h x", x=CC)[:, :, CC - 1], AF.Exp))
        yield
        K.wait("vector", "sS", ei)
        def k2arg(e):
            gend = t["G"][:, 0:WD].rearrange("p (h x) -> p h x", x=CC)[:, :, CC - 1]
            return e.tensor_tensor(
                t["Gtmp"][:, 0:WD].rearrange("p (h x) -> p h x", x=CC),
                gend.rearrange("p (h o) -> p h o", o=1).broadcast_to((DH, HPC, CC)),
                t["G"][:, 0:WD].rearrange("p (h x) -> p h x", x=CC), ALU.subtract)
        c6 = K.v(k2arg)
        K.wait("scalar", "sV", c6)
        ek = K.s(lambda e: e.activation(t["EK2"][:, 0:WD], t["Gtmp"][:, 0:WD], AF.Exp))
        yield
        K.wait("gpsimd", "sS", ek)
        K.wait("gpsimd", "sV", aprev["sV"])
        K.wait("gpsimd", "sMM", max(aprev["sMM"], bpar["sMM"]))
        qf3 = h3("qf", cc)
        kf3 = h3("kf", cc)
        for dst, a3, b3 in (("kbar", kf3, "Eloc16"), ("qbar", qf3, "Eloc16"),
                            ("ktil", kf3, "Einv"), ("kbarg" + sx, kf3, "EG"),
                            ("qbarg" + sx, qf3, "EG"), ("ktil2", kf3, "EK2")):
            K.p(lambda e, dst=dst, a3=a3, b3=b3: e.tensor_tensor(
                w3(dst), a3, w3(b3), ALU.mult))
        sc = K.p(lambda e, qf3=qf3, kf3=kf3: e.tensor_tensor(
            w3("qk16"), qf3, kf3, ALU.mult))
        yield
        K.wait("vector", "sS", ek)
        for h in range(HPC):
            def ea(e, h=h):
                r = t["G"][:, 0:WD].rearrange("p (h x) -> p h x", x=CC)[:, h, F0 // 2 - 1::F0]
                return e.tensor_tensor(
                    t["Gtmp"][:, h * 16:(h + 1) * 16].rearrange("p (i j) -> p i j", j=4),
                    r.rearrange("p (i o) -> p i o", o=1).broadcast_to((DH, 4, 4)),
                    r.rearrange("p (o j) -> p o j", o=1).broadcast_to((DH, 4, 4)),
                    ALU.subtract)
            K.v(ea)
        c8 = K.v(lambda e: e.tensor_scalar(t["Gloc"][:, 0:HPC * 16], t["Gtmp"][:, 0:HPC * 16],
                                           0.0, None, ALU.min))
        K.wait("scalar", "sV", c8)
        ea2 = K.s(lambda e: e.activation(t["EAll"][:, 0:HPC * 16], t["Gloc"][:, 0:HPC * 16],
                                         AF.Exp))
        K.wait("vector", "sS", ea2)
        K.v(lambda e: e.tensor_scalar(t["nEAll"][:, 0:HPC * 16], t["EAll"][:, 0:HPC * 16],
                                      -1.0, None, ALU.mult))
        yield
        wmm, amm = aprev["sMM"], aprev["sMM"]
        K.wait("tensor", "sP", sc)
        for I in range(4):
            if wmm:
                K.wait("vector", "sMM", wmm)
            def rbW(e, I=I):
                nk = t["ktil"][:, 0:WD].rearrange("p (h s) -> p h s", s=CC)[:, :, 0:(I + 1) * F0]
                scl = t["nEAll"][:, 0:HPC * 16].rearrange(
                    "p (h i j) -> p h i j", i=4, j=4)[:, :, I, 0:I + 1]
                return e.tensor_tensor(
                    t["rhsS0"][:, 0:WD].rearrange("p (h s) -> p h s", s=CC)[:, :, 0:(I + 1) * F0]
                        .rearrange("p h (j f) -> p h j f", f=F0),
                    nk.rearrange("p h (j f) -> p h j f", f=F0),
                    scl.rearrange("p h (j o) -> p h j o", o=1).broadcast_to((DH, HPC, I + 1, F0)),
                    ALU.mult)
            rb = K.v(rbW)
            tp = (0, 96) if I == 3 else None
            K.wait("tensor", "sV", rb)
            for h in range(HPC):
                K.mm(psga[I * F0:(I + 1) * F0, h * CC:h * CC + (I + 1) * F0],
                     t["kbar"][:, h * CC + I * F0:h * CC + (I + 1) * F0],
                     t["rhsS0"][:, h * CC:h * CC + (I + 1) * F0], tp=tp)
            wmm = K.cnt["sMM"]
            if amm:
                K.wait("vector", "sMM", amm)
            def rbA(e, I=I):
                qb = t["qbar"][:, 0:WD].rearrange("p (h s) -> p h s", s=CC)[:, :, I * F0:CC]
                scl = t["nEAll"][:, 0:HPC * 16].rearrange(
                    "p (h i j) -> p h i j", i=4, j=4)[:, :, I:4, I]
                return e.tensor_tensor(
                    t["rhsS1"][:, 0:WD].rearrange("p (h s) -> p h s", s=CC)[:, :, I * F0:CC]
                        .rearrange("p h (j f) -> p h j f", f=F0),
                    qb.rearrange("p h (j f) -> p h j f", f=F0),
                    scl.rearrange("p h (j o) -> p h j o", o=1).broadcast_to((DH, HPC, 4 - I, F0)),
                    ALU.mult)
            rb2 = K.v(rbA)
            K.wait("tensor", "sV", rb2)
            for h in range(HPC):
                K.mm(psgb[I * F0:(I + 1) * F0, h * CC + I * F0:h * CC + CC],
                     t["ktil"][:, h * CC + I * F0:h * CC + (I + 1) * F0],
                     t["rhsS1"][:, h * CC + I * F0:h * CC + CC], tp=tp)
            amm = K.cnt["sMM"]
            yield
        K.wait("vector", "sMM", amm)
        lt = K.v(lambda e: e.select(
            t["Lt"][:, 0:WD], t["c_maskSL"][:, :], psga[:, 0:WD],
            t["c_zero"][:, 0:1].broadcast_to((CC, WD)), add_drain=True))
        K.wait("vector", "sMM", bpar["sMM"])
        ap_ = K.v(lambda e, sx=sx: e.select(
            t["App" + sx][:, 0:WD], t["c_maskSU"][:, :], psgb[:, 0:WD],
            t["c_zero"][:, 0:1].broadcast_to((CC, WD)), add_drain=True))
        ltm = K.v(lambda e, cc=cc: e.tensor_tensor(
            w3("Lt"), w3("Lt"),
            t["betaTs"][:, cc * 16 + 8:cc * 16 + 16].rearrange("p (h o) -> p h o", o=1)
                .broadcast_to((CC, HPC, CC)), ALU.mult))
        yield
        K.wait("tensor", "sV", ltm)
        for h in range(HPC):
            K.tr(pst16[:, h * CC:(h + 1) * CC], w_ap("Lt", h), t["c_ident16"][:, :])
        lttm = K.cnt["sMM"]
        K.wait("vector", "sMM", lttm)
        ltt = K.v(lambda e: e.tensor_copy(t["LtT"][:, 0:WD], pst16[:, 0:WD]))
        yield
        K.wait("tensor", "sV", ltt)
        sq_mms(psga, "Lt", "LtT")
        mmL = sq_mms(psgb, "LtT", "Lt")
        K.wait("vector", "sMM", mmL)
        t0c = K.v(lambda e: e.tensor_copy(t["T0"][:, 0:WD], psga[:, 0:WD]))
        K.wait("scalar", "sMM", mmL)
        pac = K.s(lambda e: e.copy(t["Pa"][:, 0:WD], psgb[:, 0:WD]))
        yield
        K.wait("tensor", "sV", t0c)
        K.wait("tensor", "sS", pac)
        sq_mms(psga, "Pa", "T0")
        mmL = sq_mms(psgb, "T0", "Pa")
        K.wait("vector", "sMM", mmL)
        t1c = K.v(lambda e: e.tensor_copy(t["T1"][:, 0:WD], psga[:, 0:WD]))
        K.wait("scalar", "sMM", mmL)
        pbc = K.s(lambda e: e.copy(t["Pb"][:, 0:WD], psgb[:, 0:WD]))
        yield
        K.wait("gpsimd", "sV", t1c)
        K.wait("gpsimd", "sS", pbc)
        c3t_p = K.p(lambda e: e.tensor_tensor(t["C3Tp"][:, 0:WD], t["T0"][:, 0:WD],
                                              t["T1"][:, 0:WD], ALU.add))
        c3p_p = K.p(lambda e: e.tensor_tensor(t["C3p"][:, 0:WD], t["Pa"][:, 0:WD],
                                              t["Pb"][:, 0:WD], ALU.add))
        K.wait("tensor", "sV", t1c)
        K.wait("tensor", "sS", pbc)
        sq_mms(psga, "Pa", "T1")
        mmL = sq_mms(psgb, "T1", "Pa")
        K.wait("vector", "sMM", mmL)
        K.wait("vector", "sP", c3p_p)
        c3t = K.v(lambda e: e.tensor_tensor(t["C3Tp"][:, 0:WD], psga[:, 0:WD],
                                            t["C3Tp"][:, 0:WD], ALU.add))
        c3pl = K.v(lambda e: e.tensor_tensor(t["C3p"][:, 0:WD], psgb[:, 0:WD],
                                             t["C3p"][:, 0:WD], ALU.add))
        yield
        K.wait("tensor", "sV", c3pl)
        sq_mms(psga, "Pb", "T1")
        mmL = sq_mms(psgb, "T1", "Pb")
        K.wait("vector", "sMM", mmL)
        K.wait("vector", "sP", c3t_p)
        t0c = K.v(lambda e: e.tensor_copy(t["T0"][:, 0:WD], psga[:, 0:WD]))
        K.wait("scalar", "sMM", mmL)
        K.wait("scalar", "sP", c3p_p)
        pac = K.s(lambda e: e.copy(t["Pa"][:, 0:WD], psgb[:, 0:WD]))
        yield
        K.wait("tensor", "sV", t0c)
        K.wait("tensor", "sS", pac)
        sq_mms(psga, "Pa", "T0")
        mmL = sq_mms(psgb, "T0", "Pa")
        K.wait("vector", "sMM", mmL)
        K.wait("vector", "sP", c3t_p)
        t1c = K.v(lambda e: e.tensor_copy(t["T1"][:, 0:WD], psga[:, 0:WD]))
        K.wait("scalar", "sMM", mmL)
        K.wait("scalar", "sP", c3p_p)
        pbc = K.s(lambda e: e.copy(t["Pb"][:, 0:WD], psgb[:, 0:WD]))
        yield
        K.wait("gpsimd", "sV", t1c)
        K.wait("gpsimd", "sMM", bpar["sMM"])
        c2t_p = K.p(lambda e, sx=sx: e.tensor_tensor(
            t["C2T" + sx][:, 0:WD], t["T0"][:, 0:WD], t["T1"][:, 0:WD], ALU.add))
        K.wait("tensor", "sV", t1c)
        mmL = sq_mms(psga, "Pa", "T1")
        K.wait("vector", "sMM", mmL)
        K.wait("vector", "sP", c2t_p)
        c2t = K.v(lambda e, sx=sx: e.tensor_tensor(
            t["C2T" + sx][:, 0:WD], psga[:, 0:WD], t["C2T" + sx][:, 0:WD], ALU.add))
        yield
        K.wait("tensor", "sV", c2t)
        K.wait("tensor", "sS", pbc)
        sq_mms(psga, "Pb", "T1")
        mmL = sq_mms(psgb, "T1", "Pb")
        K.wait("vector", "sMM", mmL)
        K.wait("vector", "sP", c2t_p)
        t0c = K.v(lambda e: e.tensor_copy(t["T0"][:, 0:WD], psga[:, 0:WD]))
        K.wait("scalar", "sMM", mmL)
        pac = K.s(lambda e: e.copy(t["Pa"][:, 0:WD], psgb[:, 0:WD]))
        yield
        K.wait("tensor", "sV", t0c)
        K.wait("tensor", "sS", pac)
        mmL = sq_mms(psga, "Pa", "T0")
        K.wait("vector", "sMM", mmL)
        K.wait("vector", "sP", c2t_p)
        t1c = K.v(lambda e: e.tensor_copy(t["T1"][:, 0:WD], psga[:, 0:WD]))
        yield
        K.wait("gpsimd", "sV", t1c)
        K.wait("gpsimd", "sMM", bpar["sMM"])
        c1t_p = K.p(lambda e, sx=sx: e.tensor_tensor(
            t["C1T" + sx][:, 0:WD], t["T0"][:, 0:WD], t["T1"][:, 0:WD], ALU.add))
        K.wait("tensor", "sV", t1c)
        mmL = sq_mms(psga, "Pa", "T1")
        K.wait("vector", "sMM", mmL)
        K.wait("vector", "sP", c1t_p)
        K.v(lambda e, sx=sx: e.tensor_tensor(
            t["C1T" + sx][:, 0:WD], psga[:, 0:WD], t["C1T" + sx][:, 0:WD], ALU.add))
        yield
        K.wait("tensor", "sV", c3pl)
        K.wait("tensor", "sS", pac)
        mmL = sq_mms(psgb, "C3p", "LtT")
        K.wait("gpsimd", "sV", c3t)
        K.wait("gpsimd", "sMM", bpar["sMM"])
        c4t_p = K.p(lambda e, sx=sx: e.tensor_tensor(
            t["C4T" + sx][:, 0:WD], t["C3Tp"][:, 0:WD], t["LtT"][:, 0:WD], ALU.subtract))
        K.wait("vector", "sMM", mmL)
        K.wait("vector", "sP", c4t_p)
        c4t = K.v(lambda e, sx=sx: e.tensor_tensor(
            t["C4T" + sx][:, 0:WD], t["C4T" + sx][:, 0:WD], psgb[:, 0:WD], ALU.subtract))
        yield
        K.wait("tensor", "sV", c4t)
        K.wait("tensor", "sP", sc)
        for h in range(HPC):
            K.tr(pst16[:, h * CC:(h + 1) * CC], w_ap("ktil2", h), t["c_ident16"][:, :])
        k2m = K.cnt["sMM"]
        for h in range(HPC):
            K.mm(psgb[:, h:h + 1], w_ap("qk16", h), t["c_onescol16"][:, 0:1])
        dcm = K.cnt["sMM"]
        K.wait("vector", "sMM", k2m)
        K.wait("vector", "sMM", bpar["sMM"])
        K.v(lambda e, sx=sx: e.tensor_copy(t["k2t" + sx][:, 0:WD], pst16[:, 0:WD]))
        K.wait("vector", "sMM", dcm)
        K.wait("vector", "sP", bpar["sP"])
        K.v(lambda e, sx=sx: e.tensor_copy(t["dcol" + sx][:, 0:HPC], psgb[:, 0:HPC]))
        yield

    def B_gen(s, cc, bprev, colg):
        p = cc % 2
        sx = str(p)
        Scur = "S" if colg % 2 == 0 else "S2"
        Snxt = "S2" if colg % 2 == 0 else "S"
        K.wait("tensor", "sV", bprev["sV"])
        K.wait("tensor", "sP", bprev["sP"])
        K.wait("tensor", "sS", bprev["sS"])
        for h in range(HPC):
            K.mm(psal[:, h * CC:(h + 1) * CC], w_ap("kbarg" + sx, h),
                 t["S16"][:, h * DH:(h + 1) * DH], start=True, stop=False)
            K.mm(psal[:, h * CC:(h + 1) * CC], h_ap("vv", h, cc), t["c_negident16"][:, :],
                 start=False, stop=True)
        rmm = K.cnt["sMM"]
        yield
        K.wait("vector", "sMM", rmm)
        K.wait("vector", "sS", bprev["sS"])
        K.wait("vector", "sP", bprev["sP"])
        rt = K.v(lambda e, cc=cc: e.tensor_tensor(
            w3("Rt"), psal[:, 0:WD].rearrange("p (h x) -> p h x", x=CC),
            t["betaTs"][:, cc * 16:cc * 16 + 8].rearrange("p (h o) -> p h o", o=1)
                .broadcast_to((CC, HPC, CC)), ALU.mult))
        yield
        cur, ev, m_ = "Rt", rt, 0
        for fac_, nxt in (("C1T", "Ya"), ("C2T", "Rt"), ("C4T", "Ya")):
            K.wait("tensor", "sV", ev)
            for h in range(HPC):
                K.mm(psal[:, h * CC:(h + 1) * CC], w_ap(fac_ + sx, h), w_ap(cur, h))
            m_ = K.cnt["sMM"]
            K.wait("vector", "sMM", m_)
            ev = K.v(lambda e, nxt=nxt, cur=cur: e.tensor_tensor(
                t[nxt][:, 0:WD], psal[:, 0:WD], t[cur][:, 0:WD], ALU.add))
            cur = nxt
            yield
        ya2, lastapply = ev, m_
        K.wait("tensor", "sV", ya2)
        for h in range(HPC):
            K.mm(psal[:, h * DH:(h + 1) * DH], w_ap("k2t" + sx, h), w_ap("Ya", h))
        m4 = K.cnt["sMM"]
        K.wait("vector", "sS", bprev["sS"])
        K.v(lambda e: e.tensor_tensor(
            t[Snxt][:, 0:WD].rearrange("p (h x) -> p h x", x=DH),
            t[Scur][:, 0:WD].rearrange("p (h x) -> p h x", x=DH),
            t["EGend" + sx][:, 0:HPC].rearrange("p (h o) -> p h o", o=1)
                .broadcast_to((DH, HPC, DH)), ALU.mult))
        K.wait("vector", "sMM", m4)
        st = K.v(lambda e: e.tensor_tensor(
            t[Snxt][:, 0:WD], t[Snxt][:, 0:WD], psal[:, 0:WD], ALU.subtract))
        yield
        K.wait("vector", "sMM", lastapply)
        rtd = K.v(lambda e: e.tensor_tensor(
            w3("Rt"), w3("Ya"),
            t["dcol" + sx][:, 0:HPC].rearrange("p (h o) -> p h o", o=1)
                .broadcast_to((CC, HPC, CC)), ALU.mult))
        K.wait("tensor", "sV", rtd)
        K.wait("tensor", "sS", bprev["sS"])
        K.wait("tensor", "sV", bprev["sV"])
        o, omm = 0, 0
        for g in range(2):
            if g == 1:
                K.wait("tensor", "sV", o)
            for h in range(4 * g, 4 * g + 4):
                hc = (h - 4 * g) * CC
                K.mm(psbe[:, hc:hc + CC], t["S16"][:, h * DH:(h + 1) * DH],
                     w_ap("qbarg" + sx, h), start=True, stop=False)
                K.mm(psbe[:, hc:hc + CC], w_ap("Ya", h), w_ap("App" + sx, h),
                     start=False, stop=False)
                K.mm(psbe[:, hc:hc + CC], w_ap("Rt", h), t["c_negident16"][:, :],
                     start=False, stop=True)
            omm = K.cnt["sMM"]
            K.wait("vector", "sMM", omm)
            o = K.v(lambda e, g=g: e.tensor_copy(
                t["o16"][:, g * SLAB:(g + 1) * SLAB], psbe[:, 0:SLAB]))
        K.v(lambda e: e.tensor_copy(t["S16"][:, 0:HPC * DH], t[Snxt][:, 0:HPC * DH]))
        yield
        K.wait("gpsimd", "sV", o)
        K.wait("gpsimd", "sOG", bprev["sOG"])
        sq = K.p(lambda e: e.tensor_tensor(t["osq"][:, 0:WD], t["o16"][:, 0:WD],
                                           t["o16"][:, 0:WD], ALU.mult))
        K.wait("tensor", "sV", o)
        K.wait("tensor", "sP", sq)
        K.mm(psbe[:, 0:SLAB], t["c_onesblk16"][:, :], t["osq"][:, 0:SLAB])
        smm = K.cnt["sMM"]
        K.wait("scalar", "sMM", smm)
        K.wait("scalar", "sV", st)
        ln1 = K.s(lambda e, Scur=Scur: e.activation(t[Scur][:, 0:SLAB], psbe[:, 0:SLAB],
                                                    AF.Ln, bias=t["c_epsv"][:, 0:1],
                                                    scale=1.0 / DH))
        K.wait("tensor", "sS", ln1)
        K.mm(psbe[:, 0:SLAB], t["c_onesblk16"][:, :], t["osq"][:, SLAB:WD])
        smm = K.cnt["sMM"]
        yield
        K.wait("scalar", "sMM", smm)
        K.s(lambda e, Scur=Scur: e.activation(t[Scur][:, SLAB:WD], psbe[:, 0:SLAB],
                                              AF.Ln, bias=t["c_epsv"][:, 0:1],
                                              scale=1.0 / DH))
        rs = K.s(lambda e, Scur=Scur: e.activation(t["rstd16"][:, 0:WD], t[Scur][:, 0:WD],
                                                   AF.Exp, scale=-0.5))
        K.wait("gpsimd", "sS", rs)
        gmp = K.p(lambda e: e.tensor_tensor(t["rstd16"][:, 0:WD], t["rstd16"][:, 0:WD],
                                            t["o16"][:, 0:WD], ALU.mult))
        K.wait("vector", "sP", gmp)
        og = K.v(lambda e, cc=cc: e.scalar_tensor_tensor(
            w3("osq"), w3("rstd16"), t["nwcol"][:, 0:1], h3("gout", cc),
            ALU.mult, ALU.mult))
        K.wait("gpsimd", "sV", og)
        for h in range(HPC):
            K.dma("gpsimd", ib[s][h * DH:(h + 1) * DH, cc * CC:(cc + 1) * CC],
                  w_ap("osq", h), "sOG")
        if debug:
            K.dma("gpsimd",
                  K.dr["d_og"].rearrange("p (h x) -> p h x", x=T)[:, :, s * SLAB + cc * CC:
                                                                 s * SLAB + (cc + 1) * CC],
                  w3("osq"), "sO")
        yield

    bendp = [dict(sMM=0, sV=0, sS=0, sP=0, sOG=0), dict(sMM=0, sV=0, sS=0, sP=0, sOG=0)]
    colg = [0]
    for s in range(NSLAB):
        evac["psP"] = K.cnt["sV"]
        evac["psQ"] = K.cnt["sV"]
        evacS["psP"] = K.cnt["sS"]
        evacS["psQ"] = K.cnt["sS"]
        slab_qkv(s)
        slab_gates(s)
        if s == 0:
            K.wait("vector", "sD", fac_out_dma)
            K.v(lambda e: e.memset(t["S"][:, :], 0.0))
            K.v(lambda e: e.memset(t["S2"][:, :], 0.0))
            K.v(lambda e: e.memset(t["S16"][:, :], 0.0))
        pend = K.snap()
        tgen = gates_tail_gen(s)
        agen0 = A_gen(s, 0, pend, bendp[0])
        while tgen is not None or agen0 is not None:
            if tgen is not None:
                try:
                    next(tgen)
                except StopIteration:
                    tgen = None
            if agen0 is not None:
                try:
                    next(agen0)
                except StopIteration:
                    agen0 = None
        pend = K.snap()
        aprev = pend
        bprev = pend
        for cc in range(1, NCOL + 1):
            bgen = B_gen(s, cc - 1, bprev, colg[0])
            agen = A_gen(s, cc, aprev, bendp[cc % 2]) if cc < NCOL else None
            while bgen is not None or agen is not None:
                if bgen is not None:
                    try:
                        next(bgen)
                    except StopIteration:
                        bgen = None
                if agen is not None:
                    for _ in range(2):
                        try:
                            next(agen)
                        except StopIteration:
                            agen = None
                            break
            pend = K.snap()
            bendp[(cc - 1) % 2] = pend
            bprev = pend
            aprev = pend
            colg[0] += 1
        if debug:
            for nm, buf in (("d_qf", "qf"), ("d_kf", "kf"), ("d_v", "vv"),
                            ("d_gout", "gout")):
                K.dma("gpsimd",
                      K.dr[nm].rearrange("p (h x) -> p h x", x=T)[:, :, s * SLAB:(s + 1) * SLAB],
                      t[buf][:, 0:HPC * SLAB].rearrange("p (h x) -> p h x", x=SLAB), "sO")
            K.dma("gpsimd",
                  K.dr["d_g"].rearrange("p (h x) -> p h x", x=T)[:, :, s * SLAB:(s + 1) * SLAB],
                  t["gg"][:, 0:HPC * SLAB].rearrange("p (h x) -> p h x", x=SLAB), "sO")
            K.dma("gpsimd", K.dr["d_beta"][:, s * SLAB:(s + 1) * SLAB], t["beta"][0:HPC, :], "sO")
        if s > 0:
            K.wait("gpsimd", f"sCC{s - 1}", 1)
            K.dma("gpsimd", obAll[:, (s - 1) * SLAB:s * SLAB], obs[s - 1][:, :], "sD")
        K.wait("gpsimd", "sOG")
        K.inc("gpsimd", lambda e, s=s: e.collective_compute(
            "AllGather", ALU.bypass, replica_groups=[[0, 1], [2, 3], [4, 5], [6, 7]],
            ins=[ib[s][:, :].opt()], outs=[obs[s][:, :].opt()]), f"sCC{s}", 1)
    K.wait("gpsimd", f"sCC{NSLAB - 1}", 1)
    K.dma("gpsimd", obAll[:, (NSLAB - 1) * SLAB:NSLAB * SLAB], obs[NSLAB - 1][:, :], "sD")

    if stop_after == "scan":
        K.wait("gpsimd", "sD")
        K.wait("gpsimd", "sO", K.cnt["sO"])
        K.run()
        return nc, K

    SL_V, SL_MM, SL_D, SL_S = K.cnt["sV"], K.cnt["sMM"], K.cnt["sD"], K.cnt["sS"]
    slab_stack.close()
    post = contextlib.ExitStack()
    po = K.sbuf("po", [DH, NKCH * SUB], BF16, stack=post)
    pwf0 = K.sbuf("pwf0", [DH, NCH * DH], F32, stack=post)
    pwf1 = K.sbuf("pwf1", [DH, NCH * DH], F32, stack=post)
    pz = K.sbuf("pz", [DH, NCH * SUB], F32, stack=post)
    pzb = K.sbuf("pzb", [DH, NCH * SUB], BF16, stack=post)
    px = K.sbuf("px", [DH, NCH * SUB], F32, stack=post)
    pwa = K.sbuf("pwa", [DH, NKCH * DH], BF16, stack=post)
    pwb_ = K.sbuf("pwb", [DH, NKCH * DH], BF16, stack=post)
    prow = K.sbuf("prow", [1, 8 * SUB], F32, stack=post)
    pab = K.sbuf("pab", [DH, 2 * SUB], F32, stack=post)
    pzt = K.sbuf("pzt", [CC, 4 * DM], F32, stack=post)
    pcol = K.sbuf("pcol", [CC, 40], F32, stack=post)

    for eng in ("sync", "gpsimd", "vector", "scalar", "tensor"):
        K.wait(eng, "sV", SL_V)
        K.wait(eng, "sMM", SL_MM)
        K.wait(eng, "sD", SL_D)
        K.wait(eng, "sS", SL_S)

    pwcnt = [0, 0]
    pwcast = [0, 0]
    pevac = {"psP": 0, "psQ": 0}

    _offcache = {}
    def _post_off(e, i):
        key = (id(e), i)
        if key not in _offcache:
            pk = (id(e), "pid")
            if pk not in _offcache:
                _offcache[pk] = e.partition_id()
            _offcache[key] = (_offcache[pk] % 2) * OWN + i * SUB
        return _offcache[key]

    for i in range(NSUB):
        K.wait("sync", "sMM", K.cnt["sMM"])
        K.wait("sync", "sV", K.cnt["sV"])
        def ld_o(e, i=i):
            off = _post_off(e, i)
            return e.dma_start(out=po[:, :],
                               in_=obAll.rearrange("(c p) t -> p c t", p=DH)[:, :, bass.ds(off, SUB)])
        K.inc("sync", ld_o, "sXa", 16)
        def ld_x(e, i=i):
            off = _post_off(e, i)
            return e.dma_start(out=px[:, :],
                               in_=xT.rearrange("(c p) t -> p c t", p=DH)[:, :, bass.ds(off, SUB)])
        K.inc("sync", ld_x, "sXb", 16)
        XDa, XDb = K.cnt["sXa"], K.cnt["sXb"]
        for dc in range(NCH):
            slot = dc % 2
            pwt = pwa if slot == 0 else pwb_
            if pwcnt[slot]:
                K.wait("sync", "sMM", pwcnt[slot])
            wsem = "sWa" if slot == 0 else "sWb"
            K.dma("sync", pwt[:, :], wo16t.rearrange("o p c d -> o p (c d)")[dc], wsem)
            psX, psXn = (psP, "psP") if dc % 2 == 0 else (psQ, "psQ")
            K.wait("tensor", wsem)
            K.wait("tensor", "sXa", XDa)
            K.wait("tensor", "sXb", XDb)
            K.wait("tensor", "sV", pevac[psXn])
            for c in range(NKCH):
                K.mm(psX[:, 0:SUB], pwt[:, c * DH:(c + 1) * DH], po[:, c * SUB:(c + 1) * SUB],
                     start=(c == 0), stop=(c == NKCH - 1))
            pwcnt[slot] = K.cnt["sMM"]
            K.wait("vector", "sMM")
            zv = K.v(lambda e, dc=dc, psX=psX: e.tensor_tensor(
                pz[:, dc * SUB:(dc + 1) * SUB], psX[:, 0:SUB],
                px[:, dc * SUB:(dc + 1) * SUB], ALU.add))
            pevac[psXn] = zv
        K.v(lambda e: e.tensor_copy(pzb[:, :], pz[:, :]))
        sqv = K.v(lambda e: e.tensor_mul(po[:, 0:NCH * SUB], pzb[:, :], pzb[:, :]))
        K.wait("tensor", "sV", sqv)
        for c in range(NCH):
            K.mm(psP[0:1, 0:SUB], t["c_onescol16"][:, 0:1], pzb[:, c * SUB:(c + 1) * SUB],
                 start=(c == 0), stop=(c == NCH - 1))
        for c in range(NCH):
            K.mm(psQ[0:1, 0:SUB], t["c_onescol16"][:, 0:1], po[:, c * SUB:(c + 1) * SUB],
                 start=(c == 0), stop=(c == NCH - 1))
        K.wait("vector", "sMM")
        K.v(lambda e: e.tensor_scalar(prow[0:1, 0:SUB], psP[0:1, 0:SUB], 1.0 / DM, None, ALU.mult))
        K.v(lambda e: e.tensor_mul(prow[0:1, SUB:2 * SUB], prow[0:1, 0:SUB], prow[0:1, 0:SUB]))
        vv_ = K.v(lambda e: e.scalar_tensor_tensor(
            prow[0:1, 2 * SUB:3 * SUB], psQ[0:1, 0:SUB], 1.0 / DM,
            prow[0:1, SUB:2 * SUB], ALU.mult, ALU.subtract))
        pevac["psP"] = vv_; pevac["psQ"] = vv_
        K.wait("scalar", "sV")
        K.s(lambda e: e.activation(prow[0:1, 3 * SUB:4 * SUB], prow[0:1, 2 * SUB:3 * SUB],
                                   AF.Ln, bias=t["c_epsln1"][0:1, 0:1]))
        av = K.s(lambda e: e.activation(prow[0:1, 4 * SUB:5 * SUB], prow[0:1, 3 * SUB:4 * SUB],
                                        AF.Exp, scale=-0.5))
        K.wait("vector", "sS", av)
        K.v(lambda e: e.tensor_mul(prow[0:1, 5 * SUB:6 * SUB], prow[0:1, 0:SUB],
                                   prow[0:1, 4 * SUB:5 * SUB]))
        bv = K.v(lambda e: e.tensor_scalar(prow[0:1, 6 * SUB:7 * SUB], prow[0:1, 5 * SUB:6 * SUB],
                                           -1.0, None, ALU.mult))
        K.wait("tensor", "sV", bv)
        K.mm(psP[:, 0:SUB], t["c_onesrowf"][0:1, :], prow[0:1, 4 * SUB:5 * SUB])
        K.mm(psQ[:, 0:SUB], t["c_onesrowf"][0:1, :], prow[0:1, 6 * SUB:7 * SUB])
        K.wait("vector", "sMM")
        K.v(lambda e: e.tensor_copy(pab[:, 0:SUB], psP[:, 0:SUB]))
        abv = K.v(lambda e: e.tensor_copy(pab[:, SUB:2 * SUB], psQ[:, 0:SUB]))
        pevac["psP"] = abv; pevac["psQ"] = abv
        for c in range(NCH):
            K.v(lambda e, c=c: e.tensor_mul(px[:, c * SUB:(c + 1) * SUB],
                                            pz[:, c * SUB:(c + 1) * SUB], pab[:, 0:SUB]))
            K.v(lambda e, c=c: e.tensor_tensor(
                px[:, c * SUB:(c + 1) * SUB], px[:, c * SUB:(c + 1) * SUB],
                pab[:, SUB:2 * SUB], ALU.add))
        xbv = K.v(lambda e: e.tensor_copy(pzb[:, :], px[:, :]))
        for dc in range(NCH):
            slot = dc % 2
            pwt = pwa if slot == 0 else pwb_
            pwf = pwf0 if slot == 0 else pwf1
            K.wait("sync", "sV", pwcast[slot])
            wsem = "sWa" if slot == 0 else "sWb"
            K.dma("sync", pwf[:, :],
                  wsumf.rearrange("(c p) (o d) -> o p c d", p=DH, d=DH)[dc], wsem)
            psX, psXn = (psP, "psP") if dc % 2 == 0 else (psQ, "psQ")
            K.wait("vector", wsem)
            K.wait("vector", "sMM", pwcnt[slot])
            pwcast[slot] = K.v(lambda e, pwt=pwt, pwf=pwf: e.tensor_copy(pwt[:, :], pwf[:, :]))
            K.wait("tensor", "sV", pwcast[slot])
            K.wait("tensor", "sV", max(xbv, pevac[psXn]))
            for c in range(NCH):
                K.mm(psX[:, 0:SUB], pwt[:, c * DH:(c + 1) * DH], pzb[:, c * SUB:(c + 1) * SUB],
                     start=(c == 0), stop=False)
            K.mm(psX[:, 0:SUB], t["fb8"][0:1, dc * DH:(dc + 1) * DH],
                 t["c_onesrow16"][0:1, 0:SUB], start=False, stop=True)
            pwcnt[slot] = K.cnt["sMM"]
            K.wait("vector", "sMM")
            z2v = K.v(lambda e, dc=dc, psX=psX: e.scalar_tensor_tensor(
                pz[:, dc * SUB:(dc + 1) * SUB], psX[:, 0:SUB], float(1.0 / np.sqrt(8.0)),
                px[:, dc * SUB:(dc + 1) * SUB], ALU.mult, ALU.add))
            pevac[psXn] = z2v
        K.wait("tensor", "sV")
        for tt in range(4):
            for g in range(2):
                for c8 in range(8):
                    c = g * 8 + c8
                    K.tr(psal[:, c8 * DH:(c8 + 1) * DH],
                         pz[:, c * SUB + tt * DH:c * SUB + (tt + 1) * DH], t["c_ident"][:, :])
                K.wait("vector", "sMM")
                K.v(lambda e, tt=tt, g=g: e.tensor_copy(
                    pzt[:, tt * DM + g * 8 * DH:tt * DM + (g + 1) * 8 * DH], psal[:, 0:8 * DH]))
                K.wait("tensor", "sV")
        z3 = pzt[:, 0:4 * DM].rearrange("p (a d) -> p a d", d=DM)
        K.v(lambda e, z3=z3: e.reduce_sum(
            pcol[:, 0:4].rearrange("p (a o) -> p a o", o=1), z3, mybir.AxisListType.X))
        K.v(lambda e: e.tensor_mul(pz[:, 0:4 * DM], pzt[:, 0:4 * DM], pzt[:, 0:4 * DM]))
        K.v(lambda e: e.reduce_sum(
            pcol[:, 4:8].rearrange("p (a o) -> p a o", o=1),
            pz[:, 0:4 * DM].rearrange("p (a d) -> p a d", d=DM), mybir.AxisListType.X))
        K.v(lambda e: e.tensor_scalar(pcol[:, 8:12], pcol[:, 0:4], 1.0 / DM, None, ALU.mult))
        K.v(lambda e: e.tensor_mul(pcol[:, 12:16], pcol[:, 8:12], pcol[:, 8:12]))
        cv = K.v(lambda e: e.scalar_tensor_tensor(
            pcol[:, 16:20], pcol[:, 4:8], 1.0 / DM, pcol[:, 12:16], ALU.mult, ALU.subtract))
        K.wait("scalar", "sV", cv)
        K.s(lambda e: e.activation(pcol[:, 20:24], pcol[:, 16:20], AF.Ln,
                                   bias=t["c_epslnv"][:, 0:1]))
        sv2 = K.s(lambda e: e.activation(pcol[:, 24:28], pcol[:, 20:24], AF.Exp, scale=-0.5))
        K.wait("vector", "sS", sv2)
        K.v(lambda e: e.tensor_mul(pcol[:, 28:32], pcol[:, 8:12], pcol[:, 24:28]))
        K.v(lambda e: e.tensor_scalar(pcol[:, 32:36], pcol[:, 28:32], -1.0, None, ALU.mult))
        K.v(lambda e, z3=z3: e.tensor_tensor(
            z3, z3,
            pcol[:, 24:28].rearrange("p (a o) -> p a o", o=1).broadcast_to((CC, 4, DM)),
            ALU.mult))
        ov = K.v(lambda e, z3=z3: e.tensor_tensor(
            z3, z3,
            pcol[:, 32:36].rearrange("p (a o) -> p a o", o=1).broadcast_to((CC, 4, DM)),
            ALU.add))
        K.wait("gpsimd", "sV", ov)
        K.dma("gpsimd", out[i * SUB:(i + 1) * SUB, :].rearrange("(a p) d -> p a d", p=CC),
              pzt[:, 0:4 * DM].rearrange("p (a d) -> p a d", d=DM), "sO")

    K.wait("gpsimd", "sO", K.cnt["sO"])
    K.run()
    return nc, K


def _host_prep(inputs, core):
    b, hh = core // 2, core % 2
    hs = slice(HCH * hh, HCH * hh + HCH)
    hsl = slice(HPC * hh, HPC * hh + HPC)
    f32 = np.float32
    x = np.asarray(inputs["L0_attn_x"], f32)
    d = {}
    d["xT"] = np.ascontiguousarray(x[b].T)
    d["wq"] = np.ascontiguousarray(np.asarray(inputs["L0_attn_Wq"], f32)[:, hs])
    d["wk"] = np.ascontiguousarray(np.asarray(inputs["L0_attn_Wk"], f32)[:, hs])
    d["wv"] = np.ascontiguousarray(np.asarray(inputs["L0_attn_Wv"], f32)[:, hs])
    cw = np.zeros((DH, 3 * 4 * HPC), f32)
    for pi, nm in enumerate(("conv_q", "conv_k", "conv_v")):
        w = np.asarray(inputs[f"L0_attn_{nm}"], f32)[:, hs].reshape(4, HPC, DH)
        for oc in range(HPC):
            for dl in range(4):
                cw[:, pi * 32 + oc * 4 + dl] = w[dl, oc]
    d["cwres"] = cw
    d["wfa"] = np.asarray(inputs["L0_attn_Wfa"], f32)
    d["wfb"] = np.ascontiguousarray(np.asarray(inputs["L0_attn_Wfb"], f32)[:, hs])
    d["dtb"] = np.ascontiguousarray(np.asarray(inputs["L0_attn_dt_bias"], f32).reshape(H, DH)[hsl].T)
    d["alog"] = np.ascontiguousarray(np.asarray(inputs["L0_attn_A_log"], f32)[hsl, None])
    d["wb"] = np.ascontiguousarray(np.asarray(inputs["L0_attn_Wb"], f32)[:, hsl])
    d["wga"] = np.asarray(inputs["L0_attn_Wga"], f32)
    d["wgb"] = np.ascontiguousarray(np.asarray(inputs["L0_attn_Wgb"], f32)[:, hs])
    d["nw"] = np.ascontiguousarray(np.asarray(inputs["L0_attn_norm_w"], f32)[:, None])
    d["wo"] = np.asarray(inputs["L0_attn_Wo"], f32)
    d["fac"] = np.asarray(inputs["L0_ffn_factors"], f32)
    d["fbias"] = np.ascontiguousarray(np.asarray(inputs["L0_ffn_bias"], f32)[None, :])
    d["ident"] = np.eye(DH, dtype=f32)
    d["ident16"] = np.eye(DH, dtype=bfnp)
    d["negident16"] = (-np.eye(DH)).astype(bfnp)
    d["onescol16"] = np.ones((DH, 1), bfnp)
    d["onesrowf"] = np.ones((1, DH), f32)
    d["onesrow16"] = np.ones((1, SUB), bfnp)
    d["onesblk16"] = np.ones((DH, DH), bfnp)
    d["maskSL"] = np.tile(np.tril(np.ones((CC, CC), np.uint8), -1), (1, HPC))
    d["maskSU"] = np.tile(np.triu(np.ones((CC, CC), np.uint8), 1), (1, HPC))
    return d


def kernel(**inputs):
    nc, K = build_nc("full", debug=False)
    in_maps = [_host_prep(inputs, c) for c in range(8)]
    res = run_bass_kernel_spmd(nc, in_maps, core_ids=list(range(8)))
    full = np.zeros((B, T, DM), np.float32)
    for c in range(8):
        b_, hh = c // 2, c % 2
        full[b_, hh * OWN:(hh + 1) * OWN, :] = res.results[c]["out"]
    return full


# version 18 (final)
# speedup vs baseline: 1.0181x; 1.0124x over previous
import contextlib
import numpy as np
import ml_dtypes
import concourse.bass as bass
import concourse.mybir as mybir
from concourse.bass_utils import run_bass_kernel_spmd

F32 = mybir.dt.float32
BF16 = mybir.dt.bfloat16
U8 = mybir.dt.uint8
AF = mybir.ActivationFunctionType
ALU = mybir.AluOpType
bfnp = ml_dtypes.bfloat16

B, T, DM = 4, 4096, 2048
H, DH = 16, 128
KD = H * DH
HPC = 8
HCH = HPC * DH
SLAB = 512
NSLAB = T // SLAB
CC = 128
NCOL = SLAB // CC
F0 = 32
CLAMP = 43.0
OWN = T // 2
NCH = DM // DH
NKCH = KD // DH
SUB = 512
NSUB = OWN // SUB
WD = HPC * CC


class KB:
    def __init__(self):
        self.nc = bass.Bass("TRN2", target_bir_lowering=False)
        self.q = {e: [] for e in ("sync", "gpsimd", "vector", "scalar", "tensor")}
        self.cnt = {}
        self.dr = {}
        self.sb = {}
        self.ps = {}
        self.semh = {}
        self.stack = contextlib.ExitStack()

    def din(self, name, shape, dt=F32):
        self.dr[name] = self.nc.dram_tensor(name, list(shape), dt, kind="ExternalInput").ap()
        return self.dr[name]

    def dout(self, name, shape, dt=F32):
        self.dr[name] = self.nc.dram_tensor(name, list(shape), dt, kind="ExternalOutput").ap()
        return self.dr[name]

    def dint(self, name, shape, dt):
        self.dr[name] = self.nc.dram_tensor(name, list(shape), dt).ap()
        return self.dr[name]

    def sbuf(self, name, shape, dt, stack=None):
        h = (stack or self.stack).enter_context(self.nc.sbuf_tensor(name, list(shape), dt))
        self.sb[name] = h
        return h

    def psum_t(self, name, shape, dt):
        h = self.stack.enter_context(self.nc.psum_tensor(name, list(shape), dt))
        self.ps[name] = h
        return h

    def sem(self, name):
        self.semh[name] = self.stack.enter_context(self.nc.semaphore(name))
        self.cnt[name] = 0
        return self.semh[name]

    def op(self, eng, fn):
        self.q[eng].append(fn)

    def inc(self, eng, fn, sem, by):
        self.cnt[sem] += by
        def wrapped(e, fn=fn, sem=sem, by=by):
            fn(e).then_inc(self.semh[sem], by)
        self.q[eng].append(wrapped)
        return self.cnt[sem]

    def wait(self, eng, sem, val=None):
        v = self.cnt[sem] if val is None else val
        if v <= 0:
            return
        def w(e, sem=sem, v=v):
            e.wait_ge(self.semh[sem], v)
        self.q[eng].append(w)

    def dma(self, eng, out, in_, sem, accum=None):
        if accum is None:
            return self.inc(eng, lambda e, o=out, i=in_: e.dma_start(out=o, in_=i), sem, 16)
        return self.inc(eng, lambda e, o=out, i=in_, a=accum: e.dma_start(out=o, in_=i, accum_op=a), sem, 16)

    def mm(self, out, lhsT, rhs, start=True, stop=True, tp=None):
        def f(e, out=out, lhsT=lhsT, rhs=rhs, start=start, stop=stop, tp=tp):
            kw = {}
            if tp is not None:
                kw["tile_position"] = tp
            return e.matmul(out, lhsT, rhs, start=start, stop=stop,
                            skip_group_check=True, **kw)
        return self.inc("tensor", f, "sMM", 1)

    def tr(self, out, in_, ident):
        return self.inc("tensor", lambda e, o=out, i=in_, d=ident: e.transpose(o, i, d), "sMM", 1)

    def v(self, fn):
        self.wait("vector", "sV")
        return self.inc("vector", fn, "sV", 1)

    def s(self, fn):
        self.wait("scalar", "sS")
        return self.inc("scalar", fn, "sS", 1)

    def p(self, fn):
        self.wait("gpsimd", "sP")
        return self.inc("gpsimd", fn, "sP", 1)

    def snap(self):
        return {k: self.cnt[k] for k in ("sMM", "sV", "sS", "sP", "sOG")}

    def run(self):
        block = self.stack.enter_context(self.nc.Block())
        for ename in ("sync", "gpsimd", "vector", "scalar", "tensor"):
            ops = self.q[ename]
            dec = getattr(block, ename)
            def make(ops):
                def body(engine):
                    for fn in ops:
                        fn(engine)
                return body
            dec(make(ops))


def build_nc(stop_after="full", debug=False):
    K = KB()
    nc = K.nc

    xT = K.din("xT", [DM, T])
    wq = K.din("wq", [DM, HCH]); wk = K.din("wk", [DM, HCH]); wv = K.din("wv", [DM, HCH])
    cwres = K.din("cwres", [DH, 3 * 4 * HPC])
    wfa = K.din("wfa", [DM, DH]); wfb = K.din("wfb", [DH, HCH])
    dtb = K.din("dtb", [DH, HPC])
    alog = K.din("alog", [HPC, 1])
    wb = K.din("wb", [DM, HPC])
    wga = K.din("wga", [DM, DH]); wgb = K.din("wgb", [DH, HCH])
    nw = K.din("nw", [DH, 1])
    wo = K.din("wo", [KD, DM])
    fac = K.din("fac", [8, DM, DM])
    fbias = K.din("fbias", [1, DM])
    ident = K.din("ident", [DH, DH])
    ident16 = K.din("ident16", [DH, DH], BF16)
    negident16 = K.din("negident16", [DH, DH], BF16)
    onescol16 = K.din("onescol16", [DH, 1], BF16)
    onesrowf = K.din("onesrowf", [1, DH])
    onesrow16 = K.din("onesrow16", [1, SUB], BF16)
    onesblk16 = K.din("onesblk16", [DH, DH], BF16)
    maskSL = K.din("maskSL", [CC, HPC * CC], U8)
    maskSU = K.din("maskSU", [CC, HPC * CC], U8)

    out = K.dout("out", [OWN, DM])
    if debug:
        for nm in ("d_qf", "d_kf", "d_v", "d_g", "d_gout", "d_og"):
            K.dout(nm, [DH, HPC * T])
        K.dout("d_beta", [HPC, T])

    wq16t = K.dint("wq16t", [HPC, DH, NCH, DH], BF16)
    wk16t = K.dint("wk16t", [HPC, DH, NCH, DH], BF16)
    wv16t = K.dint("wv16t", [HPC, DH, NCH, DH], BF16)
    wo16t = K.dint("wo16t", [NCH, DH, NKCH, DH], BF16)
    wsumf = K.dint("wsumf", [DM, DM], F32)
    ib = [K.dint(f"ib{s}", [HCH, SLAB], BF16) for s in range(NSLAB)]
    obs = [K.dint(f"ob{s}", [KD, SLAB], BF16) for s in range(NSLAB)]
    obAll = K.dint("obAll", [KD, T], BF16)

    for s_ in ("sW", "sX", "sMM", "sV", "sS", "sD", "sO", "sG",
               "sXa", "sXb", "sWa", "sWb", "sCa", "sCb", "sOG",
               "sFa", "sFb", "sFc", "sFd", "sP"):
        K.sem(s_)
    ccsem = [K.sem(f"sCC{s}") for s in range(NSLAB)]

    psal = K.psum_t("psal", [DH, WD], F32)
    psbe = K.psum_t("psbe", [DH, SLAB], F32)
    psga = K.psum_t("psga", [DH, WD], F32)
    psgb = K.psum_t("psgb", [DH, WD], F32)
    pst16 = K.psum_t("pst16", [DH, WD], BF16)
    psP = psga[:, 0:SLAB]
    psQ = psga[:, SLAB:WD]

    for nm, shape, dt in (
        ("c_ident", [DH, DH], F32), ("c_ident16", [DH, DH], BF16),
        ("c_negident16", [DH, DH], BF16), ("c_onescol16", [DH, 1], BF16),
        ("c_onesrowf", [1, DH], F32), ("c_onesrow16", [1, SUB], BF16),
        ("c_onesblk16", [DH, DH], BF16),
        ("c_maskSL", [CC, HPC * CC], U8), ("c_maskSU", [CC, HPC * CC], U8),
        ("c_cwres", [DH, 3 * 4 * HPC], F32),
        ("c_zero", [DH, 1], F32), ("c_epsv", [DH, 1], F32), ("c_eps128", [DH, 1], F32),
        ("c_epslnv", [DH, 1], F32), ("c_epsln1", [1, 1], F32), ("c_m20", [DH, 1], F32),
        ("wfa16", [DH, NCH * DH], BF16), ("wfb16", [DH, HCH], BF16),
        ("wga16", [DH, NCH * DH], BF16), ("wgb16", [DH, HCH], BF16),
        ("wb16", [DH, NCH * HPC], BF16),
        ("dtbc", [DH, HPC], F32), ("negA", [DH, HPC], F32), ("nwcol", [DH, 1], F32),
        ("fb8", [1, DM], BF16),
        ("S", [DH, HPC * DH], F32), ("S2", [DH, HPC * DH], F32),
        ("S16", [DH, HPC * DH], BF16),
    ):
        K.sbuf(nm, shape, dt)

    slab_stack = contextlib.ExitStack()
    for nm, shape, dt in (
        ("beta", [HPC, SLAB], F32),
        ("betaTs", [CC, NCOL * 16], F32),
        ("xb", [DH, NCH * SLAB], BF16),
        ("hsc0", [DH, SLAB + 3], F32), ("hsc1", [DH, SLAB + 3], F32),
        ("stash", [DH, 3 * HPC * 3], F32),
        ("qf", [DH, HPC * SLAB], BF16), ("kf", [DH, HPC * SLAB], BF16),
        ("vv", [DH, HPC * SLAB], BF16), ("gout", [DH, HPC * SLAB], BF16),
        ("gg", [DH, HPC * SLAB], F32),
        ("y1", [DH, SLAB], BF16),
        ("g1", [DH, SLAB], F32), ("g2", [DH, SLAB], F32),
        ("rawq", [DH, SLAB], BF16), ("rstdq", [DH, SLAB], BF16),
        ("pw0", [DH, NCH * DH], BF16), ("pw1", [DH, NCH * DH], BF16),
        ("G", [DH, WD], F32), ("Gloc", [DH, WD], F32), ("Gtmp", [DH, WD], F32),
        ("Eloc16", [DH, WD], BF16), ("Einv", [DH, WD], BF16),
        ("EG", [DH, WD], BF16), ("EK2", [DH, WD], BF16),
        ("kbar", [DH, WD], BF16), ("qbar", [DH, WD], BF16), ("ktil", [DH, WD], BF16),
        ("ktil2", [DH, WD], BF16), ("qk16", [DH, WD], BF16),
        ("rhsS0", [DH, WD], BF16), ("rhsS1", [DH, WD], BF16),
        ("Lt", [DH, WD], BF16), ("LtT", [DH, WD], BF16),
        ("Pa", [DH, WD], BF16), ("Pb", [DH, WD], BF16),
        ("T0", [DH, WD], BF16), ("T1", [DH, WD], BF16),
        ("C3Tp", [DH, WD], BF16), ("C3p", [DH, WD], BF16),
        ("EAll", [DH, HPC * 16], BF16), ("nEAll", [DH, HPC * 16], BF16),
        ("kbarg0", [DH, WD], BF16), ("kbarg1", [DH, WD], BF16),
        ("qbarg0", [DH, WD], BF16), ("qbarg1", [DH, WD], BF16),
        ("k2t0", [DH, WD], BF16), ("k2t1", [DH, WD], BF16),
        ("App0", [DH, WD], BF16), ("App1", [DH, WD], BF16),
        ("C1T0", [DH, WD], BF16), ("C1T1", [DH, WD], BF16),
        ("C2T0", [DH, WD], BF16), ("C2T1", [DH, WD], BF16),
        ("C4T0", [DH, WD], BF16), ("C4T1", [DH, WD], BF16),
        ("EGend0", [DH, HPC], F32), ("EGend1", [DH, HPC], F32),
        ("dcol0", [CC, HPC], F32), ("dcol1", [CC, HPC], F32),
        ("Rt", [DH, WD], BF16), ("Ya", [DH, WD], BF16),
        ("o16", [DH, WD], BF16), ("osq", [DH, WD], BF16), ("rstd16", [DH, WD], BF16),
    ):
        K.sbuf(nm, shape, dt, stack=slab_stack)

    t = K.sb

    for nm, src in (("c_ident", ident), ("c_ident16", ident16),
                    ("c_negident16", negident16), ("c_onescol16", onescol16),
                    ("c_onesrowf", onesrowf), ("c_onesrow16", onesrow16),
                    ("c_onesblk16", onesblk16), ("c_maskSL", maskSL), ("c_maskSU", maskSU),
                    ("c_cwres", cwres)):
        K.dma("sync", t[nm][:, :], src, "sW")
    K.v(lambda e: e.memset(t["c_zero"][:, :], 0.0))
    K.v(lambda e: e.memset(t["c_epsv"][:, :], 1e-6))
    K.v(lambda e: e.memset(t["c_eps128"][:, :], 128e-6))
    K.v(lambda e: e.memset(t["c_epslnv"][:, :], 1e-5))
    K.v(lambda e: e.memset(t["c_epsln1"][:, :], 1e-5))
    K.v(lambda e: e.memset(t["c_m20"][:, :], -20.0))
    K.v(lambda e: e.memset(t["stash"][:, :], 0.0))
    K.v(lambda e: e.memset(psga[:, :], 0.0))
    K.v(lambda e: e.memset(psgb[:, :], 0.0))

    def small16(dst, view, cols):
        K.wait("sync", "sV", K.cnt["sV"])
        K.dma("sync", t["gg"][:, 0:cols], view, "sW")
        K.wait("vector", "sW")
        K.v(lambda e, dst=dst, cols=cols: e.tensor_copy(t[dst][:, 0:cols], t["gg"][:, 0:cols]))
    small16("wfa16", wfa.rearrange("(c p) d -> p c d", p=DH), NCH * DH)
    small16("wga16", wga.rearrange("(c p) d -> p c d", p=DH), NCH * DH)
    small16("wfb16", wfb, HCH)
    small16("wgb16", wgb, HCH)
    small16("wb16", wb.rearrange("(c p) d -> p c d", p=DH), NCH * HPC)
    K.wait("sync", "sW")
    K.dma("sync", t["dtbc"][:, :], dtb, "sW")
    K.wait("sync", "sW")
    K.dma("sync", t["negA"][0:HPC, 0:1], alog, "sW")
    K.wait("sync", "sW")
    K.dma("sync", t["nwcol"][:, :], nw, "sW")
    K.wait("sync", "sW")
    K.wait("sync", "sV", K.cnt["sV"])
    K.dma("sync", t["gg"][0:1, 0:DM], fbias, "sW")
    K.wait("sync", "sW")
    K.wait("scalar", "sW")
    K.s(lambda e: e.mul(t["fb8"][0:1, :], t["gg"][0:1, 0:DM], float(np.sqrt(8.0))))
    K.s(lambda e: e.activation(t["negA"][0:HPC, 1:2], t["negA"][0:HPC, 0:1], AF.Exp))
    K.wait("tensor", "sS")
    K.tr(psga[0:1, 0:HPC], t["negA"][0:HPC, 1:2], t["c_ident"][0:HPC, 0:HPC])
    K.wait("vector", "sMM")
    K.v(lambda e: e.tensor_copy(t["Gloc"][0:1, 0:HPC], psga[0:1, 0:HPC]))
    K.wait("tensor", "sV")
    K.mm(psga[0:DH, 8:16], t["c_onesrowf"][0:1, :], t["Gloc"][0:1, 0:HPC])
    K.wait("vector", "sMM")
    K.v(lambda e: e.tensor_scalar(t["negA"][:, 0:HPC], psga[0:DH, 8:16], -1.0, None, ALU.mult))

    K.wait("sync", "sS", K.cnt["sS"])
    K.wait("sync", "sMM", K.cnt["sMM"])
    K.wait("sync", "sV", K.cnt["sV"])

    def wprep(src, dstt, nr, per_oc):
        ncols = src.shape[1]
        nci = nr // DH
        for ci in range(nci):
            stg = t["gg"][:, 0:ncols] if ci % 2 == 0 else t["gg"][:, 2048:2048 + ncols]
            stb = t["qf"][:, 0:ncols] if ci % 2 == 0 else t["kf"][:, 0:ncols]
            K.wait("sync", "sV", K.cnt["sV"])
            K.dma("sync", stg, src[ci * DH:(ci + 1) * DH, :], "sW")
            K.wait("vector", "sW")
            K.wait("vector", "sD", K.cnt["sD"])
            K.v(lambda e, stb=stb, stg=stg: e.tensor_copy(stb, stg))
            K.wait("gpsimd", "sV")
            K.dma("gpsimd", dstt.rearrange("o p c d -> c p o d", d=per_oc)[ci], stb, "sD")
    wprep(wq, wq16t, DM, DH)
    wprep(wk, wk16t, DM, DH)
    wprep(wv, wv16t, DM, DH)
    fsems = ("sFa", "sFb", "sFc")
    fstage = ("S", "S2", "Gtmp")
    fchains = [(ci, hf) for ci in range(NCH) for hf in range(2)]
    base_mm, base_v = K.cnt["sMM"], K.cnt["sV"]
    K.wait("gpsimd", "sMM", base_mm)
    K.wait("gpsimd", "sV", base_v)
    fout = [0, 0, 0]
    w = 0
    while w * 3 < len(fchains):
        grp = fchains[3 * w:3 * w + 3]
        for r_ in range(8):
            for q in range(len(grp)):
                ci, hf = grp[q]
                stg = t[fstage[q]][:, 0:1024]
                if r_ == 0:
                    if fout[q]:
                        K.wait("gpsimd", "sD", fout[q])
                    K.dma("gpsimd", stg,
                          fac[0, ci * DH:(ci + 1) * DH, hf * 1024:hf * 1024 + 1024],
                          fsems[q])
                else:
                    K.wait("gpsimd", fsems[q])
                    K.dma("gpsimd", stg,
                          fac[r_, ci * DH:(ci + 1) * DH, hf * 1024:hf * 1024 + 1024],
                          fsems[q], accum=ALU.add)
        for q in range(len(grp)):
            ci, hf = grp[q]
            stg = t[fstage[q]][:, 0:1024]
            K.wait("gpsimd", fsems[q])
            K.dma("gpsimd", wsumf[ci * DH:(ci + 1) * DH, hf * 1024:hf * 1024 + 1024],
                  stg, "sD")
            fout[q] = K.cnt["sD"]
        w += 1
    fac_out_dma = K.cnt["sD"]
    wprep(wo, wo16t, KD, DH)

    if stop_after == "prep":
        K.wait("gpsimd", "sD")
        K.run()
        return nc, K

    pwmm = [0, 0]
    evac = {"psP": 0, "psQ": 0}
    evacS = {"psP": 0, "psQ": 0}
    xstage_free = [0, 0]

    def slab_qkv(s):
        t0 = s * SLAB
        if debug:
            K.wait("vector", "sO", K.cnt["sO"])
            K.wait("scalar", "sO", K.cnt["sO"])
        K.wait("vector", "sS", K.cnt["sS"])
        K.wait("vector", "sP", K.cnt["sP"])
        K.wait("vector", "sMM", K.cnt["sMM"])
        K.wait("sync", "sV", K.cnt["sV"])
        K.wait("sync", "sS", K.cnt["sS"])
        K.wait("sync", "sD", K.cnt["sD"])
        K.wait("sync", "sMM", K.cnt["sMM"])
        xbv = 0
        for ci in range(NCH):
            slot = ci % 2
            sap = t["G"][:, 0:SLAB] if slot == 0 else t["G"][:, SLAB:2 * SLAB]
            xsem = "sXa" if slot == 0 else "sXb"
            if xstage_free[slot]:
                K.wait("sync", "sV", xstage_free[slot])
            K.dma("sync", sap, xT[ci * DH:(ci + 1) * DH, t0:t0 + SLAB], xsem)
            K.wait("vector", xsem)
            xbv = K.v(lambda e, ci=ci, sap=sap: e.tensor_copy(
                t["xb"][:, ci * SLAB:(ci + 1) * SLAB], sap))
            xstage_free[slot] = xbv
        projs = ((wq16t, 0, "qf"), (wk16t, 1, "kf"), (wv16t, 2, "vv"))
        for wdt, pi, dstname in projs:
            for oc in range(HPC):
                it = pi * HPC + oc
                slot = it % 2
                pwb = t["pw0"] if slot == 0 else t["pw1"]
                psX, psXn = (psP, "psP") if it % 2 == 0 else (psQ, "psQ")
                wsem = "sWa" if slot == 0 else "sWb"
                if pwmm[slot]:
                    K.wait("sync", "sMM", pwmm[slot])
                K.dma("sync", pwb[:, :], wdt.rearrange("o p c d -> o p (c d)")[oc], wsem)
                K.wait("tensor", wsem)
                K.wait("tensor", "sV", max(xbv, evac[psXn]))
                K.wait("tensor", "sS", evacS[psXn])
                for ci in range(NCH):
                    K.mm(psX[:, 0:SLAB], pwb[:, ci * DH:(ci + 1) * DH],
                         t["xb"][:, ci * SLAB:(ci + 1) * SLAB],
                         start=(ci == 0), stop=(ci == NCH - 1))
                pwmm[slot] = K.cnt["sMM"]
                hs = "hsc0" if it % 2 == 0 else "hsc1"
                K.wait("vector", "sMM")
                hv0 = K.v(lambda e, hs=hs, psX=psX: e.tensor_copy(
                    t[hs][:, 3:3 + SLAB], psX[:, 0:SLAB]))
                evac[psXn] = hv0
                st_off = it * 3
                K.v(lambda e, hs=hs, st_off=st_off: e.tensor_copy(
                    t[hs][:, 0:3], t["stash"][:, st_off:st_off + 3]))
                K.v(lambda e, hs=hs, st_off=st_off: e.tensor_copy(
                    t["stash"][:, st_off:st_off + 3], t[hs][:, SLAB:SLAB + 3]))
                cA = t["Gloc"][:, 0:SLAB]
                cB = t["Gloc"][:, SLAB:2 * SLAB]
                cwc = lambda dl: t["c_cwres"][:, pi * 32 + oc * 4 + dl:pi * 32 + oc * 4 + dl + 1]
                K.v(lambda e, hs=hs, cA=cA, w0=cwc(0): e.tensor_scalar(
                    cA, t[hs][:, 0:SLAB], w0, None, ALU.mult))
                K.v(lambda e, hs=hs, cA=cA, cB=cB, w1=cwc(1): e.scalar_tensor_tensor(
                    cB, t[hs][:, 1:1 + SLAB], w1, cA, ALU.mult, ALU.add))
                K.v(lambda e, hs=hs, cA=cA, cB=cB, w2=cwc(2): e.scalar_tensor_tensor(
                    cA, t[hs][:, 2:2 + SLAB], w2, cB, ALU.mult, ALU.add))
                cvv = K.v(lambda e, hs=hs, cA=cA, cB=cB, w3=cwc(3): e.scalar_tensor_tensor(
                    cB, t[hs][:, 3:3 + SLAB], w3, cA, ALU.mult, ALU.add))
                K.wait("scalar", "sV", cvv)
                sg = K.s(lambda e, cB=cB: e.activation(t["g1"][:, :], cB, AF.Sigmoid))
                K.wait("vector", "sS", sg)
                if dstname == "vv":
                    K.v(lambda e, cB=cB, oc=oc: e.tensor_mul(
                        t["vv"][:, oc * SLAB:(oc + 1) * SLAB], cB, t["g1"][:, :]))
                else:
                    K.v(lambda e, cB=cB: e.tensor_mul(
                        t["rawq"][:, :], cB, t["g1"][:, :]))
                    sqv = K.v(lambda e: e.tensor_mul(t["y1"][:, :], t["rawq"][:, :],
                                                     t["rawq"][:, :]))
                    K.wait("tensor", "sV", sqv)
                    K.mm(psX[:, 0:SLAB], t["c_onesblk16"][:, :], t["y1"][:, :])
                    K.wait("scalar", "sMM")
                    sc = 128.0 if dstname == "qf" else 1.0
                    epscol = "c_eps128" if dstname == "qf" else "c_epsv"
                    K.s(lambda e, psX=psX, sc=sc, epscol=epscol: e.activation(
                        t["g1"][:, :], psX[:, 0:SLAB], AF.Ln, bias=t[epscol][:, 0:1], scale=sc))
                    sd = K.s(lambda e: e.activation(t["rstdq"][:, :], t["g1"][:, :],
                                                    AF.Exp, scale=-0.5))
                    evacS[psXn] = sd
                    K.wait("vector", "sS", sd)
                    K.v(lambda e, dstname=dstname, oc=oc: e.tensor_mul(
                        t[dstname][:, oc * SLAB:(oc + 1) * SLAB], t["rawq"][:, :], t["rstdq"][:, :]))

    def slab_gates(s):
        K.wait("vector", "sP", K.cnt["sP"])
        K.wait("scalar", "sP", K.cnt["sP"])
        K.wait("vector", "sS", K.cnt["sS"])
        K.wait("tensor", "sV", evac["psP"])
        K.wait("tensor", "sS", evacS["psP"])
        for ci in range(NCH):
            K.mm(psP[:, 0:SLAB], t["wfa16"][:, ci * DH:(ci + 1) * DH],
                 t["xb"][:, ci * SLAB:(ci + 1) * SLAB], start=(ci == 0), stop=(ci == NCH - 1))
        K.wait("vector", "sMM")
        yv = K.v(lambda e: e.tensor_copy(t["y1"][:, :], psP[:, 0:SLAB]))
        evac["psP"] = yv
        for oc in range(HPC):
            psX, psXn = (psQ, "psQ") if oc % 2 == 0 else (psP, "psP")
            K.wait("tensor", "sV", max(evac[psXn], yv))
            K.wait("tensor", "sS", evacS[psXn])
            K.mm(psX[:, 0:SLAB], t["wfb16"][:, oc * DH:(oc + 1) * DH], t["y1"][:, :])
            K.wait("vector", "sMM")
            c1 = K.v(lambda e, psX=psX, oc=oc: e.scalar_tensor_tensor(
                t["g1"][:, :], psX[:, 0:SLAB], t["dtbc"][:, oc:oc + 1],
                t["c_m20"][:, 0:1].broadcast_to((DH, SLAB)), ALU.add, ALU.max))
            evac[psXn] = c1
            K.v(lambda e: e.tensor_scalar(t["g2"][:, :], t["g1"][:, :], 20.0, None, ALU.min))
            K.wait("scalar", "sV")
            K.s(lambda e: e.activation(t["g1"][:, :], t["g2"][:, :], AF.Exp))
            K.s(lambda e: e.activation(t["g2"][:, :], t["g1"][:, :], AF.Ln, bias=1.0))
            K.wait("vector", "sS")
            K.v(lambda e, oc=oc: e.tensor_scalar(
                t["gg"][:, oc * SLAB:(oc + 1) * SLAB], t["g2"][:, :],
                t["negA"][:, oc:oc + 1], None, ALU.mult))
    def gates_tail_gen(s):
        K.wait("tensor", "sV", evac["psP"])
        K.wait("tensor", "sS", evacS["psP"])
        for ci in range(NCH):
            K.mm(psP[:, 0:SLAB], t["wga16"][:, ci * DH:(ci + 1) * DH],
                 t["xb"][:, ci * SLAB:(ci + 1) * SLAB], start=(ci == 0), stop=(ci == NCH - 1))
        K.wait("vector", "sMM")
        yv = K.v(lambda e: e.tensor_copy(t["y1"][:, :], psP[:, 0:SLAB]))
        evac["psP"] = yv
        yield
        for oc in range(HPC):
            psX, psXn = (psQ, "psQ") if oc % 2 == 0 else (psP, "psP")
            K.wait("tensor", "sV", max(evac[psXn], yv))
            K.wait("tensor", "sS", evacS[psXn])
            K.mm(psX[:, 0:SLAB], t["wgb16"][:, oc * DH:(oc + 1) * DH], t["y1"][:, :])
            K.wait("scalar", "sMM")
            sg = K.s(lambda e, psX=psX, oc=oc: e.activation(
                t["gout"][:, oc * SLAB:(oc + 1) * SLAB], psX[:, 0:SLAB], AF.Sigmoid))
            evacS[psXn] = sg
            if oc % 2 == 1:
                yield
        K.wait("tensor", "sV", evac["psQ"])
        K.wait("tensor", "sS", evacS["psQ"])
        for ci in range(NCH):
            K.mm(psQ[0:HPC, 0:SLAB], t["wb16"][:, ci * HPC:(ci + 1) * HPC],
                 t["xb"][:, ci * SLAB:(ci + 1) * SLAB], start=(ci == 0), stop=(ci == NCH - 1))
        K.wait("scalar", "sMM")
        sg = K.s(lambda e: e.activation(t["beta"][0:HPC, :], psQ[0:HPC, 0:SLAB], AF.Sigmoid))
        evacS["psQ"] = sg
        yield
        K.wait("vector", "sS", sg)
        K.wait("vector", "sP", K.cnt["sP"])
        K.wait("vector", "sMM", K.cnt["sMM"])
        bv = K.v(lambda e: e.tensor_copy(t["y1"][0:HPC, 0:SLAB], t["beta"][0:HPC, :]))
        K.wait("tensor", "sV", K.cnt["sV"])
        K.wait("tensor", "sS", K.cnt["sS"])
        for c4_ in range(NCOL):
            K.tr(pst16[:, c4_ * 8:(c4_ + 1) * 8],
                 t["y1"][0:HPC, c4_ * CC:(c4_ + 1) * CC], t["c_ident16"][0:HPC, 0:HPC])
        btm = K.cnt["sMM"]
        K.wait("vector", "sMM", btm)
        K.v(lambda e: e.tensor_copy(
            t["betaTs"][:, 0:NCOL * 16].rearrange("p (c g) -> p c g", g=16)[:, :, 0:8],
            pst16[:, 0:NCOL * 8].rearrange("p (c h) -> p c h", h=8)))
        K.v(lambda e: e.tensor_scalar(
            t["betaTs"][:, 0:NCOL * 16].rearrange("p (c g) -> p c g", g=16)[:, :, 8:16],
            pst16[:, 0:NCOL * 8].rearrange("p (c h) -> p c h", h=8),
            -1.0, None, ALU.mult))

    def h_ap(name, h, cc, width=CC):
        return t[name][:, h * SLAB + cc * CC: h * SLAB + cc * CC + width]

    def w_ap(name, h, width=CC):
        return t[name][:, h * CC: h * CC + width]

    def w3(name):
        return t[name][:, 0:WD].rearrange("p (h x) -> p h x", x=CC)

    def h3(name, cc):
        return t[name][:, 0:HPC * SLAB].rearrange(
            "p (h x) -> p h x", x=SLAB)[:, :, cc * CC:(cc + 1) * CC]

    def sq_mms(dstps, lhs, rhs):
        for h in range(HPC):
            K.mm(dstps[:, h * CC:(h + 1) * CC], w_ap(lhs, h), w_ap(rhs, h))
        return K.cnt["sMM"]

    def A_gen(s, cc, aprev, bpar):
        p = cc % 2
        sx = str(p)
        K.wait("vector", "sS", aprev["sS"])
        for h in range(HPC):
            K.v(lambda e, h=h, cc=cc: e.tensor_tensor_scan(
                w_ap("G", h), h_ap("gg", h, cc), h_ap("gg", h, cc), 0.0, ALU.add, ALU.bypass))
        yield
        def gloc(e):
            g3 = t["G"][:, 0:WD].rearrange("p (h b i) -> p h b i", b=4, i=F0)
            r3 = t["G"][:, 0:WD].rearrange("p (h x) -> p h x", x=CC)[:, :, F0 // 2 - 1::F0]
            return e.tensor_tensor(
                t["Gloc"][:, 0:WD].rearrange("p (h b i) -> p h b i", b=4, i=F0),
                g3, r3.rearrange("p h (b o) -> p h b o", o=1).broadcast_to((DH, HPC, 4, F0)),
                ALU.subtract)
        K.v(gloc)
        c2 = K.v(lambda e: e.tensor_scalar(t["Gtmp"][:, 0:WD], t["Gloc"][:, 0:WD],
                                           CLAMP, None, ALU.min))
        yield
        K.wait("scalar", "sV", c2)
        K.wait("scalar", "sP", aprev["sP"])
        el = K.s(lambda e: e.activation(t["Eloc16"][:, 0:WD], t["Gtmp"][:, 0:WD], AF.Exp))
        yield
        K.wait("vector", "sS", el)
        c4 = K.v(lambda e: e.tensor_scalar(t["Gtmp"][:, 0:WD], t["Gloc"][:, 0:WD],
                                           -1.0, CLAMP, ALU.mult, ALU.min))
        yield
        K.wait("scalar", "sV", c4)
        ei = K.s(lambda e: e.activation(t["Einv"][:, 0:WD], t["Gtmp"][:, 0:WD], AF.Exp))
        K.s(lambda e: e.activation(t["EG"][:, 0:WD], t["G"][:, 0:WD], AF.Exp))
        K.wait("scalar", "sV", bpar["sV"])
        K.s(lambda e, sx=sx: e.activation(
            t["EGend" + sx][:, 0:HPC],
            t["G"][:, 0:WD].rearrange("p (h x) -> p h x", x=CC)[:, :, CC - 1], AF.Exp))
        yield
        K.wait("vector", "sS", ei)
        def k2arg(e):
            gend = t["G"][:, 0:WD].rearrange("p (h x) -> p h x", x=CC)[:, :, CC - 1]
            return e.tensor_tensor(
                t["Gtmp"][:, 0:WD].rearrange("p (h x) -> p h x", x=CC),
                gend.rearrange("p (h o) -> p h o", o=1).broadcast_to((DH, HPC, CC)),
                t["G"][:, 0:WD].rearrange("p (h x) -> p h x", x=CC), ALU.subtract)
        c6 = K.v(k2arg)
        K.wait("scalar", "sV", c6)
        ek = K.s(lambda e: e.activation(t["EK2"][:, 0:WD], t["Gtmp"][:, 0:WD], AF.Exp))
        yield
        K.wait("gpsimd", "sS", ek)
        K.wait("gpsimd", "sV", aprev["sV"])
        K.wait("gpsimd", "sMM", max(aprev["sMM"], bpar["sMM"]))
        qf3 = h3("qf", cc)
        kf3 = h3("kf", cc)
        for dst, a3, b3 in (("kbar", kf3, "Eloc16"), ("qbar", qf3, "Eloc16"),
                            ("ktil", kf3, "Einv"), ("kbarg" + sx, kf3, "EG"),
                            ("qbarg" + sx, qf3, "EG"), ("ktil2", kf3, "EK2")):
            K.p(lambda e, dst=dst, a3=a3, b3=b3: e.tensor_tensor(
                w3(dst), a3, w3(b3), ALU.mult))
        sc = K.p(lambda e, qf3=qf3, kf3=kf3: e.tensor_tensor(
            w3("qk16"), qf3, kf3, ALU.mult))
        yield
        K.wait("vector", "sS", ek)
        for h in range(HPC):
            def ea(e, h=h):
                r = t["G"][:, 0:WD].rearrange("p (h x) -> p h x", x=CC)[:, h, F0 // 2 - 1::F0]
                return e.tensor_tensor(
                    t["Gtmp"][:, h * 16:(h + 1) * 16].rearrange("p (i j) -> p i j", j=4),
                    r.rearrange("p (i o) -> p i o", o=1).broadcast_to((DH, 4, 4)),
                    r.rearrange("p (o j) -> p o j", o=1).broadcast_to((DH, 4, 4)),
                    ALU.subtract)
            K.v(ea)
        c8 = K.v(lambda e: e.tensor_scalar(t["Gloc"][:, 0:HPC * 16], t["Gtmp"][:, 0:HPC * 16],
                                           0.0, None, ALU.min))
        K.wait("scalar", "sV", c8)
        ea2 = K.s(lambda e: e.activation(t["EAll"][:, 0:HPC * 16], t["Gloc"][:, 0:HPC * 16],
                                         AF.Exp))
        K.wait("vector", "sS", ea2)
        K.v(lambda e: e.tensor_scalar(t["nEAll"][:, 0:HPC * 16], t["EAll"][:, 0:HPC * 16],
                                      -1.0, None, ALU.mult))
        yield
        wmm, amm = aprev["sMM"], aprev["sMM"]
        K.wait("tensor", "sP", sc)
        for I in range(4):
            if wmm:
                K.wait("vector", "sMM", wmm)
            def rbW(e, I=I):
                nk = t["ktil"][:, 0:WD].rearrange("p (h s) -> p h s", s=CC)[:, :, 0:(I + 1) * F0]
                scl = t["nEAll"][:, 0:HPC * 16].rearrange(
                    "p (h i j) -> p h i j", i=4, j=4)[:, :, I, 0:I + 1]
                return e.tensor_tensor(
                    t["rhsS0"][:, 0:WD].rearrange("p (h s) -> p h s", s=CC)[:, :, 0:(I + 1) * F0]
                        .rearrange("p h (j f) -> p h j f", f=F0),
                    nk.rearrange("p h (j f) -> p h j f", f=F0),
                    scl.rearrange("p h (j o) -> p h j o", o=1).broadcast_to((DH, HPC, I + 1, F0)),
                    ALU.mult)
            rb = K.v(rbW)
            tp = (0, 96) if I == 3 else None
            K.wait("tensor", "sV", rb)
            for h in range(HPC):
                K.mm(psga[I * F0:(I + 1) * F0, h * CC:h * CC + (I + 1) * F0],
                     t["kbar"][:, h * CC + I * F0:h * CC + (I + 1) * F0],
                     t["rhsS0"][:, h * CC:h * CC + (I + 1) * F0], tp=tp)
            wmm = K.cnt["sMM"]
            if amm:
                K.wait("vector", "sMM", amm)
            def rbA(e, I=I):
                qb = t["qbar"][:, 0:WD].rearrange("p (h s) -> p h s", s=CC)[:, :, I * F0:CC]
                scl = t["nEAll"][:, 0:HPC * 16].rearrange(
                    "p (h i j) -> p h i j", i=4, j=4)[:, :, I:4, I]
                return e.tensor_tensor(
                    t["rhsS1"][:, 0:WD].rearrange("p (h s) -> p h s", s=CC)[:, :, I * F0:CC]
                        .rearrange("p h (j f) -> p h j f", f=F0),
                    qb.rearrange("p h (j f) -> p h j f", f=F0),
                    scl.rearrange("p h (j o) -> p h j o", o=1).broadcast_to((DH, HPC, 4 - I, F0)),
                    ALU.mult)
            rb2 = K.v(rbA)
            K.wait("tensor", "sV", rb2)
            for h in range(HPC):
                K.mm(psgb[I * F0:(I + 1) * F0, h * CC + I * F0:h * CC + CC],
                     t["ktil"][:, h * CC + I * F0:h * CC + (I + 1) * F0],
                     t["rhsS1"][:, h * CC + I * F0:h * CC + CC], tp=tp)
            amm = K.cnt["sMM"]
            yield
        K.wait("vector", "sMM", amm)
        lt = K.v(lambda e: e.select(
            t["Lt"][:, 0:WD], t["c_maskSL"][:, :], psga[:, 0:WD],
            t["c_zero"][:, 0:1].broadcast_to((CC, WD)), add_drain=True))
        K.wait("vector", "sMM", bpar["sMM"])
        ap_ = K.v(lambda e, sx=sx: e.select(
            t["App" + sx][:, 0:WD], t["c_maskSU"][:, :], psgb[:, 0:WD],
            t["c_zero"][:, 0:1].broadcast_to((CC, WD)), add_drain=True))
        ltm = K.v(lambda e, cc=cc: e.tensor_tensor(
            w3("Lt"), w3("Lt"),
            t["betaTs"][:, cc * 16 + 8:cc * 16 + 16].rearrange("p (h o) -> p h o", o=1)
                .broadcast_to((CC, HPC, CC)), ALU.mult))
        yield
        K.wait("tensor", "sV", ltm)
        for h in range(HPC):
            K.tr(pst16[:, h * CC:(h + 1) * CC], w_ap("Lt", h), t["c_ident16"][:, :])
        lttm = K.cnt["sMM"]
        K.wait("vector", "sMM", lttm)
        ltt = K.v(lambda e: e.tensor_copy(t["LtT"][:, 0:WD], pst16[:, 0:WD]))
        yield
        K.wait("tensor", "sV", ltt)
        sq_mms(psga, "Lt", "LtT")
        mmL = sq_mms(psgb, "LtT", "Lt")
        K.wait("vector", "sMM", mmL)
        t0c = K.v(lambda e: e.tensor_copy(t["T0"][:, 0:WD], psga[:, 0:WD]))
        K.wait("scalar", "sMM", mmL)
        pac = K.s(lambda e: e.copy(t["Pa"][:, 0:WD], psgb[:, 0:WD]))
        yield
        K.wait("tensor", "sV", t0c)
        K.wait("tensor", "sS", pac)
        sq_mms(psga, "Pa", "T0")
        mmL = sq_mms(psgb, "T0", "Pa")
        K.wait("vector", "sMM", mmL)
        t1c = K.v(lambda e: e.tensor_copy(t["T1"][:, 0:WD], psga[:, 0:WD]))
        K.wait("scalar", "sMM", mmL)
        pbc = K.s(lambda e: e.copy(t["Pb"][:, 0:WD], psgb[:, 0:WD]))
        yield
        K.wait("gpsimd", "sV", t1c)
        K.wait("gpsimd", "sS", pbc)
        c3t_p = K.p(lambda e: e.tensor_tensor(t["C3Tp"][:, 0:WD], t["T0"][:, 0:WD],
                                              t["T1"][:, 0:WD], ALU.add))
        c3p_p = K.p(lambda e: e.tensor_tensor(t["C3p"][:, 0:WD], t["Pa"][:, 0:WD],
                                              t["Pb"][:, 0:WD], ALU.add))
        K.wait("tensor", "sV", t1c)
        K.wait("tensor", "sS", pbc)
        sq_mms(psga, "Pa", "T1")
        mmL = sq_mms(psgb, "T1", "Pa")
        K.wait("vector", "sMM", mmL)
        K.wait("vector", "sP", c3p_p)
        c3t = K.v(lambda e: e.tensor_tensor(t["C3Tp"][:, 0:WD], psga[:, 0:WD],
                                            t["C3Tp"][:, 0:WD], ALU.add))
        c3pl = K.v(lambda e: e.tensor_tensor(t["C3p"][:, 0:WD], psgb[:, 0:WD],
                                             t["C3p"][:, 0:WD], ALU.add))
        yield
        K.wait("tensor", "sV", c3pl)
        sq_mms(psga, "Pb", "T1")
        mmL = sq_mms(psgb, "T1", "Pb")
        K.wait("vector", "sMM", mmL)
        K.wait("vector", "sP", c3t_p)
        t0c = K.v(lambda e: e.tensor_copy(t["T0"][:, 0:WD], psga[:, 0:WD]))
        K.wait("scalar", "sMM", mmL)
        K.wait("scalar", "sP", c3p_p)
        pac = K.s(lambda e: e.copy(t["Pa"][:, 0:WD], psgb[:, 0:WD]))
        yield
        K.wait("tensor", "sV", t0c)
        K.wait("tensor", "sS", pac)
        sq_mms(psga, "Pa", "T0")
        mmL = sq_mms(psgb, "T0", "Pa")
        K.wait("vector", "sMM", mmL)
        K.wait("vector", "sP", c3t_p)
        t1c = K.v(lambda e: e.tensor_copy(t["T1"][:, 0:WD], psga[:, 0:WD]))
        K.wait("scalar", "sMM", mmL)
        K.wait("scalar", "sP", c3p_p)
        pbc = K.s(lambda e: e.copy(t["Pb"][:, 0:WD], psgb[:, 0:WD]))
        yield
        K.wait("gpsimd", "sV", t1c)
        K.wait("gpsimd", "sMM", bpar["sMM"])
        c2t_p = K.p(lambda e, sx=sx: e.tensor_tensor(
            t["C2T" + sx][:, 0:WD], t["T0"][:, 0:WD], t["T1"][:, 0:WD], ALU.add))
        K.wait("tensor", "sV", t1c)
        mmL = sq_mms(psga, "Pa", "T1")
        K.wait("vector", "sMM", mmL)
        K.wait("vector", "sP", c2t_p)
        c2t = K.v(lambda e, sx=sx: e.tensor_tensor(
            t["C2T" + sx][:, 0:WD], psga[:, 0:WD], t["C2T" + sx][:, 0:WD], ALU.add))
        yield
        K.wait("tensor", "sV", c2t)
        K.wait("tensor", "sS", pbc)
        sq_mms(psga, "Pb", "T1")
        mmL = sq_mms(psgb, "T1", "Pb")
        K.wait("vector", "sMM", mmL)
        K.wait("vector", "sP", c2t_p)
        t0c = K.v(lambda e: e.tensor_copy(t["T0"][:, 0:WD], psga[:, 0:WD]))
        K.wait("scalar", "sMM", mmL)
        pac = K.s(lambda e: e.copy(t["Pa"][:, 0:WD], psgb[:, 0:WD]))
        yield
        K.wait("tensor", "sV", t0c)
        K.wait("tensor", "sS", pac)
        mmL = sq_mms(psga, "Pa", "T0")
        K.wait("vector", "sMM", mmL)
        K.wait("vector", "sP", c2t_p)
        t1c = K.v(lambda e: e.tensor_copy(t["T1"][:, 0:WD], psga[:, 0:WD]))
        yield
        K.wait("gpsimd", "sV", t1c)
        K.wait("gpsimd", "sMM", bpar["sMM"])
        c1t_p = K.p(lambda e, sx=sx: e.tensor_tensor(
            t["C1T" + sx][:, 0:WD], t["T0"][:, 0:WD], t["T1"][:, 0:WD], ALU.add))
        K.wait("tensor", "sV", t1c)
        mmL = sq_mms(psga, "Pa", "T1")
        K.wait("vector", "sMM", mmL)
        K.wait("vector", "sP", c1t_p)
        K.v(lambda e, sx=sx: e.tensor_tensor(
            t["C1T" + sx][:, 0:WD], psga[:, 0:WD], t["C1T" + sx][:, 0:WD], ALU.add))
        yield
        K.wait("tensor", "sV", c3pl)
        K.wait("tensor", "sS", pac)
        mmL = sq_mms(psgb, "C3p", "LtT")
        K.wait("gpsimd", "sV", c3t)
        K.wait("gpsimd", "sMM", bpar["sMM"])
        c4t_p = K.p(lambda e, sx=sx: e.tensor_tensor(
            t["C4T" + sx][:, 0:WD], t["C3Tp"][:, 0:WD], t["LtT"][:, 0:WD], ALU.subtract))
        K.wait("vector", "sMM", mmL)
        K.wait("vector", "sP", c4t_p)
        c4t = K.v(lambda e, sx=sx: e.tensor_tensor(
            t["C4T" + sx][:, 0:WD], t["C4T" + sx][:, 0:WD], psgb[:, 0:WD], ALU.subtract))
        yield
        K.wait("tensor", "sV", c4t)
        K.wait("tensor", "sP", sc)
        for h in range(HPC):
            K.tr(pst16[:, h * CC:(h + 1) * CC], w_ap("ktil2", h), t["c_ident16"][:, :])
        k2m = K.cnt["sMM"]
        for h in range(HPC):
            K.mm(psgb[:, h:h + 1], w_ap("qk16", h), t["c_onescol16"][:, 0:1])
        dcm = K.cnt["sMM"]
        K.wait("vector", "sMM", k2m)
        K.wait("vector", "sMM", bpar["sMM"])
        K.v(lambda e, sx=sx: e.tensor_copy(t["k2t" + sx][:, 0:WD], pst16[:, 0:WD]))
        K.wait("vector", "sMM", dcm)
        K.wait("vector", "sP", bpar["sP"])
        K.v(lambda e, sx=sx: e.tensor_copy(t["dcol" + sx][:, 0:HPC], psgb[:, 0:HPC]))
        yield

    def B_gen(s, cc, bprev, colg):
        p = cc % 2
        sx = str(p)
        Scur = "S" if colg % 2 == 0 else "S2"
        Snxt = "S2" if colg % 2 == 0 else "S"
        K.wait("tensor", "sV", bprev["sV"])
        K.wait("tensor", "sP", bprev["sP"])
        K.wait("tensor", "sS", bprev["sS"])
        for h in range(HPC):
            K.mm(psal[:, h * CC:(h + 1) * CC], w_ap("kbarg" + sx, h),
                 t["S16"][:, h * DH:(h + 1) * DH], start=True, stop=False)
            K.mm(psal[:, h * CC:(h + 1) * CC], h_ap("vv", h, cc), t["c_negident16"][:, :],
                 start=False, stop=True)
        rmm = K.cnt["sMM"]
        yield
        K.wait("vector", "sMM", rmm)
        K.wait("vector", "sS", bprev["sS"])
        K.wait("vector", "sP", bprev["sP"])
        rt = K.v(lambda e, cc=cc: e.tensor_tensor(
            w3("Rt"), psal[:, 0:WD].rearrange("p (h x) -> p h x", x=CC),
            t["betaTs"][:, cc * 16:cc * 16 + 8].rearrange("p (h o) -> p h o", o=1)
                .broadcast_to((CC, HPC, CC)), ALU.mult))
        yield
        cur, ev, m_ = "Rt", rt, 0
        for fac_, nxt in (("C1T", "Ya"), ("C2T", "Rt"), ("C4T", "Ya")):
            K.wait("tensor", "sV", ev)
            for h in range(HPC):
                K.mm(psal[:, h * CC:(h + 1) * CC], w_ap(fac_ + sx, h), w_ap(cur, h))
            m_ = K.cnt["sMM"]
            K.wait("vector", "sMM", m_)
            ev = K.v(lambda e, nxt=nxt, cur=cur: e.tensor_tensor(
                t[nxt][:, 0:WD], psal[:, 0:WD], t[cur][:, 0:WD], ALU.add))
            cur = nxt
            yield
        ya2, lastapply = ev, m_
        K.wait("tensor", "sV", ya2)
        for h in range(HPC):
            K.mm(psal[:, h * DH:(h + 1) * DH], w_ap("k2t" + sx, h), w_ap("Ya", h))
        m4 = K.cnt["sMM"]
        K.wait("vector", "sS", bprev["sS"])
        K.v(lambda e: e.tensor_tensor(
            t[Snxt][:, 0:WD].rearrange("p (h x) -> p h x", x=DH),
            t[Scur][:, 0:WD].rearrange("p (h x) -> p h x", x=DH),
            t["EGend" + sx][:, 0:HPC].rearrange("p (h o) -> p h o", o=1)
                .broadcast_to((DH, HPC, DH)), ALU.mult))
        K.wait("vector", "sMM", m4)
        st = K.v(lambda e: e.tensor_tensor(
            t[Snxt][:, 0:WD], t[Snxt][:, 0:WD], psal[:, 0:WD], ALU.subtract))
        yield
        K.wait("vector", "sMM", lastapply)
        rtd = K.v(lambda e: e.tensor_tensor(
            w3("Rt"), w3("Ya"),
            t["dcol" + sx][:, 0:HPC].rearrange("p (h o) -> p h o", o=1)
                .broadcast_to((CC, HPC, CC)), ALU.mult))
        K.wait("tensor", "sV", rtd)
        K.wait("tensor", "sS", bprev["sS"])
        K.wait("tensor", "sV", bprev["sV"])
        o, omm = 0, 0
        for g in range(2):
            if g == 1:
                K.wait("tensor", "sV", o)
            for h in range(4 * g, 4 * g + 4):
                hc = (h - 4 * g) * CC
                K.mm(psbe[:, hc:hc + CC], t["S16"][:, h * DH:(h + 1) * DH],
                     w_ap("qbarg" + sx, h), start=True, stop=False)
                K.mm(psbe[:, hc:hc + CC], w_ap("Ya", h), w_ap("App" + sx, h),
                     start=False, stop=False)
                K.mm(psbe[:, hc:hc + CC], w_ap("Rt", h), t["c_negident16"][:, :],
                     start=False, stop=True)
            omm = K.cnt["sMM"]
            K.wait("vector", "sMM", omm)
            o = K.v(lambda e, g=g: e.tensor_copy(
                t["o16"][:, g * SLAB:(g + 1) * SLAB], psbe[:, 0:SLAB]))
        K.v(lambda e: e.tensor_copy(t["S16"][:, 0:HPC * DH], t[Snxt][:, 0:HPC * DH]))
        yield
        K.wait("gpsimd", "sV", o)
        K.wait("gpsimd", "sOG", bprev["sOG"])
        sq = K.p(lambda e: e.tensor_tensor(t["osq"][:, 0:WD], t["o16"][:, 0:WD],
                                           t["o16"][:, 0:WD], ALU.mult))
        K.wait("tensor", "sV", o)
        K.wait("tensor", "sP", sq)
        K.mm(psbe[:, 0:SLAB], t["c_onesblk16"][:, :], t["osq"][:, 0:SLAB])
        smm = K.cnt["sMM"]
        K.wait("scalar", "sMM", smm)
        K.wait("scalar", "sV", st)
        ln1 = K.s(lambda e, Scur=Scur: e.activation(t[Scur][:, 0:SLAB], psbe[:, 0:SLAB],
                                                    AF.Ln, bias=t["c_epsv"][:, 0:1],
                                                    scale=1.0 / DH))
        K.wait("tensor", "sS", ln1)
        K.mm(psbe[:, 0:SLAB], t["c_onesblk16"][:, :], t["osq"][:, SLAB:WD])
        smm = K.cnt["sMM"]
        yield
        K.wait("scalar", "sMM", smm)
        K.s(lambda e, Scur=Scur: e.activation(t[Scur][:, SLAB:WD], psbe[:, 0:SLAB],
                                              AF.Ln, bias=t["c_epsv"][:, 0:1],
                                              scale=1.0 / DH))
        rs = K.s(lambda e, Scur=Scur: e.activation(t["rstd16"][:, 0:WD], t[Scur][:, 0:WD],
                                                   AF.Exp, scale=-0.5))
        K.wait("gpsimd", "sS", rs)
        gmp = K.p(lambda e: e.tensor_tensor(t["rstd16"][:, 0:WD], t["rstd16"][:, 0:WD],
                                            t["o16"][:, 0:WD], ALU.mult))
        K.wait("vector", "sP", gmp)
        og = K.v(lambda e, cc=cc: e.scalar_tensor_tensor(
            w3("osq"), w3("rstd16"), t["nwcol"][:, 0:1], h3("gout", cc),
            ALU.mult, ALU.mult))
        K.wait("gpsimd", "sV", og)
        for h in range(HPC):
            K.dma("gpsimd", ib[s][h * DH:(h + 1) * DH, cc * CC:(cc + 1) * CC],
                  w_ap("osq", h), "sOG")
        if debug:
            K.dma("gpsimd",
                  K.dr["d_og"].rearrange("p (h x) -> p h x", x=T)[:, :, s * SLAB + cc * CC:
                                                                 s * SLAB + (cc + 1) * CC],
                  w3("osq"), "sO")
        yield

    bendp = [dict(sMM=0, sV=0, sS=0, sP=0, sOG=0), dict(sMM=0, sV=0, sS=0, sP=0, sOG=0)]
    colg = [0]
    for s in range(NSLAB):
        evac["psP"] = K.cnt["sV"]
        evac["psQ"] = K.cnt["sV"]
        evacS["psP"] = K.cnt["sS"]
        evacS["psQ"] = K.cnt["sS"]
        slab_qkv(s)
        slab_gates(s)
        if s == 0:
            K.wait("vector", "sD", fac_out_dma)
            K.v(lambda e: e.memset(t["S"][:, :], 0.0))
            K.v(lambda e: e.memset(t["S2"][:, :], 0.0))
            K.v(lambda e: e.memset(t["S16"][:, :], 0.0))
        pend = K.snap()
        tgen = gates_tail_gen(s)
        agen0 = A_gen(s, 0, pend, bendp[0])
        while tgen is not None or agen0 is not None:
            if tgen is not None:
                try:
                    next(tgen)
                except StopIteration:
                    tgen = None
            if agen0 is not None:
                try:
                    next(agen0)
                except StopIteration:
                    agen0 = None
        pend = K.snap()
        aprev = pend
        bprev = pend
        for cc in range(1, NCOL + 1):
            bgen = B_gen(s, cc - 1, bprev, colg[0])
            agen = A_gen(s, cc, aprev, bendp[cc % 2]) if cc < NCOL else None
            while bgen is not None or agen is not None:
                if bgen is not None:
                    try:
                        next(bgen)
                    except StopIteration:
                        bgen = None
                if agen is not None:
                    for _ in range(2):
                        try:
                            next(agen)
                        except StopIteration:
                            agen = None
                            break
            pend = K.snap()
            bendp[(cc - 1) % 2] = pend
            bprev = pend
            aprev = pend
            colg[0] += 1
        if debug:
            for nm, buf in (("d_qf", "qf"), ("d_kf", "kf"), ("d_v", "vv"),
                            ("d_gout", "gout")):
                K.dma("gpsimd",
                      K.dr[nm].rearrange("p (h x) -> p h x", x=T)[:, :, s * SLAB:(s + 1) * SLAB],
                      t[buf][:, 0:HPC * SLAB].rearrange("p (h x) -> p h x", x=SLAB), "sO")
            K.dma("gpsimd",
                  K.dr["d_g"].rearrange("p (h x) -> p h x", x=T)[:, :, s * SLAB:(s + 1) * SLAB],
                  t["gg"][:, 0:HPC * SLAB].rearrange("p (h x) -> p h x", x=SLAB), "sO")
            K.dma("gpsimd", K.dr["d_beta"][:, s * SLAB:(s + 1) * SLAB], t["beta"][0:HPC, :], "sO")
        if s > 0:
            K.wait("gpsimd", f"sCC{s - 1}", 1)
            K.dma("gpsimd", obAll[:, (s - 1) * SLAB:s * SLAB], obs[s - 1][:, :], "sD")
        K.wait("gpsimd", "sOG")
        K.inc("gpsimd", lambda e, s=s: e.collective_compute(
            "AllGather", ALU.bypass, replica_groups=[[0, 1], [2, 3], [4, 5], [6, 7]],
            ins=[ib[s][:, :].opt()], outs=[obs[s][:, :].opt()]), f"sCC{s}", 1)
    K.wait("gpsimd", f"sCC{NSLAB - 1}", 1)
    K.dma("gpsimd", obAll[:, (NSLAB - 1) * SLAB:NSLAB * SLAB], obs[NSLAB - 1][:, :], "sD")

    if stop_after == "scan":
        K.wait("gpsimd", "sD")
        K.wait("gpsimd", "sO", K.cnt["sO"])
        K.run()
        return nc, K

    SL_V, SL_MM, SL_D, SL_S = K.cnt["sV"], K.cnt["sMM"], K.cnt["sD"], K.cnt["sS"]
    slab_stack.close()
    post = contextlib.ExitStack()
    po = K.sbuf("po", [DH, NKCH * SUB], BF16, stack=post)
    pwf0 = K.sbuf("pwf0", [DH, NCH * DH], F32, stack=post)
    pwf1 = K.sbuf("pwf1", [DH, NCH * DH], F32, stack=post)
    pz = K.sbuf("pz", [DH, NCH * SUB], F32, stack=post)
    pzb = K.sbuf("pzb", [DH, NCH * SUB], BF16, stack=post)
    px = K.sbuf("px", [DH, NCH * SUB], F32, stack=post)
    pwa = K.sbuf("pwa", [DH, NKCH * DH], BF16, stack=post)
    pwb_ = K.sbuf("pwb", [DH, NKCH * DH], BF16, stack=post)
    prow = K.sbuf("prow", [1, 8 * SUB], F32, stack=post)
    pab = K.sbuf("pab", [DH, 2 * SUB], F32, stack=post)
    pzt = K.sbuf("pzt", [CC, 4 * DM], F32, stack=post)
    pcol = K.sbuf("pcol", [CC, 40], F32, stack=post)

    for eng in ("sync", "gpsimd", "vector", "scalar", "tensor"):
        K.wait(eng, "sV", SL_V)
        K.wait(eng, "sMM", SL_MM)
        K.wait(eng, "sD", SL_D)
        K.wait(eng, "sS", SL_S)

    pwcnt = [0, 0]
    pwcast = [0, 0]
    pevac = {"psP": 0, "psQ": 0}

    _offcache = {}
    def _post_off(e, i):
        key = (id(e), i)
        if key not in _offcache:
            pk = (id(e), "pid")
            if pk not in _offcache:
                _offcache[pk] = e.partition_id()
            _offcache[key] = (_offcache[pk] % 2) * OWN + i * SUB
        return _offcache[key]

    ld_gate = [0, 0]
    for i in range(NSUB):
        K.wait("sync", "sMM", ld_gate[0])
        K.wait("sync", "sV", ld_gate[1])
        def ld_o(e, i=i):
            off = _post_off(e, i)
            return e.dma_start(out=po[:, :],
                               in_=obAll.rearrange("(c p) t -> p c t", p=DH)[:, :, bass.ds(off, SUB)])
        K.inc("sync", ld_o, "sXa", 16)
        def ld_x(e, i=i):
            off = _post_off(e, i)
            return e.dma_start(out=px[:, :],
                               in_=xT.rearrange("(c p) t -> p c t", p=DH)[:, :, bass.ds(off, SUB)])
        K.inc("sync", ld_x, "sXb", 16)
        XDa, XDb = K.cnt["sXa"], K.cnt["sXb"]
        for dc in range(NCH):
            slot = dc % 2
            pwt = pwa if slot == 0 else pwb_
            if pwcnt[slot]:
                K.wait("sync", "sMM", pwcnt[slot])
            wsem = "sWa" if slot == 0 else "sWb"
            K.dma("sync", pwt[:, :], wo16t.rearrange("o p c d -> o p (c d)")[dc], wsem)
            psX, psXn = (psP, "psP") if dc % 2 == 0 else (psQ, "psQ")
            K.wait("tensor", wsem)
            K.wait("tensor", "sXa", XDa)
            K.wait("tensor", "sXb", XDb)
            K.wait("tensor", "sV", pevac[psXn])
            for c in range(NKCH):
                K.mm(psX[:, 0:SUB], pwt[:, c * DH:(c + 1) * DH], po[:, c * SUB:(c + 1) * SUB],
                     start=(c == 0), stop=(c == NKCH - 1))
            pwcnt[slot] = K.cnt["sMM"]
            K.wait("vector", "sMM")
            zv = K.v(lambda e, dc=dc, psX=psX: e.tensor_tensor(
                pz[:, dc * SUB:(dc + 1) * SUB], psX[:, 0:SUB],
                px[:, dc * SUB:(dc + 1) * SUB], ALU.add))
            pevac[psXn] = zv
        ld_gate[0] = K.cnt["sMM"]
        K.v(lambda e: e.tensor_copy(pzb[:, :], pz[:, :]))
        sqv = K.v(lambda e: e.tensor_mul(po[:, 0:NCH * SUB], pzb[:, :], pzb[:, :]))
        K.wait("tensor", "sV", sqv)
        for c in range(NCH):
            K.mm(psP[0:1, 0:SUB], t["c_onescol16"][:, 0:1], pzb[:, c * SUB:(c + 1) * SUB],
                 start=(c == 0), stop=(c == NCH - 1))
        for c in range(NCH):
            K.mm(psQ[0:1, 0:SUB], t["c_onescol16"][:, 0:1], po[:, c * SUB:(c + 1) * SUB],
                 start=(c == 0), stop=(c == NCH - 1))
        K.wait("vector", "sMM")
        K.v(lambda e: e.tensor_scalar(prow[0:1, 0:SUB], psP[0:1, 0:SUB], 1.0 / DM, None, ALU.mult))
        K.v(lambda e: e.tensor_mul(prow[0:1, SUB:2 * SUB], prow[0:1, 0:SUB], prow[0:1, 0:SUB]))
        vv_ = K.v(lambda e: e.scalar_tensor_tensor(
            prow[0:1, 2 * SUB:3 * SUB], psQ[0:1, 0:SUB], 1.0 / DM,
            prow[0:1, SUB:2 * SUB], ALU.mult, ALU.subtract))
        pevac["psP"] = vv_; pevac["psQ"] = vv_
        K.wait("scalar", "sV")
        K.s(lambda e: e.activation(prow[0:1, 3 * SUB:4 * SUB], prow[0:1, 2 * SUB:3 * SUB],
                                   AF.Ln, bias=t["c_epsln1"][0:1, 0:1]))
        av = K.s(lambda e: e.activation(prow[0:1, 4 * SUB:5 * SUB], prow[0:1, 3 * SUB:4 * SUB],
                                        AF.Exp, scale=-0.5))
        K.wait("vector", "sS", av)
        K.v(lambda e: e.tensor_mul(prow[0:1, 5 * SUB:6 * SUB], prow[0:1, 0:SUB],
                                   prow[0:1, 4 * SUB:5 * SUB]))
        bv = K.v(lambda e: e.tensor_scalar(prow[0:1, 6 * SUB:7 * SUB], prow[0:1, 5 * SUB:6 * SUB],
                                           -1.0, None, ALU.mult))
        K.wait("tensor", "sV", bv)
        K.mm(psP[:, 0:SUB], t["c_onesrowf"][0:1, :], prow[0:1, 4 * SUB:5 * SUB])
        K.mm(psQ[:, 0:SUB], t["c_onesrowf"][0:1, :], prow[0:1, 6 * SUB:7 * SUB])
        K.wait("vector", "sMM")
        K.v(lambda e: e.tensor_copy(pab[:, 0:SUB], psP[:, 0:SUB]))
        abv = K.v(lambda e: e.tensor_copy(pab[:, SUB:2 * SUB], psQ[:, 0:SUB]))
        pevac["psP"] = abv; pevac["psQ"] = abv
        for c in range(NCH):
            K.v(lambda e, c=c: e.tensor_mul(px[:, c * SUB:(c + 1) * SUB],
                                            pz[:, c * SUB:(c + 1) * SUB], pab[:, 0:SUB]))
            K.v(lambda e, c=c: e.tensor_tensor(
                px[:, c * SUB:(c + 1) * SUB], px[:, c * SUB:(c + 1) * SUB],
                pab[:, SUB:2 * SUB], ALU.add))
        xbv = K.v(lambda e: e.tensor_copy(pzb[:, :], px[:, :]))
        for dc in range(NCH):
            slot = dc % 2
            pwt = pwa if slot == 0 else pwb_
            pwf = pwf0 if slot == 0 else pwf1
            K.wait("sync", "sV", pwcast[slot])
            wsem = "sWa" if slot == 0 else "sWb"
            K.dma("sync", pwf[:, :],
                  wsumf.rearrange("(c p) (o d) -> o p c d", p=DH, d=DH)[dc], wsem)
            psX, psXn = (psP, "psP") if dc % 2 == 0 else (psQ, "psQ")
            K.wait("vector", wsem)
            K.wait("vector", "sMM", pwcnt[slot])
            pwcast[slot] = K.v(lambda e, pwt=pwt, pwf=pwf: e.tensor_copy(pwt[:, :], pwf[:, :]))
            K.wait("tensor", "sV", pwcast[slot])
            K.wait("tensor", "sV", max(xbv, pevac[psXn]))
            for c in range(NCH):
                K.mm(psX[:, 0:SUB], pwt[:, c * DH:(c + 1) * DH], pzb[:, c * SUB:(c + 1) * SUB],
                     start=(c == 0), stop=False)
            K.mm(psX[:, 0:SUB], t["fb8"][0:1, dc * DH:(dc + 1) * DH],
                 t["c_onesrow16"][0:1, 0:SUB], start=False, stop=True)
            pwcnt[slot] = K.cnt["sMM"]
            K.wait("vector", "sMM")
            z2v = K.v(lambda e, dc=dc, psX=psX: e.scalar_tensor_tensor(
                pz[:, dc * SUB:(dc + 1) * SUB], psX[:, 0:SUB], float(1.0 / np.sqrt(8.0)),
                px[:, dc * SUB:(dc + 1) * SUB], ALU.mult, ALU.add))
            pevac[psXn] = z2v
        ld_gate[1] = K.cnt["sV"]
        K.wait("tensor", "sV")
        for tt in range(4):
            for g in range(2):
                for c8 in range(8):
                    c = g * 8 + c8
                    K.tr(psal[:, c8 * DH:(c8 + 1) * DH],
                         pz[:, c * SUB + tt * DH:c * SUB + (tt + 1) * DH], t["c_ident"][:, :])
                K.wait("vector", "sMM")
                K.v(lambda e, tt=tt, g=g: e.tensor_copy(
                    pzt[:, tt * DM + g * 8 * DH:tt * DM + (g + 1) * 8 * DH], psal[:, 0:8 * DH]))
                K.wait("tensor", "sV")
        z3 = pzt[:, 0:4 * DM].rearrange("p (a d) -> p a d", d=DM)
        K.v(lambda e, z3=z3: e.reduce_sum(
            pcol[:, 0:4].rearrange("p (a o) -> p a o", o=1), z3, mybir.AxisListType.X))
        K.v(lambda e: e.tensor_mul(pz[:, 0:4 * DM], pzt[:, 0:4 * DM], pzt[:, 0:4 * DM]))
        K.v(lambda e: e.reduce_sum(
            pcol[:, 4:8].rearrange("p (a o) -> p a o", o=1),
            pz[:, 0:4 * DM].rearrange("p (a d) -> p a d", d=DM), mybir.AxisListType.X))
        K.v(lambda e: e.tensor_scalar(pcol[:, 8:12], pcol[:, 0:4], 1.0 / DM, None, ALU.mult))
        K.v(lambda e: e.tensor_mul(pcol[:, 12:16], pcol[:, 8:12], pcol[:, 8:12]))
        cv = K.v(lambda e: e.scalar_tensor_tensor(
            pcol[:, 16:20], pcol[:, 4:8], 1.0 / DM, pcol[:, 12:16], ALU.mult, ALU.subtract))
        K.wait("scalar", "sV", cv)
        K.s(lambda e: e.activation(pcol[:, 20:24], pcol[:, 16:20], AF.Ln,
                                   bias=t["c_epslnv"][:, 0:1]))
        sv2 = K.s(lambda e: e.activation(pcol[:, 24:28], pcol[:, 20:24], AF.Exp, scale=-0.5))
        K.wait("vector", "sS", sv2)
        K.v(lambda e: e.tensor_mul(pcol[:, 28:32], pcol[:, 8:12], pcol[:, 24:28]))
        K.v(lambda e: e.tensor_scalar(pcol[:, 32:36], pcol[:, 28:32], -1.0, None, ALU.mult))
        K.v(lambda e, z3=z3: e.tensor_tensor(
            z3, z3,
            pcol[:, 24:28].rearrange("p (a o) -> p a o", o=1).broadcast_to((CC, 4, DM)),
            ALU.mult))
        ov = K.v(lambda e, z3=z3: e.tensor_tensor(
            z3, z3,
            pcol[:, 32:36].rearrange("p (a o) -> p a o", o=1).broadcast_to((CC, 4, DM)),
            ALU.add))
        K.wait("gpsimd", "sV", ov)
        K.dma("gpsimd", out[i * SUB:(i + 1) * SUB, :].rearrange("(a p) d -> p a d", p=CC),
              pzt[:, 0:4 * DM].rearrange("p (a d) -> p a d", d=DM), "sO")

    K.wait("gpsimd", "sO", K.cnt["sO"])
    K.run()
    return nc, K


def _host_prep(inputs, core):
    b, hh = core // 2, core % 2
    hs = slice(HCH * hh, HCH * hh + HCH)
    hsl = slice(HPC * hh, HPC * hh + HPC)
    f32 = np.float32
    x = np.asarray(inputs["L0_attn_x"], f32)
    d = {}
    d["xT"] = np.ascontiguousarray(x[b].T)
    d["wq"] = np.ascontiguousarray(np.asarray(inputs["L0_attn_Wq"], f32)[:, hs])
    d["wk"] = np.ascontiguousarray(np.asarray(inputs["L0_attn_Wk"], f32)[:, hs])
    d["wv"] = np.ascontiguousarray(np.asarray(inputs["L0_attn_Wv"], f32)[:, hs])
    cw = np.zeros((DH, 3 * 4 * HPC), f32)
    for pi, nm in enumerate(("conv_q", "conv_k", "conv_v")):
        w = np.asarray(inputs[f"L0_attn_{nm}"], f32)[:, hs].reshape(4, HPC, DH)
        for oc in range(HPC):
            for dl in range(4):
                cw[:, pi * 32 + oc * 4 + dl] = w[dl, oc]
    d["cwres"] = cw
    d["wfa"] = np.asarray(inputs["L0_attn_Wfa"], f32)
    d["wfb"] = np.ascontiguousarray(np.asarray(inputs["L0_attn_Wfb"], f32)[:, hs])
    d["dtb"] = np.ascontiguousarray(np.asarray(inputs["L0_attn_dt_bias"], f32).reshape(H, DH)[hsl].T)
    d["alog"] = np.ascontiguousarray(np.asarray(inputs["L0_attn_A_log"], f32)[hsl, None])
    d["wb"] = np.ascontiguousarray(np.asarray(inputs["L0_attn_Wb"], f32)[:, hsl])
    d["wga"] = np.asarray(inputs["L0_attn_Wga"], f32)
    d["wgb"] = np.ascontiguousarray(np.asarray(inputs["L0_attn_Wgb"], f32)[:, hs])
    d["nw"] = np.ascontiguousarray(np.asarray(inputs["L0_attn_norm_w"], f32)[:, None])
    d["wo"] = np.asarray(inputs["L0_attn_Wo"], f32)
    d["fac"] = np.asarray(inputs["L0_ffn_factors"], f32)
    d["fbias"] = np.ascontiguousarray(np.asarray(inputs["L0_ffn_bias"], f32)[None, :])
    d["ident"] = np.eye(DH, dtype=f32)
    d["ident16"] = np.eye(DH, dtype=bfnp)
    d["negident16"] = (-np.eye(DH)).astype(bfnp)
    d["onescol16"] = np.ones((DH, 1), bfnp)
    d["onesrowf"] = np.ones((1, DH), f32)
    d["onesrow16"] = np.ones((1, SUB), bfnp)
    d["onesblk16"] = np.ones((DH, DH), bfnp)
    d["maskSL"] = np.tile(np.tril(np.ones((CC, CC), np.uint8), -1), (1, HPC))
    d["maskSU"] = np.tile(np.triu(np.ones((CC, CC), np.uint8), 1), (1, HPC))
    return d


def kernel(**inputs):
    nc, K = build_nc("full", debug=False)
    in_maps = [_host_prep(inputs, c) for c in range(8)]
    res = run_bass_kernel_spmd(nc, in_maps, core_ids=list(range(8)))
    full = np.zeros((B, T, DM), np.float32)
    for c in range(8):
        b_, hh = c // 2, c % 2
        full[b_, hh * OWN:(hh + 1) * OWN, :] = res.results[c]["out"]
    return full


# version 19
# speedup vs baseline: 1.0484x; 1.0298x over previous
import contextlib
import numpy as np
import ml_dtypes
import concourse.bass as bass
import concourse.mybir as mybir
from concourse.bass_utils import run_bass_kernel_spmd

F32 = mybir.dt.float32
BF16 = mybir.dt.bfloat16
U8 = mybir.dt.uint8
AF = mybir.ActivationFunctionType
ALU = mybir.AluOpType
bfnp = ml_dtypes.bfloat16

B, T, DM = 4, 4096, 2048
H, DH = 16, 128
KD = H * DH
HPC = 8
HCH = HPC * DH
SLAB = 512
NSLAB = T // SLAB
CC = 128
NCOL = SLAB // CC
F0 = 32
CLAMP = 43.0
OWN = T // 2
NCH = DM // DH
NKCH = KD // DH
SUB = 512
NSUB = OWN // SUB
WD = HPC * CC


class KB:
    def __init__(self):
        self.nc = bass.Bass("TRN2", target_bir_lowering=False)
        self.q = {e: [] for e in ("sync", "gpsimd", "vector", "scalar", "tensor")}
        self.cnt = {}
        self.dr = {}
        self.sb = {}
        self.ps = {}
        self.semh = {}
        self.stack = contextlib.ExitStack()

    def din(self, name, shape, dt=F32):
        self.dr[name] = self.nc.dram_tensor(name, list(shape), dt, kind="ExternalInput").ap()
        return self.dr[name]

    def dout(self, name, shape, dt=F32):
        self.dr[name] = self.nc.dram_tensor(name, list(shape), dt, kind="ExternalOutput").ap()
        return self.dr[name]

    def dint(self, name, shape, dt):
        self.dr[name] = self.nc.dram_tensor(name, list(shape), dt).ap()
        return self.dr[name]

    def sbuf(self, name, shape, dt, stack=None):
        h = (stack or self.stack).enter_context(self.nc.sbuf_tensor(name, list(shape), dt))
        self.sb[name] = h
        return h

    def psum_t(self, name, shape, dt):
        h = self.stack.enter_context(self.nc.psum_tensor(name, list(shape), dt))
        self.ps[name] = h
        return h

    def sem(self, name):
        self.semh[name] = self.stack.enter_context(self.nc.semaphore(name))
        self.cnt[name] = 0
        return self.semh[name]

    def op(self, eng, fn):
        self.q[eng].append(fn)

    def inc(self, eng, fn, sem, by):
        self.cnt[sem] += by
        def wrapped(e, fn=fn, sem=sem, by=by):
            fn(e).then_inc(self.semh[sem], by)
        self.q[eng].append(wrapped)
        return self.cnt[sem]

    def wait(self, eng, sem, val=None):
        v = self.cnt[sem] if val is None else val
        if v <= 0:
            return
        def w(e, sem=sem, v=v):
            e.wait_ge(self.semh[sem], v)
        self.q[eng].append(w)

    def dma(self, eng, out, in_, sem, accum=None):
        if accum is None:
            return self.inc(eng, lambda e, o=out, i=in_: e.dma_start(out=o, in_=i), sem, 16)
        return self.inc(eng, lambda e, o=out, i=in_, a=accum: e.dma_start(out=o, in_=i, accum_op=a), sem, 16)

    def mm(self, out, lhsT, rhs, start=True, stop=True, tp=None):
        def f(e, out=out, lhsT=lhsT, rhs=rhs, start=start, stop=stop, tp=tp):
            kw = {}
            if tp is not None:
                kw["tile_position"] = tp
            return e.matmul(out, lhsT, rhs, start=start, stop=stop,
                            skip_group_check=True, **kw)
        return self.inc("tensor", f, "sMM", 1)

    def tr(self, out, in_, ident):
        return self.inc("tensor", lambda e, o=out, i=in_, d=ident: e.transpose(o, i, d), "sMM", 1)

    def v(self, fn):
        self.wait("vector", "sV")
        return self.inc("vector", fn, "sV", 1)

    def s(self, fn):
        self.wait("scalar", "sS")
        return self.inc("scalar", fn, "sS", 1)

    def p(self, fn):
        self.wait("gpsimd", "sP")
        return self.inc("gpsimd", fn, "sP", 1)

    def snap(self):
        return {k: self.cnt[k] for k in ("sMM", "sV", "sS", "sP", "sOG")}

    def run(self):
        block = self.stack.enter_context(self.nc.Block())
        for ename in ("sync", "gpsimd", "vector", "scalar", "tensor"):
            ops = self.q[ename]
            dec = getattr(block, ename)
            def make(ops):
                def body(engine):
                    for fn in ops:
                        fn(engine)
                return body
            dec(make(ops))


def build_nc(stop_after="full", debug=False):
    K = KB()
    nc = K.nc

    xT = K.din("xT", [DM, T])
    wq = K.din("wq", [DM, HCH]); wk = K.din("wk", [DM, HCH]); wv = K.din("wv", [DM, HCH])
    cwres = K.din("cwres", [DH, 3 * 4 * HPC])
    wfa = K.din("wfa", [DM, DH]); wfb = K.din("wfb", [DH, HCH])
    dtb = K.din("dtb", [DH, HPC])
    alog = K.din("alog", [HPC, 1])
    wb = K.din("wb", [DM, HPC])
    wga = K.din("wga", [DM, DH]); wgb = K.din("wgb", [DH, HCH])
    nw = K.din("nw", [DH, 1])
    wo = K.din("wo", [KD, DM])
    fac = K.din("fac", [8, DM, DM])
    fbias = K.din("fbias", [1, DM])
    ident = K.din("ident", [DH, DH])
    ident16 = K.din("ident16", [DH, DH], BF16)
    negident16 = K.din("negident16", [DH, DH], BF16)
    onescol16 = K.din("onescol16", [DH, 1], BF16)
    onesrowf = K.din("onesrowf", [1, DH])
    onesrow16 = K.din("onesrow16", [1, SUB], BF16)
    onesblk16 = K.din("onesblk16", [DH, DH], BF16)
    maskSL = K.din("maskSL", [CC, HPC * CC], U8)
    maskSU = K.din("maskSU", [CC, HPC * CC], U8)

    out = K.dout("out", [OWN, DM])
    if debug:
        for nm in ("d_qf", "d_kf", "d_v", "d_g", "d_gout", "d_og"):
            K.dout(nm, [DH, HPC * T])
        K.dout("d_beta", [HPC, T])

    wq16t = K.dint("wq16t", [HPC, DH, NCH, DH], BF16)
    wk16t = K.dint("wk16t", [HPC, DH, NCH, DH], BF16)
    wv16t = K.dint("wv16t", [HPC, DH, NCH, DH], BF16)
    wo16t = K.dint("wo16t", [NCH, DH, NKCH, DH], BF16)
    wsumf = K.dint("wsumf", [DM, DM], F32)
    ib = [K.dint(f"ib{s}", [HCH, SLAB], BF16) for s in range(NSLAB)]
    obs = [K.dint(f"ob{s}", [KD, SLAB], BF16) for s in range(NSLAB)]
    obAll = K.dint("obAll", [KD, T], BF16)

    for s_ in ("sW", "sX", "sMM", "sV", "sS", "sD", "sO", "sG",
               "sXa", "sXb", "sWa", "sWb", "sCa", "sCb", "sOG",
               "sFa", "sFb", "sFc", "sFd", "sP"):
        K.sem(s_)
    ccsem = [K.sem(f"sCC{s}") for s in range(NSLAB)]

    psal = K.psum_t("psal", [DH, WD], F32)
    psbe = K.psum_t("psbe", [DH, SLAB], F32)
    psga = K.psum_t("psga", [DH, WD], F32)
    psgb = K.psum_t("psgb", [DH, WD], F32)
    pst16 = K.psum_t("pst16", [DH, WD], BF16)
    psP = psga[:, 0:SLAB]
    psQ = psga[:, SLAB:WD]

    for nm, shape, dt in (
        ("c_ident", [DH, DH], F32), ("c_ident16", [DH, DH], BF16),
        ("c_negident16", [DH, DH], BF16), ("c_onescol16", [DH, 1], BF16),
        ("c_onesrowf", [1, DH], F32), ("c_onesrow16", [1, SUB], BF16),
        ("c_onesblk16", [DH, DH], BF16),
        ("c_maskSL", [CC, HPC * CC], U8), ("c_maskSU", [CC, HPC * CC], U8),
        ("c_cwres", [DH, 3 * 4 * HPC], F32),
        ("c_zero", [DH, 1], F32), ("c_epsv", [DH, 1], F32), ("c_eps128", [DH, 1], F32),
        ("c_epslnv", [DH, 1], F32), ("c_epsln1", [1, 1], F32), ("c_m20", [DH, 1], F32),
        ("wfa16", [DH, NCH * DH], BF16), ("wfb16", [DH, HCH], BF16),
        ("wga16", [DH, NCH * DH], BF16), ("wgb16", [DH, HCH], BF16),
        ("wb16", [DH, NCH * HPC], BF16),
        ("dtbc", [DH, HPC], F32), ("negA", [DH, HPC], F32), ("nwcol", [DH, 1], F32),
        ("fb8", [1, DM], BF16),
        ("S", [DH, HPC * DH], F32), ("S2", [DH, HPC * DH], F32),
        ("S16", [DH, HPC * DH], BF16),
    ):
        K.sbuf(nm, shape, dt)

    slab_stack = contextlib.ExitStack()
    for nm, shape, dt in (
        ("beta", [HPC, SLAB], F32),
        ("betaTs", [CC, NCOL * 16], F32),
        ("xb", [DH, NCH * SLAB], BF16),
        ("hsc0", [DH, SLAB + 3], F32), ("hsc1", [DH, SLAB + 3], F32),
        ("stash", [DH, 3 * HPC * 3], F32),
        ("qf", [DH, HPC * SLAB], BF16), ("kf", [DH, HPC * SLAB], BF16),
        ("vv", [DH, HPC * SLAB], BF16), ("gout", [DH, HPC * SLAB], BF16),
        ("gg", [DH, HPC * SLAB], F32),
        ("y1", [DH, SLAB], BF16),
        ("g1", [DH, SLAB], F32), ("g2", [DH, SLAB], F32),
        ("rawq", [DH, SLAB], BF16), ("rstdq", [DH, SLAB], BF16),
        ("pw0", [DH, NCH * DH], BF16), ("pw1", [DH, NCH * DH], BF16),
        ("G", [DH, WD], F32), ("Gloc", [DH, WD], F32), ("Gtmp", [DH, WD], F32),
        ("Eloc16", [DH, WD], BF16), ("Einv", [DH, WD], BF16),
        ("EG", [DH, WD], BF16), ("EK2", [DH, WD], BF16),
        ("kbar", [DH, WD], BF16), ("qbar", [DH, WD], BF16), ("ktil", [DH, WD], BF16),
        ("ktil2", [DH, WD], BF16), ("qk16", [DH, WD], BF16),
        ("rhsS0", [DH, WD], BF16), ("rhsS1", [DH, WD], BF16),
        ("Lt", [DH, WD], BF16), ("LtT", [DH, WD], BF16),
        ("Pa", [DH, WD], BF16), ("Pb", [DH, WD], BF16),
        ("T0", [DH, WD], BF16), ("T1", [DH, WD], BF16),
        ("C3Tp", [DH, WD], BF16), ("C3p", [DH, WD], BF16),
        ("EAll", [DH, HPC * 16], BF16), ("nEAll", [DH, HPC * 16], BF16),
        ("kbarg0", [DH, WD], BF16), ("kbarg1", [DH, WD], BF16),
        ("qbarg0", [DH, WD], BF16), ("qbarg1", [DH, WD], BF16),
        ("k2t0", [DH, WD], BF16), ("k2t1", [DH, WD], BF16),
        ("App0", [DH, WD], BF16), ("App1", [DH, WD], BF16),
        ("C1T0", [DH, WD], BF16), ("C1T1", [DH, WD], BF16),
        ("C2T0", [DH, WD], BF16), ("C2T1", [DH, WD], BF16),
        ("C4T0", [DH, WD], BF16), ("C4T1", [DH, WD], BF16),
        ("EGend0", [DH, HPC], F32), ("EGend1", [DH, HPC], F32),
        ("dcol0", [CC, HPC], F32), ("dcol1", [CC, HPC], F32),
        ("Rt", [DH, WD], BF16), ("Ya", [DH, WD], BF16),
        ("o16", [DH, WD], BF16), ("osq", [DH, WD], BF16), ("rstd16", [DH, WD], BF16),
    ):
        K.sbuf(nm, shape, dt, stack=slab_stack)

    t = K.sb

    for nm, src in (("c_ident", ident), ("c_ident16", ident16),
                    ("c_negident16", negident16), ("c_onescol16", onescol16),
                    ("c_onesrowf", onesrowf), ("c_onesrow16", onesrow16),
                    ("c_onesblk16", onesblk16), ("c_maskSL", maskSL), ("c_maskSU", maskSU),
                    ("c_cwres", cwres)):
        K.dma("sync", t[nm][:, :], src, "sW")
    K.v(lambda e: e.memset(t["c_zero"][:, :], 0.0))
    K.v(lambda e: e.memset(t["c_epsv"][:, :], 1e-6))
    K.v(lambda e: e.memset(t["c_eps128"][:, :], 128e-6))
    K.v(lambda e: e.memset(t["c_epslnv"][:, :], 1e-5))
    K.v(lambda e: e.memset(t["c_epsln1"][:, :], 1e-5))
    K.v(lambda e: e.memset(t["c_m20"][:, :], -20.0))
    K.v(lambda e: e.memset(t["stash"][:, :], 0.0))
    K.v(lambda e: e.memset(psga[:, :], 0.0))
    K.v(lambda e: e.memset(psgb[:, :], 0.0))

    def small16(dst, view, cols):
        K.wait("sync", "sV", K.cnt["sV"])
        K.dma("sync", t["gg"][:, 0:cols], view, "sW")
        K.wait("vector", "sW")
        K.v(lambda e, dst=dst, cols=cols: e.tensor_copy(t[dst][:, 0:cols], t["gg"][:, 0:cols]))
    small16("wfa16", wfa.rearrange("(c p) d -> p c d", p=DH), NCH * DH)
    small16("wga16", wga.rearrange("(c p) d -> p c d", p=DH), NCH * DH)
    small16("wfb16", wfb, HCH)
    small16("wgb16", wgb, HCH)
    small16("wb16", wb.rearrange("(c p) d -> p c d", p=DH), NCH * HPC)
    K.wait("sync", "sW")
    K.dma("sync", t["dtbc"][:, :], dtb, "sW")
    K.wait("sync", "sW")
    K.dma("sync", t["negA"][0:HPC, 0:1], alog, "sW")
    K.wait("sync", "sW")
    K.dma("sync", t["nwcol"][:, :], nw, "sW")
    K.wait("sync", "sW")
    K.wait("sync", "sV", K.cnt["sV"])
    K.dma("sync", t["gg"][0:1, 0:DM], fbias, "sW")
    K.wait("sync", "sW")
    K.wait("scalar", "sW")
    K.s(lambda e: e.mul(t["fb8"][0:1, :], t["gg"][0:1, 0:DM], float(np.sqrt(8.0))))
    K.s(lambda e: e.activation(t["negA"][0:HPC, 1:2], t["negA"][0:HPC, 0:1], AF.Exp))
    K.wait("tensor", "sS")
    K.tr(psga[0:1, 0:HPC], t["negA"][0:HPC, 1:2], t["c_ident"][0:HPC, 0:HPC])
    K.wait("vector", "sMM")
    K.v(lambda e: e.tensor_copy(t["Gloc"][0:1, 0:HPC], psga[0:1, 0:HPC]))
    K.wait("tensor", "sV")
    K.mm(psga[0:DH, 8:16], t["c_onesrowf"][0:1, :], t["Gloc"][0:1, 0:HPC])
    K.wait("vector", "sMM")
    K.v(lambda e: e.tensor_scalar(t["negA"][:, 0:HPC], psga[0:DH, 8:16], -1.0, None, ALU.mult))

    K.wait("sync", "sS", K.cnt["sS"])
    K.wait("sync", "sMM", K.cnt["sMM"])
    K.wait("sync", "sV", K.cnt["sV"])

    def wprep(src, dstt, nr, per_oc):
        ncols = src.shape[1]
        nci = nr // DH
        for ci in range(nci):
            stg = t["gg"][:, 0:ncols] if ci % 2 == 0 else t["gg"][:, 2048:2048 + ncols]
            stb = t["qf"][:, 0:ncols] if ci % 2 == 0 else t["kf"][:, 0:ncols]
            K.wait("sync", "sV", K.cnt["sV"])
            K.dma("sync", stg, src[ci * DH:(ci + 1) * DH, :], "sW")
            K.wait("vector", "sW")
            K.wait("vector", "sD", K.cnt["sD"])
            K.v(lambda e, stb=stb, stg=stg: e.tensor_copy(stb, stg))
            K.wait("gpsimd", "sV")
            K.dma("gpsimd", dstt.rearrange("o p c d -> c p o d", d=per_oc)[ci], stb, "sD")
    wprep(wq, wq16t, DM, DH)
    wprep(wk, wk16t, DM, DH)
    wprep(wv, wv16t, DM, DH)
    fsems = ("sFa", "sFb", "sFc")
    fstage = ("S", "S2", "Gtmp")
    fchains = [(ci, hf) for ci in range(NCH) for hf in range(2)]
    base_mm, base_v = K.cnt["sMM"], K.cnt["sV"]
    K.wait("gpsimd", "sMM", base_mm)
    K.wait("gpsimd", "sV", base_v)
    fout = [0, 0, 0]
    w = 0
    while w * 3 < len(fchains):
        grp = fchains[3 * w:3 * w + 3]
        for r_ in range(8):
            for q in range(len(grp)):
                ci, hf = grp[q]
                stg = t[fstage[q]][:, 0:1024]
                if r_ == 0:
                    if fout[q]:
                        K.wait("gpsimd", "sD", fout[q])
                    K.dma("gpsimd", stg,
                          fac[0, ci * DH:(ci + 1) * DH, hf * 1024:hf * 1024 + 1024],
                          fsems[q])
                else:
                    K.wait("gpsimd", fsems[q])
                    K.dma("gpsimd", stg,
                          fac[r_, ci * DH:(ci + 1) * DH, hf * 1024:hf * 1024 + 1024],
                          fsems[q], accum=ALU.add)
        for q in range(len(grp)):
            ci, hf = grp[q]
            stg = t[fstage[q]][:, 0:1024]
            K.wait("gpsimd", fsems[q])
            K.dma("gpsimd", wsumf[ci * DH:(ci + 1) * DH, hf * 1024:hf * 1024 + 1024],
                  stg, "sD")
            fout[q] = K.cnt["sD"]
        w += 1
    fac_out_dma = K.cnt["sD"]
    wprep(wo, wo16t, KD, DH)

    if stop_after == "prep":
        K.wait("gpsimd", "sD")
        K.run()
        return nc, K

    pwmm = [0, 0]
    evac = {"psP": 0, "psQ": 0}
    evacS = {"psP": 0, "psQ": 0}
    xstage_free = [0, 0]

    def slab_qkv(s):
        t0 = s * SLAB
        if debug:
            K.wait("vector", "sO", K.cnt["sO"])
            K.wait("scalar", "sO", K.cnt["sO"])
        K.wait("vector", "sS", K.cnt["sS"])
        K.wait("vector", "sP", K.cnt["sP"])
        K.wait("vector", "sMM", K.cnt["sMM"])
        K.wait("sync", "sV", K.cnt["sV"])
        K.wait("sync", "sS", K.cnt["sS"])
        K.wait("sync", "sD", K.cnt["sD"])
        K.wait("sync", "sMM", K.cnt["sMM"])
        xbv = 0
        for ci in range(NCH):
            slot = ci % 2
            sap = t["G"][:, 0:SLAB] if slot == 0 else t["G"][:, SLAB:2 * SLAB]
            xsem = "sXa" if slot == 0 else "sXb"
            if xstage_free[slot]:
                K.wait("sync", "sV", xstage_free[slot])
            K.dma("sync", sap, xT[ci * DH:(ci + 1) * DH, t0:t0 + SLAB], xsem)
            K.wait("vector", xsem)
            xbv = K.v(lambda e, ci=ci, sap=sap: e.tensor_copy(
                t["xb"][:, ci * SLAB:(ci + 1) * SLAB], sap))
            xstage_free[slot] = xbv
        projs = ((wq16t, 0, "qf"), (wk16t, 1, "kf"), (wv16t, 2, "vv"))
        for wdt, pi, dstname in projs:
            for oc in range(HPC):
                it = pi * HPC + oc
                slot = it % 2
                pwb = t["pw0"] if slot == 0 else t["pw1"]
                psX, psXn = (psP, "psP") if it % 2 == 0 else (psQ, "psQ")
                wsem = "sWa" if slot == 0 else "sWb"
                if pwmm[slot]:
                    K.wait("sync", "sMM", pwmm[slot])
                K.dma("sync", pwb[:, :], wdt.rearrange("o p c d -> o p (c d)")[oc], wsem)
                K.wait("tensor", wsem)
                K.wait("tensor", "sV", max(xbv, evac[psXn]))
                K.wait("tensor", "sS", evacS[psXn])
                for ci in range(NCH):
                    K.mm(psX[:, 0:SLAB], pwb[:, ci * DH:(ci + 1) * DH],
                         t["xb"][:, ci * SLAB:(ci + 1) * SLAB],
                         start=(ci == 0), stop=(ci == NCH - 1))
                pwmm[slot] = K.cnt["sMM"]
                hs = "hsc0" if it % 2 == 0 else "hsc1"
                K.wait("vector", "sMM")
                hv0 = K.v(lambda e, hs=hs, psX=psX: e.tensor_copy(
                    t[hs][:, 3:3 + SLAB], psX[:, 0:SLAB]))
                evac[psXn] = hv0
                st_off = it * 3
                K.v(lambda e, hs=hs, st_off=st_off: e.tensor_copy(
                    t[hs][:, 0:3], t["stash"][:, st_off:st_off + 3]))
                K.v(lambda e, hs=hs, st_off=st_off: e.tensor_copy(
                    t["stash"][:, st_off:st_off + 3], t[hs][:, SLAB:SLAB + 3]))
                cA = t["Gloc"][:, 0:SLAB]
                cB = t["Gloc"][:, SLAB:2 * SLAB]
                cwc = lambda dl: t["c_cwres"][:, pi * 32 + oc * 4 + dl:pi * 32 + oc * 4 + dl + 1]
                K.v(lambda e, hs=hs, cA=cA, w0=cwc(0): e.tensor_scalar(
                    cA, t[hs][:, 0:SLAB], w0, None, ALU.mult))
                K.v(lambda e, hs=hs, cA=cA, cB=cB, w1=cwc(1): e.scalar_tensor_tensor(
                    cB, t[hs][:, 1:1 + SLAB], w1, cA, ALU.mult, ALU.add))
                K.v(lambda e, hs=hs, cA=cA, cB=cB, w2=cwc(2): e.scalar_tensor_tensor(
                    cA, t[hs][:, 2:2 + SLAB], w2, cB, ALU.mult, ALU.add))
                cvv = K.v(lambda e, hs=hs, cA=cA, cB=cB, w3=cwc(3): e.scalar_tensor_tensor(
                    cB, t[hs][:, 3:3 + SLAB], w3, cA, ALU.mult, ALU.add))
                K.wait("scalar", "sV", cvv)
                sg = K.s(lambda e, cB=cB: e.activation(t["g1"][:, :], cB, AF.Sigmoid))
                K.wait("vector", "sS", sg)
                if dstname == "vv":
                    K.v(lambda e, cB=cB, oc=oc: e.tensor_mul(
                        t["vv"][:, oc * SLAB:(oc + 1) * SLAB], cB, t["g1"][:, :]))
                else:
                    K.v(lambda e, cB=cB: e.tensor_mul(
                        t["rawq"][:, :], cB, t["g1"][:, :]))
                    sqv = K.v(lambda e: e.tensor_mul(t["y1"][:, :], t["rawq"][:, :],
                                                     t["rawq"][:, :]))
                    K.wait("tensor", "sV", sqv)
                    K.mm(psX[:, 0:SLAB], t["c_onesblk16"][:, :], t["y1"][:, :])
                    K.wait("scalar", "sMM")
                    sc = 128.0 if dstname == "qf" else 1.0
                    epscol = "c_eps128" if dstname == "qf" else "c_epsv"
                    K.s(lambda e, psX=psX, sc=sc, epscol=epscol: e.activation(
                        t["g1"][:, :], psX[:, 0:SLAB], AF.Ln, bias=t[epscol][:, 0:1], scale=sc))
                    sd = K.s(lambda e: e.activation(t["rstdq"][:, :], t["g1"][:, :],
                                                    AF.Exp, scale=-0.5))
                    evacS[psXn] = sd
                    K.wait("vector", "sS", sd)
                    K.v(lambda e, dstname=dstname, oc=oc: e.tensor_mul(
                        t[dstname][:, oc * SLAB:(oc + 1) * SLAB], t["rawq"][:, :], t["rstdq"][:, :]))

    def slab_gates(s):
        K.wait("vector", "sP", K.cnt["sP"])
        K.wait("scalar", "sP", K.cnt["sP"])
        K.wait("vector", "sS", K.cnt["sS"])
        K.wait("tensor", "sV", evac["psP"])
        K.wait("tensor", "sS", evacS["psP"])
        for ci in range(NCH):
            K.mm(psP[:, 0:SLAB], t["wfa16"][:, ci * DH:(ci + 1) * DH],
                 t["xb"][:, ci * SLAB:(ci + 1) * SLAB], start=(ci == 0), stop=(ci == NCH - 1))
        K.wait("vector", "sMM")
        yv = K.v(lambda e: e.tensor_copy(t["y1"][:, :], psP[:, 0:SLAB]))
        evac["psP"] = yv
        for oc in range(HPC):
            psX, psXn = (psQ, "psQ") if oc % 2 == 0 else (psP, "psP")
            K.wait("tensor", "sV", max(evac[psXn], yv))
            K.wait("tensor", "sS", evacS[psXn])
            K.mm(psX[:, 0:SLAB], t["wfb16"][:, oc * DH:(oc + 1) * DH], t["y1"][:, :])
            K.wait("vector", "sMM")
            c1 = K.v(lambda e, psX=psX, oc=oc: e.scalar_tensor_tensor(
                t["g1"][:, :], psX[:, 0:SLAB], t["dtbc"][:, oc:oc + 1],
                t["c_m20"][:, 0:1].broadcast_to((DH, SLAB)), ALU.add, ALU.max))
            evac[psXn] = c1
            K.v(lambda e: e.tensor_scalar(t["g2"][:, :], t["g1"][:, :], 20.0, None, ALU.min))
            K.wait("scalar", "sV")
            K.s(lambda e: e.activation(t["g1"][:, :], t["g2"][:, :], AF.Exp))
            K.s(lambda e: e.activation(t["g2"][:, :], t["g1"][:, :], AF.Ln, bias=1.0))
            K.wait("vector", "sS")
            K.v(lambda e, oc=oc: e.tensor_scalar(
                t["gg"][:, oc * SLAB:(oc + 1) * SLAB], t["g2"][:, :],
                t["negA"][:, oc:oc + 1], None, ALU.mult))
    def gates_tail_gen(s):
        K.wait("tensor", "sV", evac["psP"])
        K.wait("tensor", "sS", evacS["psP"])
        for ci in range(NCH):
            K.mm(psP[:, 0:SLAB], t["wga16"][:, ci * DH:(ci + 1) * DH],
                 t["xb"][:, ci * SLAB:(ci + 1) * SLAB], start=(ci == 0), stop=(ci == NCH - 1))
        K.wait("vector", "sMM")
        yv = K.v(lambda e: e.tensor_copy(t["y1"][:, :], psP[:, 0:SLAB]))
        evac["psP"] = yv
        yield
        for oc in range(HPC):
            psX, psXn = (psQ, "psQ") if oc % 2 == 0 else (psP, "psP")
            K.wait("tensor", "sV", max(evac[psXn], yv))
            K.wait("tensor", "sS", evacS[psXn])
            K.mm(psX[:, 0:SLAB], t["wgb16"][:, oc * DH:(oc + 1) * DH], t["y1"][:, :])
            K.wait("scalar", "sMM")
            sg = K.s(lambda e, psX=psX, oc=oc: e.activation(
                t["gout"][:, oc * SLAB:(oc + 1) * SLAB], psX[:, 0:SLAB], AF.Sigmoid))
            evacS[psXn] = sg
            if oc % 2 == 1:
                yield
        K.wait("tensor", "sV", evac["psQ"])
        K.wait("tensor", "sS", evacS["psQ"])
        for ci in range(NCH):
            K.mm(psQ[0:HPC, 0:SLAB], t["wb16"][:, ci * HPC:(ci + 1) * HPC],
                 t["xb"][:, ci * SLAB:(ci + 1) * SLAB], start=(ci == 0), stop=(ci == NCH - 1))
        K.wait("scalar", "sMM")
        sg = K.s(lambda e: e.activation(t["beta"][0:HPC, :], psQ[0:HPC, 0:SLAB], AF.Sigmoid))
        evacS["psQ"] = sg
        yield
        K.wait("vector", "sS", sg)
        K.wait("vector", "sP", K.cnt["sP"])
        K.wait("vector", "sMM", K.cnt["sMM"])
        bv = K.v(lambda e: e.tensor_copy(t["y1"][0:HPC, 0:SLAB], t["beta"][0:HPC, :]))
        K.wait("tensor", "sV", K.cnt["sV"])
        K.wait("tensor", "sS", K.cnt["sS"])
        for c4_ in range(NCOL):
            K.tr(pst16[:, c4_ * 8:(c4_ + 1) * 8],
                 t["y1"][0:HPC, c4_ * CC:(c4_ + 1) * CC], t["c_ident16"][0:HPC, 0:HPC])
        btm = K.cnt["sMM"]
        K.wait("vector", "sMM", btm)
        K.v(lambda e: e.tensor_copy(
            t["betaTs"][:, 0:NCOL * 16].rearrange("p (c g) -> p c g", g=16)[:, :, 0:8],
            pst16[:, 0:NCOL * 8].rearrange("p (c h) -> p c h", h=8)))
        K.v(lambda e: e.tensor_scalar(
            t["betaTs"][:, 0:NCOL * 16].rearrange("p (c g) -> p c g", g=16)[:, :, 8:16],
            pst16[:, 0:NCOL * 8].rearrange("p (c h) -> p c h", h=8),
            -1.0, None, ALU.mult))

    def h_ap(name, h, cc, width=CC):
        return t[name][:, h * SLAB + cc * CC: h * SLAB + cc * CC + width]

    def w_ap(name, h, width=CC):
        return t[name][:, h * CC: h * CC + width]

    def w3(name):
        return t[name][:, 0:WD].rearrange("p (h x) -> p h x", x=CC)

    def h3(name, cc):
        return t[name][:, 0:HPC * SLAB].rearrange(
            "p (h x) -> p h x", x=SLAB)[:, :, cc * CC:(cc + 1) * CC]

    def sq_mms(dstps, lhs, rhs):
        for h in range(HPC):
            K.mm(dstps[:, h * CC:(h + 1) * CC], w_ap(lhs, h), w_ap(rhs, h))
        return K.cnt["sMM"]

    def A_gen(s, cc, aprev, bpar):
        p = cc % 2
        sx = str(p)
        K.wait("vector", "sS", aprev["sS"])
        for h in range(HPC):
            K.v(lambda e, h=h, cc=cc: e.tensor_tensor_scan(
                w_ap("G", h), h_ap("gg", h, cc), h_ap("gg", h, cc), 0.0, ALU.add, ALU.bypass))
        yield
        def gloc(e):
            g3 = t["G"][:, 0:WD].rearrange("p (h b i) -> p h b i", b=4, i=F0)
            r3 = t["G"][:, 0:WD].rearrange("p (h x) -> p h x", x=CC)[:, :, F0 // 2 - 1::F0]
            return e.tensor_tensor(
                t["Gloc"][:, 0:WD].rearrange("p (h b i) -> p h b i", b=4, i=F0),
                g3, r3.rearrange("p h (b o) -> p h b o", o=1).broadcast_to((DH, HPC, 4, F0)),
                ALU.subtract)
        K.v(gloc)
        c2 = K.v(lambda e: e.tensor_scalar(t["Gtmp"][:, 0:WD], t["Gloc"][:, 0:WD],
                                           CLAMP, None, ALU.min))
        yield
        K.wait("scalar", "sV", c2)
        K.wait("scalar", "sP", aprev["sP"])
        el = K.s(lambda e: e.activation(t["Eloc16"][:, 0:WD], t["Gtmp"][:, 0:WD], AF.Exp))
        yield
        K.wait("vector", "sS", el)
        c4 = K.v(lambda e: e.tensor_scalar(t["Gtmp"][:, 0:WD], t["Gloc"][:, 0:WD],
                                           -1.0, CLAMP, ALU.mult, ALU.min))
        yield
        K.wait("scalar", "sV", c4)
        ei = K.s(lambda e: e.activation(t["Einv"][:, 0:WD], t["Gtmp"][:, 0:WD], AF.Exp))
        K.s(lambda e: e.activation(t["EG"][:, 0:WD], t["G"][:, 0:WD], AF.Exp))
        K.wait("scalar", "sV", bpar["sV"])
        K.s(lambda e, sx=sx: e.activation(
            t["EGend" + sx][:, 0:HPC],
            t["G"][:, 0:WD].rearrange("p (h x) -> p h x", x=CC)[:, :, CC - 1], AF.Exp))
        yield
        K.wait("vector", "sS", ei)
        def k2arg(e):
            gend = t["G"][:, 0:WD].rearrange("p (h x) -> p h x", x=CC)[:, :, CC - 1]
            return e.tensor_tensor(
                t["Gtmp"][:, 0:WD].rearrange("p (h x) -> p h x", x=CC),
                gend.rearrange("p (h o) -> p h o", o=1).broadcast_to((DH, HPC, CC)),
                t["G"][:, 0:WD].rearrange("p (h x) -> p h x", x=CC), ALU.subtract)
        c6 = K.v(k2arg)
        K.wait("scalar", "sV", c6)
        ek = K.s(lambda e: e.activation(t["EK2"][:, 0:WD], t["Gtmp"][:, 0:WD], AF.Exp))
        yield
        K.wait("gpsimd", "sS", ek)
        K.wait("gpsimd", "sV", aprev["sV"])
        K.wait("gpsimd", "sMM", max(aprev["sMM"], bpar["sMM"]))
        qf3 = h3("qf", cc)
        kf3 = h3("kf", cc)
        for dst, a3, b3 in (("kbar", kf3, "Eloc16"), ("qbar", qf3, "Eloc16"),
                            ("ktil", kf3, "Einv"), ("kbarg" + sx, kf3, "EG"),
                            ("qbarg" + sx, qf3, "EG"), ("ktil2", kf3, "EK2")):
            K.p(lambda e, dst=dst, a3=a3, b3=b3: e.tensor_tensor(
                w3(dst), a3, w3(b3), ALU.mult))
        sc = K.p(lambda e, qf3=qf3, kf3=kf3: e.tensor_tensor(
            w3("qk16"), qf3, kf3, ALU.mult))
        yield
        K.wait("vector", "sS", ek)
        for h in range(HPC):
            def ea(e, h=h):
                r = t["G"][:, 0:WD].rearrange("p (h x) -> p h x", x=CC)[:, h, F0 // 2 - 1::F0]
                return e.tensor_tensor(
                    t["Gtmp"][:, h * 16:(h + 1) * 16].rearrange("p (i j) -> p i j", j=4),
                    r.rearrange("p (i o) -> p i o", o=1).broadcast_to((DH, 4, 4)),
                    r.rearrange("p (o j) -> p o j", o=1).broadcast_to((DH, 4, 4)),
                    ALU.subtract)
            K.v(ea)
        c8 = K.v(lambda e: e.tensor_scalar(t["Gloc"][:, 0:HPC * 16], t["Gtmp"][:, 0:HPC * 16],
                                           0.0, None, ALU.min))
        K.wait("scalar", "sV", c8)
        ea2 = K.s(lambda e: e.activation(t["EAll"][:, 0:HPC * 16], t["Gloc"][:, 0:HPC * 16],
                                         AF.Exp))
        K.wait("vector", "sS", ea2)
        K.v(lambda e: e.tensor_scalar(t["nEAll"][:, 0:HPC * 16], t["EAll"][:, 0:HPC * 16],
                                      -1.0, None, ALU.mult))
        yield
        wmm, amm = aprev["sMM"], aprev["sMM"]
        K.wait("tensor", "sP", sc)
        for I in range(4):
            if wmm:
                K.wait("vector", "sMM", wmm)
            def rbW(e, I=I):
                nk = t["ktil"][:, 0:WD].rearrange("p (h s) -> p h s", s=CC)[:, :, 0:(I + 1) * F0]
                scl = t["nEAll"][:, 0:HPC * 16].rearrange(
                    "p (h i j) -> p h i j", i=4, j=4)[:, :, I, 0:I + 1]
                return e.tensor_tensor(
                    t["rhsS0"][:, 0:WD].rearrange("p (h s) -> p h s", s=CC)[:, :, 0:(I + 1) * F0]
                        .rearrange("p h (j f) -> p h j f", f=F0),
                    nk.rearrange("p h (j f) -> p h j f", f=F0),
                    scl.rearrange("p h (j o) -> p h j o", o=1).broadcast_to((DH, HPC, I + 1, F0)),
                    ALU.mult)
            rb = K.v(rbW)
            tp = (0, 96) if I == 3 else None
            K.wait("tensor", "sV", rb)
            for h in range(HPC):
                K.mm(psga[I * F0:(I + 1) * F0, h * CC:h * CC + (I + 1) * F0],
                     t["kbar"][:, h * CC + I * F0:h * CC + (I + 1) * F0],
                     t["rhsS0"][:, h * CC:h * CC + (I + 1) * F0], tp=tp)
            wmm = K.cnt["sMM"]
            if amm:
                K.wait("vector", "sMM", amm)
            def rbA(e, I=I):
                qb = t["qbar"][:, 0:WD].rearrange("p (h s) -> p h s", s=CC)[:, :, I * F0:CC]
                scl = t["nEAll"][:, 0:HPC * 16].rearrange(
                    "p (h i j) -> p h i j", i=4, j=4)[:, :, I:4, I]
                return e.tensor_tensor(
                    t["rhsS1"][:, 0:WD].rearrange("p (h s) -> p h s", s=CC)[:, :, I * F0:CC]
                        .rearrange("p h (j f) -> p h j f", f=F0),
                    qb.rearrange("p h (j f) -> p h j f", f=F0),
                    scl.rearrange("p h (j o) -> p h j o", o=1).broadcast_to((DH, HPC, 4 - I, F0)),
                    ALU.mult)
            rb2 = K.v(rbA)
            K.wait("tensor", "sV", rb2)
            for h in range(HPC):
                K.mm(psgb[I * F0:(I + 1) * F0, h * CC + I * F0:h * CC + CC],
                     t["ktil"][:, h * CC + I * F0:h * CC + (I + 1) * F0],
                     t["rhsS1"][:, h * CC + I * F0:h * CC + CC], tp=tp)
            amm = K.cnt["sMM"]
            yield
        K.wait("vector", "sMM", amm)
        lt = K.v(lambda e: e.select(
            t["Lt"][:, 0:WD], t["c_maskSL"][:, :], psga[:, 0:WD],
            t["c_zero"][:, 0:1].broadcast_to((CC, WD)), add_drain=True))
        K.wait("vector", "sMM", bpar["sMM"])
        ap_ = K.v(lambda e, sx=sx: e.select(
            t["App" + sx][:, 0:WD], t["c_maskSU"][:, :], psgb[:, 0:WD],
            t["c_zero"][:, 0:1].broadcast_to((CC, WD)), add_drain=True))
        ltm = K.v(lambda e, cc=cc: e.tensor_tensor(
            w3("Lt"), w3("Lt"),
            t["betaTs"][:, cc * 16 + 8:cc * 16 + 16].rearrange("p (h o) -> p h o", o=1)
                .broadcast_to((CC, HPC, CC)), ALU.mult))
        yield
        K.wait("tensor", "sV", ltm)
        for h in range(HPC):
            K.tr(pst16[:, h * CC:(h + 1) * CC], w_ap("Lt", h), t["c_ident16"][:, :])
        lttm = K.cnt["sMM"]
        K.wait("vector", "sMM", lttm)
        ltt = K.v(lambda e: e.tensor_copy(t["LtT"][:, 0:WD], pst16[:, 0:WD]))
        yield
        K.wait("tensor", "sV", ltt)
        sq_mms(psga, "Lt", "LtT")
        mmL = sq_mms(psgb, "LtT", "Lt")
        K.wait("vector", "sMM", mmL)
        t0c = K.v(lambda e: e.tensor_copy(t["T0"][:, 0:WD], psga[:, 0:WD]))
        K.wait("scalar", "sMM", mmL)
        pac = K.s(lambda e: e.copy(t["Pa"][:, 0:WD], psgb[:, 0:WD]))
        yield
        K.wait("tensor", "sV", t0c)
        K.wait("tensor", "sS", pac)
        sq_mms(psga, "Pa", "T0")
        mmL = sq_mms(psgb, "T0", "Pa")
        K.wait("vector", "sMM", mmL)
        t1c = K.v(lambda e: e.tensor_copy(t["T1"][:, 0:WD], psga[:, 0:WD]))
        K.wait("scalar", "sMM", mmL)
        pbc = K.s(lambda e: e.copy(t["Pb"][:, 0:WD], psgb[:, 0:WD]))
        yield
        K.wait("gpsimd", "sV", t1c)
        K.wait("gpsimd", "sS", pbc)
        c3t_p = K.p(lambda e: e.tensor_tensor(t["C3Tp"][:, 0:WD], t["T0"][:, 0:WD],
                                              t["T1"][:, 0:WD], ALU.add))
        c3p_p = K.p(lambda e: e.tensor_tensor(t["C3p"][:, 0:WD], t["Pa"][:, 0:WD],
                                              t["Pb"][:, 0:WD], ALU.add))
        K.wait("tensor", "sV", t1c)
        K.wait("tensor", "sS", pbc)
        sq_mms(psga, "Pa", "T1")
        mmL = sq_mms(psgb, "T1", "Pa")
        K.wait("vector", "sMM", mmL)
        K.wait("vector", "sP", c3p_p)
        c3t = K.v(lambda e: e.tensor_tensor(t["C3Tp"][:, 0:WD], psga[:, 0:WD],
                                            t["C3Tp"][:, 0:WD], ALU.add))
        c3pl = K.v(lambda e: e.tensor_tensor(t["C3p"][:, 0:WD], psgb[:, 0:WD],
                                             t["C3p"][:, 0:WD], ALU.add))
        yield
        K.wait("tensor", "sV", c3pl)
        sq_mms(psga, "Pb", "T1")
        mmL = sq_mms(psgb, "T1", "Pb")
        K.wait("vector", "sMM", mmL)
        K.wait("vector", "sP", c3t_p)
        t0c = K.v(lambda e: e.tensor_copy(t["T0"][:, 0:WD], psga[:, 0:WD]))
        K.wait("scalar", "sMM", mmL)
        K.wait("scalar", "sP", c3p_p)
        pac = K.s(lambda e: e.copy(t["Pa"][:, 0:WD], psgb[:, 0:WD]))
        yield
        K.wait("tensor", "sV", t0c)
        K.wait("tensor", "sS", pac)
        sq_mms(psga, "Pa", "T0")
        mmL = sq_mms(psgb, "T0", "Pa")
        K.wait("vector", "sMM", mmL)
        K.wait("vector", "sP", c3t_p)
        t1c = K.v(lambda e: e.tensor_copy(t["T1"][:, 0:WD], psga[:, 0:WD]))
        K.wait("scalar", "sMM", mmL)
        K.wait("scalar", "sP", c3p_p)
        pbc = K.s(lambda e: e.copy(t["Pb"][:, 0:WD], psgb[:, 0:WD]))
        yield
        K.wait("gpsimd", "sV", t1c)
        K.wait("gpsimd", "sMM", bpar["sMM"])
        c2t_p = K.p(lambda e, sx=sx: e.tensor_tensor(
            t["C2T" + sx][:, 0:WD], t["T0"][:, 0:WD], t["T1"][:, 0:WD], ALU.add))
        K.wait("tensor", "sV", t1c)
        mmL = sq_mms(psga, "Pa", "T1")
        K.wait("vector", "sMM", mmL)
        K.wait("vector", "sP", c2t_p)
        c2t = K.v(lambda e, sx=sx: e.tensor_tensor(
            t["C2T" + sx][:, 0:WD], psga[:, 0:WD], t["C2T" + sx][:, 0:WD], ALU.add))
        yield
        K.wait("tensor", "sV", c2t)
        K.wait("tensor", "sS", pbc)
        sq_mms(psga, "Pb", "T1")
        mmL = sq_mms(psgb, "T1", "Pb")
        K.wait("vector", "sMM", mmL)
        K.wait("vector", "sP", c2t_p)
        t0c = K.v(lambda e: e.tensor_copy(t["T0"][:, 0:WD], psga[:, 0:WD]))
        K.wait("scalar", "sMM", mmL)
        pac = K.s(lambda e: e.copy(t["Pa"][:, 0:WD], psgb[:, 0:WD]))
        yield
        K.wait("tensor", "sV", t0c)
        K.wait("tensor", "sS", pac)
        mmL = sq_mms(psga, "Pa", "T0")
        K.wait("vector", "sMM", mmL)
        K.wait("vector", "sP", c2t_p)
        t1c = K.v(lambda e: e.tensor_copy(t["T1"][:, 0:WD], psga[:, 0:WD]))
        yield
        K.wait("gpsimd", "sV", t1c)
        K.wait("gpsimd", "sMM", bpar["sMM"])
        c1t_p = K.p(lambda e, sx=sx: e.tensor_tensor(
            t["C1T" + sx][:, 0:WD], t["T0"][:, 0:WD], t["T1"][:, 0:WD], ALU.add))
        K.wait("tensor", "sV", t1c)
        mmL = sq_mms(psga, "Pa", "T1")
        K.wait("vector", "sMM", mmL)
        K.wait("vector", "sP", c1t_p)
        K.v(lambda e, sx=sx: e.tensor_tensor(
            t["C1T" + sx][:, 0:WD], psga[:, 0:WD], t["C1T" + sx][:, 0:WD], ALU.add))
        yield
        K.wait("tensor", "sV", c3pl)
        K.wait("tensor", "sS", pac)
        mmL = sq_mms(psgb, "C3p", "LtT")
        K.wait("gpsimd", "sV", c3t)
        K.wait("gpsimd", "sMM", bpar["sMM"])
        c4t_p = K.p(lambda e, sx=sx: e.tensor_tensor(
            t["C4T" + sx][:, 0:WD], t["C3Tp"][:, 0:WD], t["LtT"][:, 0:WD], ALU.subtract))
        K.wait("vector", "sMM", mmL)
        K.wait("vector", "sP", c4t_p)
        c4t = K.v(lambda e, sx=sx: e.tensor_tensor(
            t["C4T" + sx][:, 0:WD], t["C4T" + sx][:, 0:WD], psgb[:, 0:WD], ALU.subtract))
        yield
        K.wait("tensor", "sV", c4t)
        K.wait("tensor", "sP", sc)
        for h in range(HPC):
            K.tr(pst16[:, h * CC:(h + 1) * CC], w_ap("ktil2", h), t["c_ident16"][:, :])
        k2m = K.cnt["sMM"]
        for h in range(HPC):
            K.mm(psgb[:, h:h + 1], w_ap("qk16", h), t["c_onescol16"][:, 0:1])
        dcm = K.cnt["sMM"]
        K.wait("vector", "sMM", k2m)
        K.wait("vector", "sMM", bpar["sMM"])
        K.v(lambda e, sx=sx: e.tensor_copy(t["k2t" + sx][:, 0:WD], pst16[:, 0:WD]))
        K.wait("vector", "sMM", dcm)
        K.wait("vector", "sP", bpar["sP"])
        K.v(lambda e, sx=sx: e.tensor_copy(t["dcol" + sx][:, 0:HPC], psgb[:, 0:HPC]))
        yield

    def B_gen(s, cc, bprev, colg):
        p = cc % 2
        sx = str(p)
        Scur = "S" if colg % 2 == 0 else "S2"
        Snxt = "S2" if colg % 2 == 0 else "S"
        K.wait("tensor", "sV", bprev["sV"])
        K.wait("tensor", "sP", bprev["sP"])
        K.wait("tensor", "sS", bprev["sS"])
        for h in range(HPC):
            K.mm(psal[:, h * CC:(h + 1) * CC], w_ap("kbarg" + sx, h),
                 t["S16"][:, h * DH:(h + 1) * DH], start=True, stop=False)
            K.mm(psal[:, h * CC:(h + 1) * CC], h_ap("vv", h, cc), t["c_negident16"][:, :],
                 start=False, stop=True)
        rmm = K.cnt["sMM"]
        yield
        K.wait("vector", "sMM", rmm)
        K.wait("vector", "sS", bprev["sS"])
        K.wait("vector", "sP", bprev["sP"])
        rt = K.v(lambda e, cc=cc: e.tensor_tensor(
            w3("Rt"), psal[:, 0:WD].rearrange("p (h x) -> p h x", x=CC),
            t["betaTs"][:, cc * 16:cc * 16 + 8].rearrange("p (h o) -> p h o", o=1)
                .broadcast_to((CC, HPC, CC)), ALU.mult))
        yield
        cur, ev, m_ = "Rt", rt, 0
        for fac_, nxt in (("C1T", "Ya"), ("C2T", "Rt"), ("C4T", "Ya")):
            K.wait("tensor", "sV", ev)
            for h in range(HPC):
                K.mm(psal[:, h * CC:(h + 1) * CC], w_ap(fac_ + sx, h), w_ap(cur, h))
            m_ = K.cnt["sMM"]
            K.wait("vector", "sMM", m_)
            ev = K.v(lambda e, nxt=nxt, cur=cur: e.tensor_tensor(
                t[nxt][:, 0:WD], psal[:, 0:WD], t[cur][:, 0:WD], ALU.add))
            cur = nxt
            yield
        ya2, lastapply = ev, m_
        K.wait("tensor", "sV", ya2)
        for h in range(HPC):
            K.mm(psal[:, h * DH:(h + 1) * DH], w_ap("k2t" + sx, h), w_ap("Ya", h))
        m4 = K.cnt["sMM"]
        K.wait("vector", "sS", bprev["sS"])
        K.v(lambda e: e.tensor_tensor(
            t[Snxt][:, 0:WD].rearrange("p (h x) -> p h x", x=DH),
            t[Scur][:, 0:WD].rearrange("p (h x) -> p h x", x=DH),
            t["EGend" + sx][:, 0:HPC].rearrange("p (h o) -> p h o", o=1)
                .broadcast_to((DH, HPC, DH)), ALU.mult))
        K.wait("vector", "sMM", m4)
        st = K.v(lambda e: e.tensor_tensor(
            t[Snxt][:, 0:WD], t[Snxt][:, 0:WD], psal[:, 0:WD], ALU.subtract))
        yield
        K.wait("vector", "sMM", lastapply)
        rtd = K.v(lambda e: e.tensor_tensor(
            w3("Rt"), w3("Ya"),
            t["dcol" + sx][:, 0:HPC].rearrange("p (h o) -> p h o", o=1)
                .broadcast_to((CC, HPC, CC)), ALU.mult))
        K.wait("tensor", "sV", rtd)
        K.wait("tensor", "sS", bprev["sS"])
        K.wait("tensor", "sV", bprev["sV"])
        o, omm = 0, 0
        for g in range(2):
            if g == 1:
                K.wait("tensor", "sV", o)
            for h in range(4 * g, 4 * g + 4):
                hc = (h - 4 * g) * CC
                K.mm(psbe[:, hc:hc + CC], t["S16"][:, h * DH:(h + 1) * DH],
                     w_ap("qbarg" + sx, h), start=True, stop=False)
                K.mm(psbe[:, hc:hc + CC], w_ap("Ya", h), w_ap("App" + sx, h),
                     start=False, stop=False)
                K.mm(psbe[:, hc:hc + CC], w_ap("Rt", h), t["c_negident16"][:, :],
                     start=False, stop=True)
            omm = K.cnt["sMM"]
            K.wait("vector", "sMM", omm)
            o = K.v(lambda e, g=g: e.tensor_copy(
                t["o16"][:, g * SLAB:(g + 1) * SLAB], psbe[:, 0:SLAB]))
        K.v(lambda e: e.tensor_copy(t["S16"][:, 0:HPC * DH], t[Snxt][:, 0:HPC * DH]))
        yield
        K.wait("gpsimd", "sV", o)
        K.wait("gpsimd", "sOG", bprev["sOG"])
        sq = K.p(lambda e: e.tensor_tensor(t["osq"][:, 0:WD], t["o16"][:, 0:WD],
                                           t["o16"][:, 0:WD], ALU.mult))
        K.wait("tensor", "sV", o)
        K.wait("tensor", "sP", sq)
        K.mm(psbe[:, 0:SLAB], t["c_onesblk16"][:, :], t["osq"][:, 0:SLAB])
        smm = K.cnt["sMM"]
        K.wait("scalar", "sMM", smm)
        K.wait("scalar", "sV", st)
        ln1 = K.s(lambda e, Scur=Scur: e.activation(t[Scur][:, 0:SLAB], psbe[:, 0:SLAB],
                                                    AF.Ln, bias=t["c_epsv"][:, 0:1],
                                                    scale=1.0 / DH))
        K.wait("tensor", "sS", ln1)
        K.mm(psbe[:, 0:SLAB], t["c_onesblk16"][:, :], t["osq"][:, SLAB:WD])
        smm = K.cnt["sMM"]
        yield
        K.wait("scalar", "sMM", smm)
        K.s(lambda e, Scur=Scur: e.activation(t[Scur][:, SLAB:WD], psbe[:, 0:SLAB],
                                              AF.Ln, bias=t["c_epsv"][:, 0:1],
                                              scale=1.0 / DH))
        rs = K.s(lambda e, Scur=Scur: e.activation(t["rstd16"][:, 0:WD], t[Scur][:, 0:WD],
                                                   AF.Exp, scale=-0.5))
        K.wait("gpsimd", "sS", rs)
        gmp = K.p(lambda e: e.tensor_tensor(t["rstd16"][:, 0:WD], t["rstd16"][:, 0:WD],
                                            t["o16"][:, 0:WD], ALU.mult))
        K.wait("vector", "sP", gmp)
        og = K.v(lambda e, cc=cc: e.scalar_tensor_tensor(
            w3("osq"), w3("rstd16"), t["nwcol"][:, 0:1], h3("gout", cc),
            ALU.mult, ALU.mult))
        K.wait("gpsimd", "sV", og)
        for h in range(HPC):
            K.dma("gpsimd", ib[s][h * DH:(h + 1) * DH, cc * CC:(cc + 1) * CC],
                  w_ap("osq", h), "sOG")
        if debug:
            K.dma("gpsimd",
                  K.dr["d_og"].rearrange("p (h x) -> p h x", x=T)[:, :, s * SLAB + cc * CC:
                                                                 s * SLAB + (cc + 1) * CC],
                  w3("osq"), "sO")
        yield

    bendp = [dict(sMM=0, sV=0, sS=0, sP=0, sOG=0), dict(sMM=0, sV=0, sS=0, sP=0, sOG=0)]
    colg = [0]
    for s in range(NSLAB):
        evac["psP"] = K.cnt["sV"]
        evac["psQ"] = K.cnt["sV"]
        evacS["psP"] = K.cnt["sS"]
        evacS["psQ"] = K.cnt["sS"]
        slab_qkv(s)
        slab_gates(s)
        if s == 0:
            K.wait("vector", "sD", fac_out_dma)
            K.v(lambda e: e.memset(t["S"][:, :], 0.0))
            K.v(lambda e: e.memset(t["S2"][:, :], 0.0))
            K.v(lambda e: e.memset(t["S16"][:, :], 0.0))
        pend = K.snap()
        tgen = gates_tail_gen(s)
        agen0 = A_gen(s, 0, pend, bendp[0])
        while tgen is not None or agen0 is not None:
            if tgen is not None:
                try:
                    next(tgen)
                except StopIteration:
                    tgen = None
            if agen0 is not None:
                try:
                    next(agen0)
                except StopIteration:
                    agen0 = None
        pend = K.snap()
        aprev = pend
        bprev = pend
        for cc in range(1, NCOL + 1):
            bgen = B_gen(s, cc - 1, bprev, colg[0])
            agen = A_gen(s, cc, aprev, bendp[cc % 2]) if cc < NCOL else None
            while bgen is not None or agen is not None:
                if bgen is not None:
                    try:
                        next(bgen)
                    except StopIteration:
                        bgen = None
                if agen is not None:
                    for _ in range(3):
                        try:
                            next(agen)
                        except StopIteration:
                            agen = None
                            break
            pend = K.snap()
            bendp[(cc - 1) % 2] = pend
            bprev = pend
            aprev = pend
            colg[0] += 1
        if debug:
            for nm, buf in (("d_qf", "qf"), ("d_kf", "kf"), ("d_v", "vv"),
                            ("d_gout", "gout")):
                K.dma("gpsimd",
                      K.dr[nm].rearrange("p (h x) -> p h x", x=T)[:, :, s * SLAB:(s + 1) * SLAB],
                      t[buf][:, 0:HPC * SLAB].rearrange("p (h x) -> p h x", x=SLAB), "sO")
            K.dma("gpsimd",
                  K.dr["d_g"].rearrange("p (h x) -> p h x", x=T)[:, :, s * SLAB:(s + 1) * SLAB],
                  t["gg"][:, 0:HPC * SLAB].rearrange("p (h x) -> p h x", x=SLAB), "sO")
            K.dma("gpsimd", K.dr["d_beta"][:, s * SLAB:(s + 1) * SLAB], t["beta"][0:HPC, :], "sO")
        if s > 0:
            K.wait("gpsimd", f"sCC{s - 1}", 1)
            K.dma("gpsimd", obAll[:, (s - 1) * SLAB:s * SLAB], obs[s - 1][:, :], "sD")
        K.wait("gpsimd", "sOG")
        K.inc("gpsimd", lambda e, s=s: e.collective_compute(
            "AllGather", ALU.bypass, replica_groups=[[0, 1], [2, 3], [4, 5], [6, 7]],
            ins=[ib[s][:, :].opt()], outs=[obs[s][:, :].opt()]), f"sCC{s}", 1)
    K.wait("gpsimd", f"sCC{NSLAB - 1}", 1)
    K.dma("gpsimd", obAll[:, (NSLAB - 1) * SLAB:NSLAB * SLAB], obs[NSLAB - 1][:, :], "sD")

    if stop_after == "scan":
        K.wait("gpsimd", "sD")
        K.wait("gpsimd", "sO", K.cnt["sO"])
        K.run()
        return nc, K

    SL_V, SL_MM, SL_D, SL_S = K.cnt["sV"], K.cnt["sMM"], K.cnt["sD"], K.cnt["sS"]
    slab_stack.close()
    post = contextlib.ExitStack()
    po = K.sbuf("po", [DH, NKCH * SUB], BF16, stack=post)
    pwf0 = K.sbuf("pwf0", [DH, NCH * DH], F32, stack=post)
    pwf1 = K.sbuf("pwf1", [DH, NCH * DH], F32, stack=post)
    pz = K.sbuf("pz", [DH, NCH * SUB], F32, stack=post)
    pzb = K.sbuf("pzb", [DH, NCH * SUB], BF16, stack=post)
    px = K.sbuf("px", [DH, NCH * SUB], F32, stack=post)
    pwa = K.sbuf("pwa", [DH, NKCH * DH], BF16, stack=post)
    pwb_ = K.sbuf("pwb", [DH, NKCH * DH], BF16, stack=post)
    prow = K.sbuf("prow", [1, 8 * SUB], F32, stack=post)
    pab = K.sbuf("pab", [DH, 2 * SUB], F32, stack=post)
    pzt = K.sbuf("pzt", [CC, 4 * DM], F32, stack=post)
    pcol = K.sbuf("pcol", [CC, 40], F32, stack=post)

    for eng in ("sync", "gpsimd", "vector", "scalar", "tensor"):
        K.wait(eng, "sV", SL_V)
        K.wait(eng, "sMM", SL_MM)
        K.wait(eng, "sD", SL_D)
        K.wait(eng, "sS", SL_S)

    pwcnt = [0, 0]
    pwcast = [0, 0]
    pevac = {"psP": 0, "psQ": 0}

    _offcache = {}
    def _post_off(e, i):
        key = (id(e), i)
        if key not in _offcache:
            pk = (id(e), "pid")
            if pk not in _offcache:
                _offcache[pk] = e.partition_id()
            _offcache[key] = (_offcache[pk] % 2) * OWN + i * SUB
        return _offcache[key]

    ld_gate = [0, 0]
    for i in range(NSUB):
        K.wait("sync", "sMM", ld_gate[0])
        K.wait("sync", "sV", ld_gate[1])
        def ld_o(e, i=i):
            off = _post_off(e, i)
            return e.dma_start(out=po[:, :],
                               in_=obAll.rearrange("(c p) t -> p c t", p=DH)[:, :, bass.ds(off, SUB)])
        K.inc("sync", ld_o, "sXa", 16)
        def ld_x(e, i=i):
            off = _post_off(e, i)
            return e.dma_start(out=px[:, :],
                               in_=xT.rearrange("(c p) t -> p c t", p=DH)[:, :, bass.ds(off, SUB)])
        K.inc("sync", ld_x, "sXb", 16)
        XDa, XDb = K.cnt["sXa"], K.cnt["sXb"]
        for dc in range(NCH):
            slot = dc % 2
            pwt = pwa if slot == 0 else pwb_
            if pwcnt[slot]:
                K.wait("sync", "sMM", pwcnt[slot])
            wsem = "sWa" if slot == 0 else "sWb"
            K.dma("sync", pwt[:, :], wo16t.rearrange("o p c d -> o p (c d)")[dc], wsem)
            psX, psXn = (psP, "psP") if dc % 2 == 0 else (psQ, "psQ")
            K.wait("tensor", wsem)
            K.wait("tensor", "sXa", XDa)
            K.wait("tensor", "sXb", XDb)
            K.wait("tensor", "sV", pevac[psXn])
            for c in range(NKCH):
                K.mm(psX[:, 0:SUB], pwt[:, c * DH:(c + 1) * DH], po[:, c * SUB:(c + 1) * SUB],
                     start=(c == 0), stop=(c == NKCH - 1))
            pwcnt[slot] = K.cnt["sMM"]
            K.wait("vector", "sMM")
            zv = K.v(lambda e, dc=dc, psX=psX: e.tensor_tensor(
                pz[:, dc * SUB:(dc + 1) * SUB], psX[:, 0:SUB],
                px[:, dc * SUB:(dc + 1) * SUB], ALU.add))
            pevac[psXn] = zv
        ld_gate[0] = K.cnt["sMM"]
        K.v(lambda e: e.tensor_copy(pzb[:, :], pz[:, :]))
        sqv = K.v(lambda e: e.tensor_mul(po[:, 0:NCH * SUB], pzb[:, :], pzb[:, :]))
        K.wait("tensor", "sV", sqv)
        for c in range(NCH):
            K.mm(psP[0:1, 0:SUB], t["c_onescol16"][:, 0:1], pzb[:, c * SUB:(c + 1) * SUB],
                 start=(c == 0), stop=(c == NCH - 1))
        for c in range(NCH):
            K.mm(psQ[0:1, 0:SUB], t["c_onescol16"][:, 0:1], po[:, c * SUB:(c + 1) * SUB],
                 start=(c == 0), stop=(c == NCH - 1))
        K.wait("vector", "sMM")
        K.v(lambda e: e.tensor_scalar(prow[0:1, 0:SUB], psP[0:1, 0:SUB], 1.0 / DM, None, ALU.mult))
        K.v(lambda e: e.tensor_mul(prow[0:1, SUB:2 * SUB], prow[0:1, 0:SUB], prow[0:1, 0:SUB]))
        vv_ = K.v(lambda e: e.scalar_tensor_tensor(
            prow[0:1, 2 * SUB:3 * SUB], psQ[0:1, 0:SUB], 1.0 / DM,
            prow[0:1, SUB:2 * SUB], ALU.mult, ALU.subtract))
        pevac["psP"] = vv_; pevac["psQ"] = vv_
        K.wait("scalar", "sV")
        K.s(lambda e: e.activation(prow[0:1, 3 * SUB:4 * SUB], prow[0:1, 2 * SUB:3 * SUB],
                                   AF.Ln, bias=t["c_epsln1"][0:1, 0:1]))
        av = K.s(lambda e: e.activation(prow[0:1, 4 * SUB:5 * SUB], prow[0:1, 3 * SUB:4 * SUB],
                                        AF.Exp, scale=-0.5))
        K.wait("vector", "sS", av)
        K.v(lambda e: e.tensor_mul(prow[0:1, 5 * SUB:6 * SUB], prow[0:1, 0:SUB],
                                   prow[0:1, 4 * SUB:5 * SUB]))
        bv = K.v(lambda e: e.tensor_scalar(prow[0:1, 6 * SUB:7 * SUB], prow[0:1, 5 * SUB:6 * SUB],
                                           -1.0, None, ALU.mult))
        K.wait("tensor", "sV", bv)
        K.mm(psP[:, 0:SUB], t["c_onesrowf"][0:1, :], prow[0:1, 4 * SUB:5 * SUB])
        K.mm(psQ[:, 0:SUB], t["c_onesrowf"][0:1, :], prow[0:1, 6 * SUB:7 * SUB])
        K.wait("vector", "sMM")
        K.v(lambda e: e.tensor_copy(pab[:, 0:SUB], psP[:, 0:SUB]))
        abv = K.v(lambda e: e.tensor_copy(pab[:, SUB:2 * SUB], psQ[:, 0:SUB]))
        pevac["psP"] = abv; pevac["psQ"] = abv
        for c in range(NCH):
            K.v(lambda e, c=c: e.tensor_mul(px[:, c * SUB:(c + 1) * SUB],
                                            pz[:, c * SUB:(c + 1) * SUB], pab[:, 0:SUB]))
            K.v(lambda e, c=c: e.tensor_tensor(
                px[:, c * SUB:(c + 1) * SUB], px[:, c * SUB:(c + 1) * SUB],
                pab[:, SUB:2 * SUB], ALU.add))
        xbv = K.v(lambda e: e.tensor_copy(pzb[:, :], px[:, :]))
        for dc in range(NCH):
            slot = dc % 2
            pwt = pwa if slot == 0 else pwb_
            pwf = pwf0 if slot == 0 else pwf1
            K.wait("sync", "sV", pwcast[slot])
            wsem = "sWa" if slot == 0 else "sWb"
            K.dma("sync", pwf[:, :],
                  wsumf.rearrange("(c p) (o d) -> o p c d", p=DH, d=DH)[dc], wsem)
            psX, psXn = (psP, "psP") if dc % 2 == 0 else (psQ, "psQ")
            K.wait("vector", wsem)
            K.wait("vector", "sMM", pwcnt[slot])
            pwcast[slot] = K.v(lambda e, pwt=pwt, pwf=pwf: e.tensor_copy(pwt[:, :], pwf[:, :]))
            K.wait("tensor", "sV", pwcast[slot])
            K.wait("tensor", "sV", max(xbv, pevac[psXn]))
            for c in range(NCH):
                K.mm(psX[:, 0:SUB], pwt[:, c * DH:(c + 1) * DH], pzb[:, c * SUB:(c + 1) * SUB],
                     start=(c == 0), stop=False)
            K.mm(psX[:, 0:SUB], t["fb8"][0:1, dc * DH:(dc + 1) * DH],
                 t["c_onesrow16"][0:1, 0:SUB], start=False, stop=True)
            pwcnt[slot] = K.cnt["sMM"]
            K.wait("vector", "sMM")
            z2v = K.v(lambda e, dc=dc, psX=psX: e.scalar_tensor_tensor(
                pz[:, dc * SUB:(dc + 1) * SUB], psX[:, 0:SUB], float(1.0 / np.sqrt(8.0)),
                px[:, dc * SUB:(dc + 1) * SUB], ALU.mult, ALU.add))
            pevac[psXn] = z2v
        ld_gate[1] = K.cnt["sV"]
        K.wait("tensor", "sV")
        for tt in range(4):
            for g in range(2):
                for c8 in range(8):
                    c = g * 8 + c8
                    K.tr(psal[:, c8 * DH:(c8 + 1) * DH],
                         pz[:, c * SUB + tt * DH:c * SUB + (tt + 1) * DH], t["c_ident"][:, :])
                K.wait("vector", "sMM")
                K.v(lambda e, tt=tt, g=g: e.tensor_copy(
                    pzt[:, tt * DM + g * 8 * DH:tt * DM + (g + 1) * 8 * DH], psal[:, 0:8 * DH]))
                K.wait("tensor", "sV")
        z3 = pzt[:, 0:4 * DM].rearrange("p (a d) -> p a d", d=DM)
        K.v(lambda e, z3=z3: e.reduce_sum(
            pcol[:, 0:4].rearrange("p (a o) -> p a o", o=1), z3, mybir.AxisListType.X))
        K.v(lambda e: e.tensor_mul(pz[:, 0:4 * DM], pzt[:, 0:4 * DM], pzt[:, 0:4 * DM]))
        K.v(lambda e: e.reduce_sum(
            pcol[:, 4:8].rearrange("p (a o) -> p a o", o=1),
            pz[:, 0:4 * DM].rearrange("p (a d) -> p a d", d=DM), mybir.AxisListType.X))
        K.v(lambda e: e.tensor_scalar(pcol[:, 8:12], pcol[:, 0:4], 1.0 / DM, None, ALU.mult))
        K.v(lambda e: e.tensor_mul(pcol[:, 12:16], pcol[:, 8:12], pcol[:, 8:12]))
        cv = K.v(lambda e: e.scalar_tensor_tensor(
            pcol[:, 16:20], pcol[:, 4:8], 1.0 / DM, pcol[:, 12:16], ALU.mult, ALU.subtract))
        K.wait("scalar", "sV", cv)
        K.s(lambda e: e.activation(pcol[:, 20:24], pcol[:, 16:20], AF.Ln,
                                   bias=t["c_epslnv"][:, 0:1]))
        sv2 = K.s(lambda e: e.activation(pcol[:, 24:28], pcol[:, 20:24], AF.Exp, scale=-0.5))
        K.wait("vector", "sS", sv2)
        K.v(lambda e: e.tensor_mul(pcol[:, 28:32], pcol[:, 8:12], pcol[:, 24:28]))
        K.v(lambda e: e.tensor_scalar(pcol[:, 32:36], pcol[:, 28:32], -1.0, None, ALU.mult))
        K.v(lambda e, z3=z3: e.tensor_tensor(
            z3, z3,
            pcol[:, 24:28].rearrange("p (a o) -> p a o", o=1).broadcast_to((CC, 4, DM)),
            ALU.mult))
        ov = K.v(lambda e, z3=z3: e.tensor_tensor(
            z3, z3,
            pcol[:, 32:36].rearrange("p (a o) -> p a o", o=1).broadcast_to((CC, 4, DM)),
            ALU.add))
        K.wait("gpsimd", "sV", ov)
        K.dma("gpsimd", out[i * SUB:(i + 1) * SUB, :].rearrange("(a p) d -> p a d", p=CC),
              pzt[:, 0:4 * DM].rearrange("p (a d) -> p a d", d=DM), "sO")

    K.wait("gpsimd", "sO", K.cnt["sO"])
    K.run()
    return nc, K


def _host_prep(inputs, core):
    b, hh = core // 2, core % 2
    hs = slice(HCH * hh, HCH * hh + HCH)
    hsl = slice(HPC * hh, HPC * hh + HPC)
    f32 = np.float32
    x = np.asarray(inputs["L0_attn_x"], f32)
    d = {}
    d["xT"] = np.ascontiguousarray(x[b].T)
    d["wq"] = np.ascontiguousarray(np.asarray(inputs["L0_attn_Wq"], f32)[:, hs])
    d["wk"] = np.ascontiguousarray(np.asarray(inputs["L0_attn_Wk"], f32)[:, hs])
    d["wv"] = np.ascontiguousarray(np.asarray(inputs["L0_attn_Wv"], f32)[:, hs])
    cw = np.zeros((DH, 3 * 4 * HPC), f32)
    for pi, nm in enumerate(("conv_q", "conv_k", "conv_v")):
        w = np.asarray(inputs[f"L0_attn_{nm}"], f32)[:, hs].reshape(4, HPC, DH)
        for oc in range(HPC):
            for dl in range(4):
                cw[:, pi * 32 + oc * 4 + dl] = w[dl, oc]
    d["cwres"] = cw
    d["wfa"] = np.asarray(inputs["L0_attn_Wfa"], f32)
    d["wfb"] = np.ascontiguousarray(np.asarray(inputs["L0_attn_Wfb"], f32)[:, hs])
    d["dtb"] = np.ascontiguousarray(np.asarray(inputs["L0_attn_dt_bias"], f32).reshape(H, DH)[hsl].T)
    d["alog"] = np.ascontiguousarray(np.asarray(inputs["L0_attn_A_log"], f32)[hsl, None])
    d["wb"] = np.ascontiguousarray(np.asarray(inputs["L0_attn_Wb"], f32)[:, hsl])
    d["wga"] = np.asarray(inputs["L0_attn_Wga"], f32)
    d["wgb"] = np.ascontiguousarray(np.asarray(inputs["L0_attn_Wgb"], f32)[:, hs])
    d["nw"] = np.ascontiguousarray(np.asarray(inputs["L0_attn_norm_w"], f32)[:, None])
    d["wo"] = np.asarray(inputs["L0_attn_Wo"], f32)
    d["fac"] = np.asarray(inputs["L0_ffn_factors"], f32)
    d["fbias"] = np.ascontiguousarray(np.asarray(inputs["L0_ffn_bias"], f32)[None, :])
    d["ident"] = np.eye(DH, dtype=f32)
    d["ident16"] = np.eye(DH, dtype=bfnp)
    d["negident16"] = (-np.eye(DH)).astype(bfnp)
    d["onescol16"] = np.ones((DH, 1), bfnp)
    d["onesrowf"] = np.ones((1, DH), f32)
    d["onesrow16"] = np.ones((1, SUB), bfnp)
    d["onesblk16"] = np.ones((DH, DH), bfnp)
    d["maskSL"] = np.tile(np.tril(np.ones((CC, CC), np.uint8), -1), (1, HPC))
    d["maskSU"] = np.tile(np.triu(np.ones((CC, CC), np.uint8), 1), (1, HPC))
    return d


def kernel(**inputs):
    nc, K = build_nc("full", debug=False)
    in_maps = [_host_prep(inputs, c) for c in range(8)]
    res = run_bass_kernel_spmd(nc, in_maps, core_ids=list(range(8)))
    full = np.zeros((B, T, DM), np.float32)
    for c in range(8):
        b_, hh = c // 2, c % 2
        full[b_, hh * OWN:(hh + 1) * OWN, :] = res.results[c]["out"]
    return full
